# version 16
# baseline (speedup 1.0000x reference)
from collections import deque
from contextlib import ExitStack

import numpy as np

import concourse.bass as bass
import concourse.mybir as mybir
from concourse.bass_utils import run_bass_kernel_spmd

F32 = mybir.dt.float32
BF16 = mybir.dt.bfloat16
AF = mybir.ActivationFunctionType
ALU = mybir.AluOpType
AX = mybir.AxisListType

B, S, D, M, HID = 4, 4096, 1024, 128, 4096
TOK = 2048
DECAY, LR = 0.99, 0.01
EPS = 1e-5
DECAY_C = float(DECAY ** 128)
R1024 = 1.0 / 1024.0
USE_TS = False

ENGS = ("sync", "gpsimd", "tensor", "vector", "scalar")
C_SUM, C_SQ, C_M, C_VAR, C_SD, C_R, C_NMR, C_TMP = (i * 8 for i in range(8))


class Ledger:
    def __init__(self, nc, ctx, sems):
        self.nc = nc
        self.prog = {e: [] for e in ENGS}
        self.sems = {}
        self.count = {}
        self.waited = {e: {} for e in ENGS}
        self.events = {e: [] for e in ENGS}
        for s in sems:
            self.sems[s] = ctx.enter_context(nc.semaphore(s))
            self.count[s] = 0

    def wait(self, eng, name, val=None):
        if val is None:
            val = self.count[name]
        if val <= 0 or val <= self.waited[eng].get(name, 0):
            return
        self.waited[eng][name] = val
        self.events[eng].append(("w", name, val))
        h = self.sems[name]
        self.prog[eng].append(lambda e, h=h, v=val: e.wait_ge(h, v))

    def op(self, eng, fn, inc=None, inck=1):
        if inc is None:
            self.prog[eng].append(fn)
            return None
        self.count[inc] += inck
        self.events[eng].append(("i", inc, inck))
        h = self.sems[inc]
        self.prog[eng].append(lambda e, fn=fn, h=h, k=inck: fn(e).then_inc(h, k))
        return self.count[inc]


class DmaStream:

    def __init__(self, lg, eng, qnames, sig):
        self.lg = lg
        self.eng = eng
        self.q = qnames
        self.sig = sig
        self.qi = 0
        self.issued = {q: 0 for q in qnames}
        self.open = False
        self.pending = deque()
        self.n_closed = 0

    def begin(self):
        if not self.open:
            q = self.q[self.qi]
            while any(p[0] == q for p in self.pending):
                self._emit_one()
            self.open = True

    def dma(self, fn):
        self.begin()
        q = self.q[self.qi]
        self.issued[q] += 1
        self.lg.op(self.eng, fn, inc=q, inck=16)

    def close(self):
        if not self.open:
            return self.n_closed
        q = self.q[self.qi]
        self.n_closed += 1
        self.pending.append((q, self.issued[q] * 16, self.n_closed))
        self.open = False
        self.qi = (self.qi + 1) % len(self.q)
        return self.n_closed

    def _emit_one(self):
        q, v, _ = self.pending.popleft()
        self.lg.wait(self.eng, q, v)
        h = self.lg.sems[self.sig]
        self.lg.count[self.sig] += 1
        self.lg.events[self.eng].append(("i", self.sig, 1))
        self.lg.prog[self.eng].append(lambda e, h=h: e.sem_inc(h, 1))

    def ensure(self, gtok):
        while self.pending and self.lg.count[self.sig] < gtok:
            self._emit_one()

    def emit_all(self):
        while self.pending:
            self._emit_one()

    def flush(self):
        self.close()
        while self.pending:
            self._emit_one()
        return self.lg.count[self.sig]


def build_program(debug_tag=None):
    nc = bass.Bass("TRN2", target_bir_lowering=False)

    di = lambda n, shp: nc.dram_tensor(n, shp, F32, kind="ExternalInput").ap()
    xo = di("xo", [TOK, D])
    xp = di("xp", [TOK, D])
    wk, wq, wv = di("wk", [D, M]), di("wq", [D, M]), di("wv", [D, M])
    wo = di("wo", [M, D])
    bk, bq, bv = di("bk", [M, 1]), di("bq", [M, 1]), di("bv", [M, 1])
    bo_in = di("bo", [128, D])
    s3_in = di("s3", [128, D])
    s1_in, s2_in = di("s1", [128, 32]), di("s2", [128, 32])
    sh1_in, sh2_in = di("sh1", [128, 32]), di("sh2", [128, 32])
    mka_in = di("mka", [128, 128])
    mkd_in = di("mkd", [128, 128])
    mkw_in = di("mkw", [128, 128])
    smask_in = di("smask", [128, 1])
    ident_in = di("identin", [128, 128])
    w1t = di("w1t", [32, 128, 1024])
    w2t = di("w2t", [32, 128, 1024])
    h1t = di("h1t", [32, 128, 4096])
    h2t = di("h2t", [32, 128, 4096])
    w3t = di("w3t", [4, 128, 8192])
    out = nc.dram_tensor("out", [TOK, D], F32, kind="ExternalOutput").ap()
    dbg = None
    if debug_tag is not None:
        dbg = nc.dram_tensor("dbg", [128, 4096], F32, kind="ExternalOutput").ap()

    xo_t = xo.rearrange("(t p) d -> t p d", p=128)
    xp_t = xp.rearrange("(t p) d -> t p d", p=128)
    out_t = out.rearrange("(t p) d -> t p d", p=128)

    ctx = ExitStack()
    with ctx:
        sb = lambda n, shp, dt: ctx.enter_context(nc.sbuf_tensor(n, shp, dt))
        xres = sb("xres", [128, 8 * 1024], F32)
        bk_s, bq_s, bv_s = (sb(n, [128, 1], F32) for n in ("bk_s", "bq_s", "bv_s"))
        s3_bc = sb("s3_bc", [128, 1024], F32)
        s1_s, s2_s = sb("s1_s", [128, 32], F32), sb("s2_s", [128, 32], F32)
        sh1_s, sh2_s = sb("sh1_s", [128, 32], F32), sb("sh2_s", [128, 32], F32)
        mask_a = sb("mask_a", [128, 128], F32)
        mask_dk = sb("mask_dk", [128, 128], BF16)
        mask_kw = sb("mask_kw", [128, 128], BF16)
        smask = sb("smask_b", [128, 1], F32)
        st = sb("st", [128, 256], F32)
        stats = sb("stats", [128, 64], F32)
        ident = sb("ident", [128, 128], F32)
        xfm = sb("xfm", [128, 8192], BF16)
        kqv = sb("kqv", [128, 3840], BF16)
        hbuf = sb("hbuf", [128, 32768], BF16)
        gbuf = sb("gbuf", [128, 32768], BF16)
        slab = sb("slab", [128, 8192], BF16)

        x_tile = lambda i: xres[:, i * 1024:(i + 1) * 1024]
        xfm_c = lambda c: xfm[:, c * 1024:(c + 1) * 1024]
        cw_k = slab[:, 0:1024]
        cw_q = slab[:, 1024:2048]
        cw_v = slab[:, 2048:3072]
        cw_o = slab[:, 3072:4096]
        bo_bc = gbuf[:, 24576:26624].bitcast(F32)
        k_fm = kqv[:, 0:1024]
        q_fm = kqv[:, 1024:2048]
        v_fm = kqv[:, 2048:3072]
        kw_scr = lambda p: kqv[:, 3072 + p * 128: 3072 + (p + 1) * 128]
        ktm_scr = lambda p: kqv[:, 3328 + p * 128: 3328 + (p + 1) * 128]
        vtm_scr = lambda p: kqv[:, 3584 + p * 128: 3584 + (p + 1) * 128]
        t1_buf = lambda par: kqv[:, par * 1024:(par + 1) * 1024].bitcast(F32)
        h_c = lambda ic: hbuf[:, ic * 1024:(ic + 1) * 1024]
        g2_c = h_c
        xp_buf = lambda i: hbuf[:, i * 2048:(i + 1) * 2048].bitcast(F32)
        g1_c = lambda ic: gbuf[:, ic * 1024:(ic + 1) * 1024]
        y_buf = gbuf[:, 0:2048].bitcast(F32)
        xln_buf = lambda p: gbuf[:, 2048 + p * 1024: 2048 + (p + 1) * 1024]
        sq_buf = gbuf[:, 4096:6144].bitcast(F32)
        qt_all = lambda c: gbuf[:, 6144 + c * 128: 6144 + (c + 1) * 128]
        kw_f32 = lambda c: gbuf[:, 26624 + c * 256: 26624 + (c + 1) * 256].bitcast(F32)
        ktm_all = lambda c: gbuf[:, 12288 + c * 128: 12288 + (c + 1) * 128]
        v_f32 = gbuf[:, 14336:16384].bitcast(F32)
        y3_t = lambda i: gbuf[:, 8192 + i * 2048: 8192 + (i + 1) * 2048].bitcast(F32)
        sl_w1 = lambda p: slab[:, p * 1024:(p + 1) * 1024]
        sl_w2 = lambda p: slab[:, 2048 + p * 1024: 2048 + (p + 1) * 1024]
        sl_h = lambda p: slab[:, p * 4096:(p + 1) * 4096]
        vtm_all = lambda c: slab[:, 4096 + c * 128: 4096 + (c + 1) * 128]
        abf_all = lambda c: slab[:, 5120 + c * 128: 5120 + (c + 1) * 128]
        snap_all = lambda c: slab[:, 6144 + c * 128: 6144 + (c + 1) * 128]
        outs_all = lambda c: slab[:, 7168 + c * 128: 7168 + (c + 1) * 128]
        sl_w3 = lambda p: (xfm if p == 0 else slab)[:, 0:8192]

        st_slot = [0]
        st_cur = lambda: st[:, st_slot[0] * 128:(st_slot[0] + 1) * 128]
        st_nxt = lambda: st[:, (1 - st_slot[0]) * 128:(2 - st_slot[0]) * 128]

        ps = [ctx.enter_context(nc.psum_tensor(f"ps{i}", [128, 512], F32))
              for i in range(8)]
        bank_guard = [("dve", 0)] * 8

        lg = Ledger(nc, ctx, (
            "q_sy0", "q_sy1", "q_tr0", "q_tr1", "q_tr2", "q_tr3",
            "q_ts0", "q_ts1", "q_ts2", "q_ts3",
            "q_wq0", "q_wq1", "lds", "trs", "tss", "wqs", "pe", "act", "dve", "gp"))
        SY = DmaStream(lg, "gpsimd", ["q_sy0", "q_sy1"], "lds")
        TR = DmaStream(lg, "scalar", ["q_tr0", "q_tr1", "q_tr2", "q_tr3"], "trs")
        TS = DmaStream(lg, "sync", ["q_ts0", "q_ts1", "q_ts2", "q_ts3"], "tss")
        WQ = DmaStream(lg, "gpsimd", ["q_wq0", "q_wq1"], "wqs")

        for h in lg.sems.values():
            nc.gpsimd.sem_clear(h)
        nc.all_engine_barrier()

        dbg_state = {"done": False}

        def dbg_tap(tag, ap_fn, ncols, wait_sems=()):
            if debug_tag != tag or dbg_state["done"]:
                return
            dbg_state["done"] = True
            if "lds" in wait_sems:
                gg = SY.close()
                SY.ensure(gg)
            for sname in wait_sems:
                lg.wait("gpsimd", sname)
            SY.close()
            SY.dma(lambda e, a=ap_fn(), n=ncols: e.dma_start(dbg[:, 0:n], a))
            g = SY.close()
            SY.ensure(g)
            for e_ in ENGS:
                lg.wait(e_, "lds", g)

        cwtok = {"g": 0}

        def load_cw(guard_pe):
            WQ.emit_all()
            lg.wait("gpsimd", "pe", guard_pe)
            WQ.dma(lambda e: e.dma_start(
                cw_k.rearrange("p (c m) -> p c m", m=128),
                wk.rearrange("(c p) m -> p c m", p=128)))
            WQ.dma(lambda e: e.dma_start(
                cw_q.rearrange("p (c m) -> p c m", m=128),
                wq.rearrange("(c p) m -> p c m", p=128)))
            WQ.dma(lambda e: e.dma_start(
                cw_v.rearrange("p (c m) -> p c m", m=128),
                wv.rearrange("(c p) m -> p c m", p=128)))
            WQ.dma(lambda e: e.dma_start(cw_o, wo))
            cwtok["g"] = WQ.close()
            WQ.emit_all()

        load_cw(0)
        WQ.dma(lambda e: e.dma_start(mask_dk[:], mkd_in))
        WQ.dma(lambda e: e.dma_start(mask_kw[:], mkw_in))
        g_masks = WQ.close()
        WQ.emit_all()

        for dst, src in ((bk_s[:], bk), (bq_s[:], bq), (bv_s[:], bv),
                         (s3_bc[:], s3_in), (s1_s[:], s1_in), (s2_s[:], s2_in),
                         (sh1_s[:], sh1_in), (sh2_s[:], sh2_in),
                         (mask_a[:], mka_in), (smask[:], smask_in),
                         (ident[:], ident_in)):
            SY.dma(lambda e, d=dst, s=src: e.dma_start(d, s))
        g_consts = SY.close()

        lg.op("vector", lambda e: e.memset(st[:, 0:128], 0.0))


        tok = {
            "xfm_pe": 0,
            "xln_tr": [None, None],
            "xsc_pe": [0, 0],
            "kw_tr": [0, 0],
            "vtm_pe": [0, 0],
            "w1_slab": [0, 0], "w2_slab": [0, 0],
            "h_slab_pe": [0, 0],
            "slab_attn_pe": 0,
            "qt_pe": 0,
            "t1_free": {},
            "store": [0] * 8,
            "y_sq_act": 0,
            "st_tok": 0,
            "h_read_pe": 0,
        }

        def mm_group(bank_idx, out_ap, pairs, waits=()):
            sname, v = bank_guard[bank_idx]
            lg.wait("tensor", sname, v)
            for sn, vv in waits:
                lg.wait("tensor", sn, vv)
            n = len(pairs)
            t = None
            for j, (lf, rf) in enumerate(pairs):
                t = lg.op("tensor",
                          lambda e, lf=lf, rf=rf, st_=(j == 0), sp=(j == n - 1),
                          o=out_ap: e.matmul(o, lf(), rf(), start=st_, stop=sp),
                          inc="pe" if j == n - 1 else None)
            return t

        def ln_stats_batch(srcs, src_toks, src_sem, cb=0):
            n = len(srcs)
            a_last = None
            for i, (sap, stok) in enumerate(zip(srcs, src_toks)):
                if stok:
                    lg.wait("scalar", src_sem, stok)
                a_last = lg.op(
                    "scalar",
                    lambda e, sap=sap, i=i, cb=cb: e.activation(
                        sq_buf, sap, AF.Square,
                        accum_out=stats[:, C_SQ + cb + i:C_SQ + cb + i + 1]),
                    inc="act")
            t_red = None
            for i, (sap, stok) in enumerate(zip(srcs, src_toks)):
                if stok:
                    lg.wait("vector", src_sem, stok)
                t_red = lg.op("vector",
                              lambda e, sap=sap, i=i, cb=cb: e.reduce_sum(
                                  stats[:, C_SUM + cb + i:C_SUM + cb + i + 1], sap,
                                  axis=AX.X), inc="dve")
            lg.wait("vector", "act", a_last)
            lg.wait("vector", "dve", t_red)
            tM = lg.op("vector", lambda e, n=n, cb=cb: e.tensor_scalar(
                stats[:, C_M + cb:C_M + cb + n], stats[:, C_SUM + cb:C_SUM + cb + n],
                R1024, None, ALU.mult), inc="dve")
            lg.op("vector", lambda e, n=n, cb=cb: e.tensor_scalar(
                stats[:, C_SD + cb:C_SD + cb + n], stats[:, C_SQ + cb:C_SQ + cb + n],
                R1024, None, ALU.mult))
            lg.wait("vector", "dve", tM)
            tT = lg.op("vector", lambda e, n=n, cb=cb: e.tensor_tensor(
                stats[:, C_TMP + cb:C_TMP + cb + n], stats[:, C_M + cb:C_M + cb + n],
                stats[:, C_M + cb:C_M + cb + n], ALU.mult), inc="dve")
            lg.wait("vector", "dve", tT)
            tv = lg.op("vector", lambda e, n=n, cb=cb: e.scalar_tensor_tensor(
                stats[:, C_VAR + cb:C_VAR + cb + n], stats[:, C_SD + cb:C_SD + cb + n], EPS,
                stats[:, C_TMP + cb:C_TMP + cb + n], ALU.add, ALU.subtract), inc="dve")
            lg.wait("scalar", "dve", tv)
            ta = lg.op("scalar", lambda e, n=n, cb=cb: e.activation(
                stats[:, C_SD + cb:C_SD + cb + n], stats[:, C_VAR + cb:C_VAR + cb + n],
                AF.Sqrt), inc="act")
            lg.wait("vector", "act", ta)
            tR = lg.op("vector", lambda e, n=n, cb=cb: e.reciprocal(
                stats[:, C_R + cb:C_R + cb + n], stats[:, C_SD + cb:C_SD + cb + n]), inc="dve")
            lg.wait("vector", "dve", tR)
            tn = lg.op("vector", lambda e, n=n, cb=cb: e.scalar_tensor_tensor(
                stats[:, C_NMR + cb:C_NMR + cb + n], stats[:, C_M + cb:C_M + cb + n], -1.0,
                stats[:, C_R + cb:C_R + cb + n], ALU.mult, ALU.mult), inc="dve")
            return tn

        def trx_wait(eng, toks):
            if toks is None:
                return
            for s_, v in toks:
                lg.wait(eng, s_, v)

        def tok_waits(toklists):
            m = {}
            for tl in toklists:
                if tl is None:
                    continue
                for s_, v in tl:
                    m[s_] = max(m.get(s_, 0), v)
            return list(m.items())

        def pe_tr_block(dst_ap, src_ap, bank_idx, waits, xfm_guard_pe):
            sname, v = bank_guard[bank_idx]
            lg.wait("tensor", sname, v)
            for sn, vv in waits:
                lg.wait("tensor", sn, vv)
            lg.wait("tensor", "lds", g_consts)
            t = lg.op("tensor", lambda e, d=src_ap, bi=bank_idx: e.transpose(
                ps[bi][:, 0:128], d, ident[:]), inc="pe")
            lg.wait("vector", "pe", t)
            lg.wait("vector", "pe", xfm_guard_pe)
            ev = lg.op("vector", lambda e, o=dst_ap, bi=bank_idx: e.tensor_copy(
                o, ps[bi][:, 0:128]), inc="dve")
            bank_guard[bank_idx] = ("dve", ev)
            return t, ev

        def ln_to_xfm(srcs, src_toks, src_sem, base=0, dma_path=False):
            stok = ln_stats_batch(srcs, src_toks, src_sem, cb=base)
            tr_toks = [None] * len(srcs)
            for j, sap in enumerate(srcs):
                i = base + j
                p = i % 2
                if dma_path:
                    lg.wait("scalar", "dve", stok)
                    trx_wait("scalar", tok["xln_tr"][p])
                    t_ap = lg.op("scalar", lambda e, s=sap, i=i, p=p:
                                 e.activation(
                                     xln_buf(p), s, AF.Identity,
                                     scale=stats[:, C_R + i:C_R + i + 1],
                                     bias=stats[:, C_NMR + i:C_NMR + i + 1]),
                                 inc="act")
                    lg.wait("scalar", "pe", tok["xfm_pe"])
                    lg.wait("scalar", "act", t_ap)
                    for c in range(8):
                        TR.dma(lambda e, c=c, i=i, p=p: e.dma_start(
                            xfm_c(c)[:, i * 128:(i + 1) * 128],
                            xln_buf(p)[:, c * 128:(c + 1) * 128],
                            transpose=True))
                    g = TR.close()
                    TR.emit_all()
                    tok["xln_tr"][p] = [("trs", g)]
                    tr_toks[j] = [("trs", g)]
                    continue
                xsc = y3_t(p)
                lg.wait("scalar", "dve", stok)
                lg.wait("scalar", "pe", tok["xsc_pe"][p])
                t_ap = lg.op("scalar", lambda e, s=sap, i=i, x=xsc: e.activation(
                    x, s, AF.Identity,
                    scale=stats[:, C_R + i:C_R + i + 1],
                    bias=stats[:, C_NMR + i:C_NMR + i + 1]), inc="act")
                last_ev = 0
                last_t = 0
                for c in range(8):
                    last_t, last_ev = pe_tr_block(
                        xfm_c(c)[:, i * 128:(i + 1) * 128],
                        xsc[:, c * 128:(c + 1) * 128],
                        6 + c % 2, [("act", t_ap)], tok["xfm_pe"])
                tok["xsc_pe"][p] = last_t
                tr_toks[j] = [("dve", last_ev)]
            return tr_toks

        def proj_pass(tr_toks, do_q):
            plist = [("k", cw_k, bk_s, k_fm, (0, 1)),
                     ("v", cw_v, bv_s, v_f32, (2, 3))]
            if do_q:
                plist.append(("q", cw_q, bq_s, q_fm, (4, 5)))
            ev = {}
            for name, cw, bias, dst, banks in plist:
                ev[name] = [0, 0]
                for tg in range(2):
                    bi = banks[tg]
                    tws = tok_waits(tr_toks[tg * 4:(tg + 1) * 4])
                    pe_t = mm_group(
                        bi, ps[bi][:],
                        [(lambda ic=ic, cw=cw: cw[:, ic * 128:(ic + 1) * 128],
                          lambda ic=ic, tg=tg: xfm_c(ic)[:, tg * 512:(tg + 1) * 512])
                         for ic in range(8)],
                        waits=tws + [("wqs", cwtok["g"]), ("lds", g_consts)])
                    lg.wait("scalar", "pe", pe_t)
                    lg.wait("scalar", "pe", tok["qt_pe"])
                    a_t = lg.op("scalar",
                                lambda e, b=ps[bi], d=dst, tg=tg, bias=bias:
                                e.activation(d[:, tg * 512:(tg + 1) * 512], b[:],
                                             AF.Identity, bias=bias[:]),
                                inc="act")
                    ev[name][tg] = a_t
                    bank_guard[bi] = ("act", a_t)
            return ev

        def attn_chunks(ev, full):
            kw_toks = [0] * 8
            abf_toks = [0] * 8
            qt_toks = [0] * 8
            snap_toks = [0] * 8
            for c in range(8):
                par = c % 2
                lg.wait("vector", "act", ev["k"][c // 4])
                lg.wait("vector", "wqs", g_masks)
                lg.wait("vector", "pe", tok["qt_pe"])
                kw_toks[c] = lg.op("vector", lambda e, c=c: e.tensor_tensor(
                    kw_f32(c), k_fm[:, c * 128:(c + 1) * 128], mask_kw[:],
                    ALU.mult), inc="dve")
                if full:
                    lg.wait("vector", "act", ev["q"][c // 4])
                    qt_toks[c] = lg.op("vector", lambda e, c=c: e.tensor_tensor(
                        qt_all(c), q_fm[:, c * 128:(c + 1) * 128], mask_dk[:],
                        ALU.mult), inc="dve")
                    bi_a = 6 + par
                    pe_a = mm_group(
                        bi_a, ps[bi_a][:, 0:128],
                        [(lambda c=c: k_fm[:, c * 128:(c + 1) * 128],
                          lambda c=c: q_fm[:, c * 128:(c + 1) * 128])],
                        waits=[("act", max(ev["k"][c // 4], ev["q"][c // 4]))])
                    lg.wait("vector", "pe", pe_a)
                    lg.wait("vector", "lds", g_consts)
                    lg.wait("vector", "pe", tok["slab_attn_pe"])
                    abf_toks[c] = lg.op("vector", lambda e, c=c, bi=bi_a:
                                        e.tensor_tensor(abf_all(c),
                                                        ps[bi][:, 0:128],
                                                        mask_a[:], ALU.mult),
                                        inc="dve")
                    bank_guard[bi_a] = ("dve", abf_toks[c])
            tp_toks = [None] * 8
            for c in range(8):
                _, ev_k = pe_tr_block(
                    ktm_all(c), kw_f32(c), c % 4,
                    [("dve", kw_toks[c]), ("pe", tok["qt_pe"])], 0)
                _, ev_v = pe_tr_block(
                    vtm_all(c), v_f32[:, c * 128:(c + 1) * 128], (c + 1) % 4,
                    [("act", ev["v"][c // 4]), ("pe", tok["slab_attn_pe"])], 0)
                tp_toks[c] = [("dve", ev_v)]
            tr_pair = None
            for c in range(8):
                par = c % 2
                bi = 4 + par
                pe_t = mm_group(
                    bi, ps[bi][:, 0:128],
                    [(lambda c=c: ktm_all(c), lambda c=c: vtm_all(c))],
                    waits=tp_toks[c])
                lg.wait("vector", "dve", tok["st_tok"])
                if full:
                    lg.wait("vector", "pe", tok["slab_attn_pe"])
                    _scur = st_cur()
                    snap_toks[c] = lg.op("vector", lambda e, c=c, s_=_scur:
                                         e.tensor_copy(snap_all(c), s_),
                                         inc="dve")
                lg.wait("vector", "pe", pe_t)
                cur, nxt = st_cur(), st_nxt()
                tu = lg.op("vector", lambda e, bi=bi, cur=cur, nxt=nxt:
                           e.scalar_tensor_tensor(nxt, cur, DECAY_C,
                                                  ps[bi][:, 0:128],
                                                  ALU.mult, ALU.add), inc="dve")
                st_slot[0] = 1 - st_slot[0]
                tok["st_tok"] = tu
                bank_guard[bi] = ("dve", tu)
            return abf_toks, snap_toks, qt_toks, tp_toks

        hb_act = 0
        hb_dve = 0
        for ph in range(2):
            xp_toks = []
            lg.wait("gpsimd", "act", hb_act)
            lg.wait("gpsimd", "dve", hb_dve)
            for i in range(8):
                SY.dma(lambda e, i=i, ph=ph: e.dma_start(
                    xp_buf(i), xp_t[ph * 8 + i]))
                if i % 4 == 3:
                    g = SY.close()
                    for j in range(i - 3, i + 1):
                        xp_toks.append(g)
            SY.emit_all()
            srcs = [xp_buf(i) for i in range(8)]
            if ph == 0:
                dbg_tap("pre_xp", lambda: xp_buf(0), 1024,
                        wait_sems=("lds",))
            tr_toks = ln_to_xfm(srcs, xp_toks, "lds")
            if ph == 0:
                dbg_tap("pre_stats", lambda: stats[:], 64,
                        wait_sems=("dve", "act"))
            hb_act = lg.count["act"]
            hb_dve = lg.count["dve"]
            ev = proj_pass(tr_toks, do_q=False)
            tok["xfm_pe"] = lg.count["pe"]
            attn_chunks(ev, full=False)
            tok["slab_attn_pe"] = lg.count["pe"]
            tok["qt_pe"] = lg.count["pe"]
        lg.wait("vector", "lds", g_consts)
        lg.wait("vector", "dve", tok["st_tok"])
        _cur, _nxt = st_cur(), st_nxt()
        tok["st_tok"] = lg.op("vector", lambda e, c=_cur, x=_nxt:
                              e.tensor_scalar(x, c, smask[:], None, ALU.mult),
                              inc="dve")
        st_slot[0] = 1 - st_slot[0]
        dbg_tap("prelude_st", lambda: st_cur(), 128, wait_sems=("dve",))

        g1_guard_dve = 0
        for Hh in range(2):
            WQ.emit_all()
            TR.emit_all()
            SY.emit_all()
            lg.wait("gpsimd", "pe", tok["qt_pe"])
            SY.dma(lambda e: e.dma_start(bo_bc, bo_in))
            if Hh == 1:
                load_cw(max(tok["w1_slab"] + tok["w2_slab"]))
            x_toks = []
            for i in range(8):
                if Hh == 1:
                    SY.ensure(tok["store"][i])
                    lg.wait("gpsimd", "lds", tok["store"][i])
                SY.dma(lambda e, i=i, Hh=Hh: e.dma_start(
                    x_tile(i), xo_t[Hh * 8 + i]))
                if i % 4 == 3:
                    g = SY.close()
                    for j in range(i - 3, i + 1):
                        x_toks.append(g)
            SY.emit_all()
            g_bo = x_toks[0]

            tr_toks = (ln_to_xfm([x_tile(i) for i in range(4)],
                                 x_toks[0:4], "lds", 0) +
                       ln_to_xfm([x_tile(i) for i in range(4, 8)],
                                 x_toks[4:8], "lds", 4))
            ev = proj_pass(tr_toks, do_q=True)
            tok["xfm_pe"] = lg.count["pe"]

            abf_toks, snap_toks, qt_toks, tp_toks = attn_chunks(ev, full=True)

            outs_toks = [0] * 8
            for c in range(8):
                par = c % 2
                bi_o = 6 + par
                pe_o = mm_group(
                    bi_o, ps[bi_o][:, 128:256],
                    [(lambda c=c: vtm_all(c), lambda c=c: abf_all(c)),
                     (lambda c=c: snap_all(c), lambda c=c: qt_all(c))],
                    waits=[("dve", max(abf_toks[c], snap_toks[c], qt_toks[c]))] +
                    tp_toks[c])
                tok["qt_pe"] = pe_o
                tok["slab_attn_pe"] = pe_o
                lg.wait("scalar", "pe", pe_o)
                outs_toks[c] = lg.op("scalar", lambda e, c=c, bi=bi_o:
                                     e.activation(outs_all(c), ps[bi][:, 128:256],
                                                  AF.Copy), inc="act")
                bank_guard[bi_o] = ("act", outs_toks[c])

            y_toks = [0] * 8
            for c in range(8):
                par = c % 2
                bY = (0, 1) if par == 0 else (2, 3)
                pe_y = 0
                for hd in range(2):
                    pe_y = mm_group(
                        bY[hd], ps[bY[hd]][:],
                        [(lambda c=c: outs_all(c),
                          lambda hd=hd: cw_o[:, hd * 512:(hd + 1) * 512])],
                        waits=[("act", outs_toks[c]), ("wqs", cwtok["g"])])
                tok["slab_attn_pe"] = max(tok["slab_attn_pe"], pe_y)
                lg.wait("vector", "pe", pe_y)
                lg.wait("vector", "lds", g_bo)
                ty = 0
                for hd in range(2):
                    ty = lg.op("vector", lambda e, c=c, hd=hd, bY=bY:
                               e.tensor_tensor(
                                   y3_t(c)[:, hd * 512:(hd + 1) * 512],
                                   ps[bY[hd]][:],
                                   bo_bc[:, hd * 512:(hd + 1) * 512], ALU.add),
                               inc="dve")
                    bank_guard[bY[hd]] = ("dve", ty)
                y_toks[c] = ty
            stok2 = ln_stats_batch([y3_t(c) for c in range(8)], y_toks, "dve")
            act_ln2 = lg.count["act"]
            xup = [0] * 8
            for c in range(8):
                lg.wait("vector", "dve", stok2)
                lg.op("vector", lambda e, c=c: e.scalar_tensor_tensor(
                    x_tile(c), y3_t(c), stats[:, C_R + c:C_R + c + 1],
                    x_tile(c), ALU.mult, ALU.add))
                xup[c] = lg.op("vector", lambda e, c=c: e.tensor_scalar(
                    x_tile(c), x_tile(c), stats[:, C_NMR + c:C_NMR + c + 1],
                    None, ALU.add), inc="dve")
            tr_toks3 = (ln_to_xfm([x_tile(i) for i in range(4)],
                                  xup[0:4], "dve", 0) +
                        ln_to_xfm([x_tile(i) for i in range(4, 8)],
                                  xup[4:8], "dve", 4))
            attn_x_dve = lg.count["dve"]
            if debug_tag == "attn" and Hh == 0:
                dbg_tap("attn", lambda: x_tile(0), 1024, wait_sems=("dve",))


            w12_toks = {}

            def load_w12(oc):
                par = oc % 2
                WQ.emit_all()
                lg.wait("gpsimd", "pe", max(tok["w1_slab"][par],
                                            tok["w2_slab"][par],
                                            tok["xfm_pe"],
                                            tok["slab_attn_pe"]))
                WQ.dma(lambda e, oc=oc, par=par: e.dma_start(
                    sl_w1(par), w1t[oc]))
                WQ.dma(lambda e, oc=oc, par=par: e.dma_start(
                    sl_w2(par), w2t[oc]))
                w12_toks[oc] = WQ.close()

            load_w12(0)
            load_w12(1)
            h_done_dve = 0
            for oc in range(32):
                par = oc % 2
                p1b = (0, 1) if par == 0 else (4, 5)
                p2b = (2, 3) if par == 0 else (6, 7)
                for tg in range(2):
                    pe1 = mm_group(
                        p1b[tg], ps[p1b[tg]][:],
                        [(lambda ic=ic, par=par: sl_w1(par)[:, ic * 128:(ic + 1) * 128],
                          lambda ic=ic, tg=tg: xfm_c(ic)[:, tg * 512:(tg + 1) * 512])
                         for ic in range(8)],
                        waits=tok_waits(tr_toks3[tg * 4:(tg + 1) * 4]) + [
                               ("wqs", w12_toks[oc]), ("lds", g_consts)])
                    pe2 = mm_group(
                        p2b[tg], ps[p2b[tg]][:],
                        [(lambda ic=ic, par=par: sl_w2(par)[:, ic * 128:(ic + 1) * 128],
                          lambda ic=ic, tg=tg: xfm_c(ic)[:, tg * 512:(tg + 1) * 512])
                         for ic in range(8)],
                        waits=[("wqs", w12_toks[oc])])
                    tok["w1_slab"][par] = pe1
                    tok["w2_slab"][par] = pe2
                    tok["xfm_pe"] = pe2
                    lg.wait("scalar", "pe", pe1)
                    lg.wait("scalar", "dve", tok["t1_free"].get(par, 0))
                    ta = lg.op("scalar",
                               lambda e, oc=oc, par=par, b=ps[p1b[tg]]:
                               e.activation(t1_buf(par), b[:], AF.Silu,
                                            scale=s1_s[:, oc:oc + 1]), inc="act")
                    bank_guard[p1b[tg]] = ("act", ta)
                    lg.wait("vector", "pe", pe2)
                    lg.wait("vector", "act", ta)
                    lg.wait("vector", "pe", tok["h_read_pe"])
                    th = lg.op("vector",
                               lambda e, oc=oc, par=par, tg=tg, b=ps[p2b[tg]]:
                               e.scalar_tensor_tensor(
                                   h_c(oc)[:, tg * 512:(tg + 1) * 512], b[:],
                                   s2_s[:, oc:oc + 1], t1_buf(par),
                                   ALU.mult, ALU.mult), inc="dve")
                    bank_guard[p2b[tg]] = ("dve", th)
                    tok["t1_free"][par] = th
                    h_done_dve = th
                if oc + 2 < 32:
                    load_w12(oc + 2)

            def hidden_layer(wsrc, in_c, out_c, scale_s, in_dve, in_act,
                             evac_dve_guard, first_slab_extra_pe,
                             pre0, pre1, pre_guards):
                slab_toks = {}

                def load_h(oc):
                    par = oc % 2
                    WQ.emit_all()
                    lg.wait("gpsimd", "pe", first_slab_extra_pe)
                    if oc >= 4:
                        lg.wait("gpsimd", "pe", tok["h_slab_pe"][par])
                    WQ.dma(lambda e, oc=oc, par=par, wsrc=wsrc:
                           e.dma_start(sl_h(par), wsrc[oc]))
                    slab_toks[oc] = WQ.close()

                WQ.emit_all()
                for sn, v in pre_guards:
                    lg.wait("gpsimd", sn, v)
                WQ.dma(lambda e: e.dma_start(pre0, wsrc[0]))
                slab_toks[0] = WQ.close()
                WQ.dma(lambda e: e.dma_start(pre1, wsrc[1]))
                slab_toks[1] = WQ.close()
                slab_ap = {0: pre0, 1: pre1}
                WQ.emit_all()
                lg.wait("gpsimd", "pe", first_slab_extra_pe)
                last_evac = 0
                for oc in range(32):
                    par = oc % 2
                    banks = (0, 1) if par == 0 else (2, 3)
                    for tg in range(2):
                        waits = [("wqs", slab_toks[oc]),
                                 ("lds", g_consts)]
                        if in_dve:
                            waits.append(("dve", in_dve))
                        if in_act:
                            waits.append(("act", in_act))
                        sap_ = slab_ap.get(oc)
                        pe_t = mm_group(
                            banks[tg], ps[banks[tg]][:],
                            [(lambda ic=ic, par=par, sap_=sap_:
                              (sap_ if sap_ is not None else sl_h(par))
                              [:, ic * 128:(ic + 1) * 128],
                              lambda ic=ic, tg=tg:
                              in_c(ic)[:, tg * 512:(tg + 1) * 512])
                             for ic in range(32)],
                            waits=waits)
                        tok["h_slab_pe"][par] = pe_t
                        lg.wait("scalar", "pe", pe_t)
                        lg.wait("scalar", "dve", evac_dve_guard)
                        ta = lg.op(
                            "scalar",
                            lambda e, oc=oc, tg=tg, b=ps[banks[tg]], ss=scale_s:
                            e.activation(out_c(oc)[:, tg * 512:(tg + 1) * 512],
                                         b[:], AF.Gelu,
                                         scale=ss[:, oc:oc + 1]), inc="act")
                        bank_guard[banks[tg]] = ("act", ta)
                        last_evac = ta
                    if oc + 2 < 32:
                        load_h(oc + 2)
                return last_evac

            first_extra = max(tok["w1_slab"] + tok["w2_slab"] +
                              [tok["slab_attn_pe"]])
            g1_done_act = hidden_layer(
                h1t, h_c, g1_c, sh1_s, h_done_dve, 0, g1_guard_dve,
                first_extra,
                gbuf[:, 16384:20480], gbuf[:, 20480:24576],
                [("dve", xup[7]), ("act", act_ln2), ("pe", tok["qt_pe"])])
            tok["h_read_pe"] = max(tok["h_slab_pe"])
            g2_done_act = hidden_layer(
                h2t, g1_c, g2_c, sh2_s, 0, g1_done_act, h_done_dve, 0,
                hbuf[:, 16384:20480], hbuf[:, 20480:24576],
                [("pe", max(tok["h_slab_pe"])), ("dve", h_done_dve)])
            gbuf_pe = max(tok["h_slab_pe"])

            def ln4_half(base, srcs4, toks4, Hh):
                stok4 = ln_stats_batch(srcs4, toks4, "dve", cb=base)
                for j in range(4):
                    i = base + j
                    lg.wait("vector", "dve", stok4)
                    lg.op("vector", lambda e, i=i: e.scalar_tensor_tensor(
                        x_tile(i), y3_t(i), stats[:, C_R + i:C_R + i + 1],
                        x_tile(i), ALU.mult, ALU.add))
                    tx = lg.op("vector", lambda e, i=i: e.tensor_scalar(
                        x_tile(i), x_tile(i),
                        stats[:, C_NMR + i:C_NMR + i + 1],
                        None, ALU.add), inc="dve")
                    lg.wait("gpsimd", "dve", tx)
                    SY.dma(lambda e, i=i, Hh=Hh: e.dma_start(
                        out_t[Hh * 8 + i], x_tile(i)))
                    tok["store"][i] = SY.close()

            w3_toks = {}
            w3_pe = {}

            def load_w3(oq):
                par = oq % 2
                WQ.emit_all()
                if par == 0:
                    lg.wait("gpsimd", "pe", tok["xfm_pe"])
                else:
                    lg.wait("gpsimd", "pe", max(tok["h_slab_pe"]))
                    lg.wait("gpsimd", "pe", tok["slab_attn_pe"])
                if oq >= 2:
                    lg.wait("gpsimd", "pe", w3_pe[oq - 2])
                WQ.dma(lambda e, oq=oq, par=par: e.dma_start(
                    sl_w3(par), w3t[oq % 4]))
                w3_toks[oq] = WQ.close()
                WQ.emit_all()

            load_w3(0)
            load_w3(1)
            y3_dve_toks = [0] * 8
            for L in range(8):
                grp, oq, par = L // 4, L % 4, L % 2
                last_pe = 0
                for tt in range(grp * 4, grp * 4 + 4):
                    bi = 4 + (tt % 4)
                    pe_t = mm_group(
                        bi, ps[bi][:, 0:256],
                        [(lambda ic=ic, tt=tt:
                          g2_c(ic)[:, tt * 128:(tt + 1) * 128],
                          lambda ic=ic, par=par:
                          sl_w3(par)[:, ic * 256:(ic + 1) * 256])
                         for ic in range(32)],
                        waits=[("wqs", w3_toks[L]), ("act", g2_done_act),
                               ("lds", g_consts)])
                    last_pe = pe_t
                    lg.wait("vector", "pe", pe_t)
                    tv = lg.op("vector", lambda e, oq=oq, tt=tt, bi=bi:
                               e.tensor_tensor(
                                   y3_t(tt)[:, oq * 256:(oq + 1) * 256],
                                   ps[bi][:, 0:256],
                                   s3_bc[:, oq * 256:(oq + 1) * 256], ALU.mult),
                               inc="dve")
                    bank_guard[bi] = ("dve", tv)
                    y3_dve_toks[tt] = tv
                w3_pe[L] = last_pe
                if L == 3:
                    ln4_half(0, [y3_t(i) for i in range(4)],
                             y3_dve_toks[0:4], Hh)
                if L == 7:
                    ln4_half(4, [y3_t(i) for i in range(4, 8)],
                             y3_dve_toks[4:8], Hh)
                if L + 2 < 8:
                    load_w3(L + 2)
            WQ.emit_all()
            tok["xfm_pe"] = max(tok["xfm_pe"], w3_pe[6])
            tok["slab_attn_pe"] = max(tok["slab_attn_pe"], w3_pe[7])
            tok["w1_slab"] = [w3_pe[7]] * 2
            tok["w2_slab"] = [w3_pe[7]] * 2
            tok["h_read_pe"] = max(tok["h_read_pe"], w3_pe[7])
            tok["qt_pe"] = max(tok["qt_pe"], gbuf_pe, w3_pe[7])
            if debug_tag == "y3" and Hh == 0:
                dbg_tap("y3", lambda: y3_t(0), 1024, wait_sems=("dve",))

            g1_guard_dve = lg.count["dve"]

        g_final = SY.flush()
        TR.flush()
        WQ.flush()
        lg.wait("gpsimd", "lds", g_final)

        cleanup = True
        with nc.Block() as block:
            @block.sync
            def _(e):
                for fn in lg.prog["sync"]:
                    fn(e)

            @block.gpsimd
            def _(e):
                for fn in lg.prog["gpsimd"]:
                    fn(e)

            @block.tensor
            def _(e):
                for fn in lg.prog["tensor"]:
                    fn(e)

            @block.vector
            def _(e):
                for fn in lg.prog["vector"]:
                    fn(e)

            @block.scalar
            def _(e):
                for fn in lg.prog["scalar"]:
                    fn(e)

        nc.all_engine_barrier()
        for h in lg.sems.values():
            nc.gpsimd.sem_clear(h)
        nc.all_engine_barrier()

    nc._dev_ledger = lg.events if hasattr(lg, 'events') else None
    return nc



def _prep_consts(inp):
    f = np.float32
    Wk = np.ascontiguousarray(np.asarray(inp["L0_attn_Wk"], f).T)
    Wv = np.ascontiguousarray(np.asarray(inp["L0_attn_Wv"], f).T)
    Wq = np.ascontiguousarray(np.asarray(inp["L0_attn_Wq"], f).T)
    Wo = np.ascontiguousarray(np.asarray(inp["L0_attn_Wo"], f).T)
    bk = np.ascontiguousarray(np.asarray(inp["L0_attn_bk"], f).reshape(M, 1))
    bv = np.ascontiguousarray(np.asarray(inp["L0_attn_bv"], f).reshape(M, 1))
    bq = np.ascontiguousarray(np.asarray(inp["L0_attn_bq"], f).reshape(M, 1))
    bo = np.ascontiguousarray(
        np.broadcast_to(np.asarray(inp["L0_attn_bo"], f), (128, D)))
    s3 = np.ascontiguousarray(
        np.broadcast_to(np.asarray(inp["L0_ffn_w3_s"], f), (128, D)))

    def tile_wT(w, n_in_chunks, n_out_chunks):
        wt = np.asarray(w, f).T
        t = wt.reshape(n_in_chunks, 128, n_out_chunks, 128)
        t = np.ascontiguousarray(t.transpose(2, 1, 0, 3))
        return t.reshape(n_out_chunks, 128, n_in_chunks * 128)

    w1t = tile_wT(inp["L0_ffn_w1_w"], 8, 32)
    w2t = tile_wT(inp["L0_ffn_w2_w"], 8, 32)
    h1t = tile_wT(inp["L0_ffn_h1_w"], 32, 32)
    h2t = tile_wT(inp["L0_ffn_h2_w"], 32, 32)
    w3T = np.asarray(inp["L0_ffn_w3_w"], f).T
    t = w3T.reshape(32, 128, 4, 256)
    w3t = np.ascontiguousarray(t.transpose(2, 1, 0, 3)).reshape(4, 128, 32 * 256)

    sc = lambda s_: np.ascontiguousarray(np.asarray(s_, f).reshape(32, 128).T)
    s1, s2 = sc(inp["L0_ffn_w1_s"]), sc(inp["L0_ffn_w2_s"])
    sh1, sh2 = sc(inp["L0_ffn_h1_s"]), sc(inp["L0_ffn_h2_s"])

    jj, ii = np.meshgrid(np.arange(128), np.arange(128), indexing="ij")
    mka = np.where(ii > jj, LR * DECAY ** (ii - 1 - jj), 0.0).astype(f)
    mkd = np.ascontiguousarray(
        np.broadcast_to((DECAY ** np.arange(128)).astype(f), (128, 128)))
    mkw = np.ascontiguousarray(
        np.broadcast_to((LR * DECAY ** (127 - np.arange(128))).astype(f),
                        (128, 128)))
    return dict(wk=Wk, wv=Wv, wq=Wq, wo=Wo, bk=bk, bv=bv, bq=bq, bo=bo,
                s3=s3, w1t=w1t, w2t=w2t, h1t=h1t, h2t=h2t, w3t=w3t,
                s1=s1, s2=s2, sh1=sh1, sh2=sh2, mka=mka, mkd=mkd, mkw=mkw,
                identin=np.eye(128, dtype=f))


def make_in_maps(inputs):
    x = np.asarray(inputs["L0_attn_x"], np.float32)
    consts = _prep_consts(inputs)
    zeros = np.zeros((TOK, D), np.float32)
    ones_m = np.ones((128, 1), np.float32)
    zeros_m = np.zeros((128, 1), np.float32)
    in_maps = []
    for c in range(8):
        b, h = c // 2, c % 2
        m = dict(consts)
        m["xo"] = np.ascontiguousarray(x[b, h * TOK:(h + 1) * TOK])
        m["xp"] = np.ascontiguousarray(x[b, 0:TOK]) if h == 1 else zeros
        m["smask"] = ones_m if h == 1 else zeros_m
        in_maps.append(m)
    return in_maps


def kernel(**inputs):
    nc = build_program()
    in_maps = make_in_maps(inputs)
    res = run_bass_kernel_spmd(nc, in_maps, core_ids=list(range(8)))
    out = np.empty((B, S, D), np.float32)
    for c in range(8):
        b, h = c // 2, c % 2
        out[b, h * TOK:(h + 1) * TOK] = res.results[c]["out"]
    return out


# version 17
# speedup vs baseline: 1.2045x; 1.2045x over previous
from collections import deque
from contextlib import ExitStack

import numpy as np

import concourse.bass as bass
import concourse.mybir as mybir
from concourse.bass_utils import run_bass_kernel_spmd

F32 = mybir.dt.float32
BF16 = mybir.dt.bfloat16
AF = mybir.ActivationFunctionType
ALU = mybir.AluOpType
AX = mybir.AxisListType

B, S, D, M, HID = 4, 4096, 1024, 128, 4096
TOK = 2048
DECAY, LR = 0.99, 0.01
EPS = 1e-5
DECAY_C = float(DECAY ** 128)
R1024 = 1.0 / 1024.0
USE_TS = False

ENGS = ("sync", "gpsimd", "tensor", "vector", "scalar")
C_SUM, C_SQ, C_M, C_VAR, C_SD, C_R, C_NMR, C_TMP = (i * 8 for i in range(8))


class Ledger:
    def __init__(self, nc, ctx, sems):
        self.nc = nc
        self.prog = {e: [] for e in ENGS}
        self.sems = {}
        self.count = {}
        self.waited = {e: {} for e in ENGS}
        self.events = {e: [] for e in ENGS}
        for s in sems:
            self.sems[s] = ctx.enter_context(nc.semaphore(s))
            self.count[s] = 0

    def wait(self, eng, name, val=None):
        if val is None:
            val = self.count[name]
        if val <= 0 or val <= self.waited[eng].get(name, 0):
            return
        self.waited[eng][name] = val
        self.events[eng].append(("w", name, val))
        h = self.sems[name]
        self.prog[eng].append(lambda e, h=h, v=val: e.wait_ge(h, v))

    def op(self, eng, fn, inc=None, inck=1):
        if inc is None:
            self.prog[eng].append(fn)
            return None
        self.count[inc] += inck
        self.events[eng].append(("i", inc, inck))
        h = self.sems[inc]
        self.prog[eng].append(lambda e, fn=fn, h=h, k=inck: fn(e).then_inc(h, k))
        return self.count[inc]


class DmaStream:

    def __init__(self, lg, eng, qnames, sig):
        self.lg = lg
        self.eng = eng
        self.q = qnames
        self.sig = sig
        self.qi = 0
        self.issued = {q: 0 for q in qnames}
        self.open = False
        self.pending = deque()
        self.n_closed = 0

    def begin(self):
        if not self.open:
            q = self.q[self.qi]
            while any(p[0] == q for p in self.pending):
                self._emit_one()
            self.open = True

    def dma(self, fn):
        self.begin()
        q = self.q[self.qi]
        self.issued[q] += 1
        self.lg.op(self.eng, fn, inc=q, inck=16)

    def close(self):
        if not self.open:
            return self.n_closed
        q = self.q[self.qi]
        self.n_closed += 1
        self.pending.append((q, self.issued[q] * 16, self.n_closed))
        self.open = False
        self.qi = (self.qi + 1) % len(self.q)
        return self.n_closed

    def _emit_one(self):
        q, v, _ = self.pending.popleft()
        self.lg.wait(self.eng, q, v)
        h = self.lg.sems[self.sig]
        self.lg.count[self.sig] += 1
        self.lg.events[self.eng].append(("i", self.sig, 1))
        self.lg.prog[self.eng].append(lambda e, h=h: e.sem_inc(h, 1))

    def ensure(self, gtok):
        while self.pending and self.lg.count[self.sig] < gtok:
            self._emit_one()

    def emit_all(self):
        while self.pending:
            self._emit_one()

    def flush(self):
        self.close()
        while self.pending:
            self._emit_one()
        return self.lg.count[self.sig]


def build_program(debug_tag=None):
    nc = bass.Bass("TRN2", target_bir_lowering=False)

    di = lambda n, shp: nc.dram_tensor(n, shp, F32, kind="ExternalInput").ap()
    xo = di("xo", [TOK, D])
    xp = di("xp", [TOK, D])
    wk, wq, wv = di("wk", [D, M]), di("wq", [D, M]), di("wv", [D, M])
    wo = di("wo", [M, D])
    bk, bq, bv = di("bk", [M, 1]), di("bq", [M, 1]), di("bv", [M, 1])
    bo_in = di("bo", [128, D])
    s3_in = di("s3", [128, D])
    s1_in, s2_in = di("s1", [128, 32]), di("s2", [128, 32])
    sh1_in, sh2_in = di("sh1", [128, 32]), di("sh2", [128, 32])
    mka_in = di("mka", [128, 128])
    mkd_in = di("mkd", [128, 128])
    mkw_in = di("mkw", [128, 128])
    smask_in = di("smask", [128, 1])
    ident_in = di("identin", [128, 128])
    w1t = di("w1t", [32, 128, 1024])
    w2t = di("w2t", [32, 128, 1024])
    h1t = di("h1t", [32, 128, 4096])
    h2t = di("h2t", [32, 128, 4096])
    w3t = di("w3t", [4, 128, 8192])
    out = nc.dram_tensor("out", [TOK, D], F32, kind="ExternalOutput").ap()
    dbg = None
    if debug_tag is not None:
        dbg = nc.dram_tensor("dbg", [128, 4096], F32, kind="ExternalOutput").ap()

    xo_t = xo.rearrange("(t p) d -> t p d", p=128)
    xp_t = xp.rearrange("(t p) d -> t p d", p=128)
    out_t = out.rearrange("(t p) d -> t p d", p=128)

    ctx = ExitStack()
    with ctx:
        sb = lambda n, shp, dt: ctx.enter_context(nc.sbuf_tensor(n, shp, dt))
        xres = sb("xres", [128, 8 * 1024], F32)
        bk_s, bq_s, bv_s = (sb(n, [128, 1], F32) for n in ("bk_s", "bq_s", "bv_s"))
        s3_bc = sb("s3_bc", [128, 1024], F32)
        s1_s, s2_s = sb("s1_s", [128, 32], F32), sb("s2_s", [128, 32], F32)
        sh1_s, sh2_s = sb("sh1_s", [128, 32], F32), sb("sh2_s", [128, 32], F32)
        mask_a = sb("mask_a", [128, 128], F32)
        mask_dk = sb("mask_dk", [128, 128], BF16)
        mask_kw = sb("mask_kw", [128, 128], BF16)
        smask = sb("smask_b", [128, 1], F32)
        st = sb("st", [128, 256], F32)
        stats = sb("stats", [128, 64], F32)
        ident = sb("ident", [128, 128], F32)
        xfm = sb("xfm", [128, 8192], BF16)
        kqv = sb("kqv", [128, 3840], BF16)
        hbuf = sb("hbuf", [128, 32768], BF16)
        gbuf = sb("gbuf", [128, 32768], BF16)
        slab = sb("slab", [128, 8192], BF16)

        x_tile = lambda i: xres[:, i * 1024:(i + 1) * 1024]
        xfm_c = lambda c: xfm[:, c * 1024:(c + 1) * 1024]
        cw_k = slab[:, 0:1024]
        cw_q = slab[:, 1024:2048]
        cw_v = slab[:, 2048:3072]
        cw_o = slab[:, 3072:4096]
        bo_bc = gbuf[:, 24576:26624].bitcast(F32)
        k_fm = kqv[:, 0:1024]
        q_fm = kqv[:, 1024:2048]
        v_fm = kqv[:, 2048:3072]
        kw_scr = lambda p: kqv[:, 3072 + p * 128: 3072 + (p + 1) * 128]
        ktm_scr = lambda p: kqv[:, 3328 + p * 128: 3328 + (p + 1) * 128]
        vtm_scr = lambda p: kqv[:, 3584 + p * 128: 3584 + (p + 1) * 128]
        t1_buf = lambda par: kqv[:, par * 1024:(par + 1) * 1024].bitcast(F32)
        h_c = lambda ic: hbuf[:, ic * 1024:(ic + 1) * 1024]
        g2_c = h_c
        xp_buf = lambda i: hbuf[:, i * 2048:(i + 1) * 2048].bitcast(F32)
        g1_c = lambda ic: gbuf[:, ic * 1024:(ic + 1) * 1024]
        y_buf = gbuf[:, 0:2048].bitcast(F32)
        xln_buf = lambda p: gbuf[:, 2048 + p * 1024: 2048 + (p + 1) * 1024]
        sq_buf = gbuf[:, 4096:6144].bitcast(F32)
        qt_all = lambda c: gbuf[:, 6144 + c * 128: 6144 + (c + 1) * 128]
        kw_f32 = lambda c: gbuf[:, 26624 + c * 256: 26624 + (c + 1) * 256].bitcast(F32)
        ktm_all = lambda c: gbuf[:, 12288 + c * 128: 12288 + (c + 1) * 128]
        v_f32 = gbuf[:, 14336:16384].bitcast(F32)
        y3_t = lambda i: gbuf[:, 8192 + i * 2048: 8192 + (i + 1) * 2048].bitcast(F32)
        sl_w1 = lambda p: slab[:, p * 1024:(p + 1) * 1024]
        sl_w2 = lambda p: slab[:, 2048 + p * 1024: 2048 + (p + 1) * 1024]
        sl_h = lambda p: slab[:, p * 4096:(p + 1) * 4096]
        vtm_all = lambda c: slab[:, 4096 + c * 128: 4096 + (c + 1) * 128]
        abf_all = lambda c: slab[:, 5120 + c * 128: 5120 + (c + 1) * 128]
        snap_all = lambda c: slab[:, 6144 + c * 128: 6144 + (c + 1) * 128]
        outs_all = lambda c: slab[:, 7168 + c * 128: 7168 + (c + 1) * 128]
        sl_w3 = lambda p: (xfm if p == 0 else slab)[:, 0:8192]

        st_slot = [0]
        st_cur = lambda: st[:, st_slot[0] * 128:(st_slot[0] + 1) * 128]
        st_nxt = lambda: st[:, (1 - st_slot[0]) * 128:(2 - st_slot[0]) * 128]

        ps = [ctx.enter_context(nc.psum_tensor(f"ps{i}", [128, 512], F32))
              for i in range(8)]
        bank_guard = [("dve", 0)] * 8

        lg = Ledger(nc, ctx, (
            "q_sy0", "q_sy1", "q_tr0", "q_tr1", "q_tr2", "q_tr3",
            "q_ts0", "q_ts1", "q_ts2", "q_ts3",
            "q_wq0", "q_wq1", "lds", "trs", "tss", "wqs", "pe", "act", "dve", "gp"))
        SY = DmaStream(lg, "gpsimd", ["q_sy0", "q_sy1"], "lds")
        TR = DmaStream(lg, "scalar", ["q_tr0", "q_tr1", "q_tr2", "q_tr3"], "trs")
        TS = DmaStream(lg, "sync", ["q_ts0", "q_ts1", "q_ts2", "q_ts3"], "tss")
        WQ = DmaStream(lg, "gpsimd", ["q_wq0", "q_wq1"], "wqs")

        for h in lg.sems.values():
            nc.gpsimd.sem_clear(h)
        nc.all_engine_barrier()

        dbg_state = {"done": False}

        def dbg_tap(tag, ap_fn, ncols, wait_sems=()):
            if debug_tag != tag or dbg_state["done"]:
                return
            dbg_state["done"] = True
            if "lds" in wait_sems:
                gg = SY.close()
                SY.ensure(gg)
            for sname in wait_sems:
                lg.wait("gpsimd", sname)
            SY.close()
            SY.dma(lambda e, a=ap_fn(), n=ncols: e.dma_start(dbg[:, 0:n], a))
            g = SY.close()
            SY.ensure(g)
            for e_ in ENGS:
                lg.wait(e_, "lds", g)

        cwtok = {"g": 0}

        def load_cw(guard_pe):
            WQ.emit_all()
            lg.wait("gpsimd", "pe", guard_pe)
            WQ.dma(lambda e: e.dma_start(
                cw_k.rearrange("p (c m) -> p c m", m=128),
                wk.rearrange("(c p) m -> p c m", p=128)))
            WQ.dma(lambda e: e.dma_start(
                cw_q.rearrange("p (c m) -> p c m", m=128),
                wq.rearrange("(c p) m -> p c m", p=128)))
            WQ.dma(lambda e: e.dma_start(
                cw_v.rearrange("p (c m) -> p c m", m=128),
                wv.rearrange("(c p) m -> p c m", p=128)))
            WQ.dma(lambda e: e.dma_start(cw_o, wo))
            cwtok["g"] = WQ.close()
            WQ.emit_all()

        load_cw(0)
        WQ.dma(lambda e: e.dma_start(mask_dk[:], mkd_in))
        WQ.dma(lambda e: e.dma_start(mask_kw[:], mkw_in))
        g_masks = WQ.close()
        WQ.emit_all()

        for dst, src in ((bk_s[:], bk), (bq_s[:], bq), (bv_s[:], bv),
                         (s3_bc[:], s3_in), (s1_s[:], s1_in), (s2_s[:], s2_in),
                         (sh1_s[:], sh1_in), (sh2_s[:], sh2_in),
                         (mask_a[:], mka_in), (smask[:], smask_in),
                         (ident[:], ident_in)):
            SY.dma(lambda e, d=dst, s=src: e.dma_start(d, s))
        g_consts = SY.close()

        lg.op("vector", lambda e: e.memset(st[:, 0:128], 0.0))


        tok = {
            "xfm_pe": 0,
            "xln_tr": [None, None],
            "xsc_pe": [0, 0],
            "kw_tr": [0, 0],
            "vtm_pe": [0, 0],
            "w1_slab": [0, 0], "w2_slab": [0, 0],
            "h_slab_pe": [0, 0],
            "slab_attn_pe": 0,
            "qt_pe": 0,
            "t1_free": {},
            "store": [0] * 8,
            "y_sq_act": 0,
            "st_tok": 0,
            "h_read_pe": 0,
        }

        def mm_group(bank_idx, out_ap, pairs, waits=()):
            sname, v = bank_guard[bank_idx]
            lg.wait("tensor", sname, v)
            for sn, vv in waits:
                lg.wait("tensor", sn, vv)
            n = len(pairs)
            t = None
            for j, (lf, rf) in enumerate(pairs):
                t = lg.op("tensor",
                          lambda e, lf=lf, rf=rf, st_=(j == 0), sp=(j == n - 1),
                          o=out_ap: e.matmul(o, lf(), rf(), start=st_, stop=sp),
                          inc="pe" if j == n - 1 else None)
            return t

        def ln_stats_batch(srcs, src_toks, src_sem, cb=0):
            n = len(srcs)
            a_last = None
            for i, (sap, stok) in enumerate(zip(srcs, src_toks)):
                if stok:
                    lg.wait("scalar", src_sem, stok)
                a_last = lg.op(
                    "scalar",
                    lambda e, sap=sap, i=i, cb=cb: e.activation(
                        sq_buf, sap, AF.Square,
                        accum_out=stats[:, C_SQ + cb + i:C_SQ + cb + i + 1]),
                    inc="act")
            t_red = None
            for i, (sap, stok) in enumerate(zip(srcs, src_toks)):
                if stok:
                    lg.wait("vector", src_sem, stok)
                t_red = lg.op("vector",
                              lambda e, sap=sap, i=i, cb=cb: e.reduce_sum(
                                  stats[:, C_SUM + cb + i:C_SUM + cb + i + 1], sap,
                                  axis=AX.X), inc="dve")
            lg.wait("vector", "act", a_last)
            lg.wait("vector", "dve", t_red)
            tM = lg.op("vector", lambda e, n=n, cb=cb: e.tensor_scalar(
                stats[:, C_M + cb:C_M + cb + n], stats[:, C_SUM + cb:C_SUM + cb + n],
                R1024, None, ALU.mult), inc="dve")
            lg.op("vector", lambda e, n=n, cb=cb: e.tensor_scalar(
                stats[:, C_SD + cb:C_SD + cb + n], stats[:, C_SQ + cb:C_SQ + cb + n],
                R1024, None, ALU.mult))
            lg.wait("vector", "dve", tM)
            tT = lg.op("vector", lambda e, n=n, cb=cb: e.tensor_tensor(
                stats[:, C_TMP + cb:C_TMP + cb + n], stats[:, C_M + cb:C_M + cb + n],
                stats[:, C_M + cb:C_M + cb + n], ALU.mult), inc="dve")
            lg.wait("vector", "dve", tT)
            tv = lg.op("vector", lambda e, n=n, cb=cb: e.scalar_tensor_tensor(
                stats[:, C_VAR + cb:C_VAR + cb + n], stats[:, C_SD + cb:C_SD + cb + n], EPS,
                stats[:, C_TMP + cb:C_TMP + cb + n], ALU.add, ALU.subtract), inc="dve")
            lg.wait("scalar", "dve", tv)
            ta = lg.op("scalar", lambda e, n=n, cb=cb: e.activation(
                stats[:, C_SD + cb:C_SD + cb + n], stats[:, C_VAR + cb:C_VAR + cb + n],
                AF.Sqrt), inc="act")
            lg.wait("vector", "act", ta)
            tR = lg.op("vector", lambda e, n=n, cb=cb: e.reciprocal(
                stats[:, C_R + cb:C_R + cb + n], stats[:, C_SD + cb:C_SD + cb + n]), inc="dve")
            lg.wait("vector", "dve", tR)
            tn = lg.op("vector", lambda e, n=n, cb=cb: e.scalar_tensor_tensor(
                stats[:, C_NMR + cb:C_NMR + cb + n], stats[:, C_M + cb:C_M + cb + n], -1.0,
                stats[:, C_R + cb:C_R + cb + n], ALU.mult, ALU.mult), inc="dve")
            return tn

        def trx_wait(eng, toks):
            if toks is None:
                return
            for s_, v in toks:
                lg.wait(eng, s_, v)

        def tok_waits(toklists):
            m = {}
            for tl in toklists:
                if tl is None:
                    continue
                for s_, v in tl:
                    m[s_] = max(m.get(s_, 0), v)
            return list(m.items())

        def pe_tr_block(dst_ap, src_ap, bank_idx, waits, xfm_guard_pe):
            sname, v = bank_guard[bank_idx]
            lg.wait("tensor", sname, v)
            for sn, vv in waits:
                lg.wait("tensor", sn, vv)
            lg.wait("tensor", "lds", g_consts)
            t = lg.op("tensor", lambda e, d=src_ap, bi=bank_idx: e.transpose(
                ps[bi][:, 0:128], d, ident[:]), inc="pe")
            lg.wait("vector", "pe", t)
            lg.wait("vector", "pe", xfm_guard_pe)
            ev = lg.op("vector", lambda e, o=dst_ap, bi=bank_idx: e.tensor_copy(
                o, ps[bi][:, 0:128]), inc="dve")
            bank_guard[bank_idx] = ("dve", ev)
            return t, ev

        def ln_to_xfm(srcs, src_toks, src_sem, base=0, dma_path=False):
            stok = ln_stats_batch(srcs, src_toks, src_sem, cb=base)
            tr_toks = [None] * len(srcs)
            for j, sap in enumerate(srcs):
                i = base + j
                p = i % 2
                if dma_path:
                    lg.wait("scalar", "dve", stok)
                    trx_wait("scalar", tok["xln_tr"][p])
                    t_ap = lg.op("scalar", lambda e, s=sap, i=i, p=p:
                                 e.activation(
                                     xln_buf(p), s, AF.Identity,
                                     scale=stats[:, C_R + i:C_R + i + 1],
                                     bias=stats[:, C_NMR + i:C_NMR + i + 1]),
                                 inc="act")
                    lg.wait("scalar", "pe", tok["xfm_pe"])
                    lg.wait("scalar", "act", t_ap)
                    for c in range(8):
                        TR.dma(lambda e, c=c, i=i, p=p: e.dma_start(
                            xfm_c(c)[:, i * 128:(i + 1) * 128],
                            xln_buf(p)[:, c * 128:(c + 1) * 128],
                            transpose=True))
                    g = TR.close()
                    TR.emit_all()
                    tok["xln_tr"][p] = [("trs", g)]
                    tr_toks[j] = [("trs", g)]
                    continue
                xsc = y3_t(p)
                lg.wait("scalar", "dve", stok)
                lg.wait("scalar", "pe", tok["xsc_pe"][p])
                t_ap = lg.op("scalar", lambda e, s=sap, i=i, x=xsc: e.activation(
                    x, s, AF.Identity,
                    scale=stats[:, C_R + i:C_R + i + 1],
                    bias=stats[:, C_NMR + i:C_NMR + i + 1]), inc="act")
                last_ev = 0
                last_t = 0
                for c in range(8):
                    last_t, last_ev = pe_tr_block(
                        xfm_c(c)[:, i * 128:(i + 1) * 128],
                        xsc[:, c * 128:(c + 1) * 128],
                        6 + c % 2, [("act", t_ap)], tok["xfm_pe"])
                tok["xsc_pe"][p] = last_t
                tr_toks[j] = [("dve", last_ev)]
            return tr_toks

        def proj_pass(tr_toks, do_q):
            plist = [("k", cw_k, bk_s, k_fm, (0, 1)),
                     ("v", cw_v, bv_s, v_f32, (2, 3))]
            if do_q:
                plist.append(("q", cw_q, bq_s, q_fm, (4, 5)))
            ev = {}
            for name, cw, bias, dst, banks in plist:
                ev[name] = [0, 0]
                for tg in range(2):
                    bi = banks[tg]
                    tws = tok_waits(tr_toks[tg * 4:(tg + 1) * 4])
                    pe_t = mm_group(
                        bi, ps[bi][:],
                        [(lambda ic=ic, cw=cw: cw[:, ic * 128:(ic + 1) * 128],
                          lambda ic=ic, tg=tg: xfm_c(ic)[:, tg * 512:(tg + 1) * 512])
                         for ic in range(8)],
                        waits=tws + [("wqs", cwtok["g"]), ("lds", g_consts)])
                    lg.wait("scalar", "pe", pe_t)
                    lg.wait("scalar", "pe", tok["qt_pe"])
                    a_t = lg.op("scalar",
                                lambda e, b=ps[bi], d=dst, tg=tg, bias=bias:
                                e.activation(d[:, tg * 512:(tg + 1) * 512], b[:],
                                             AF.Identity, bias=bias[:]),
                                inc="act")
                    ev[name][tg] = a_t
                    bank_guard[bi] = ("act", a_t)
            return ev

        def attn_chunks(ev, full):
            kw_toks = [0] * 8
            abf_toks = [0] * 8
            qt_toks = [0] * 8
            snap_toks = [0] * 8
            for c in range(8):
                par = c % 2
                lg.wait("vector", "act", ev["k"][c // 4])
                lg.wait("vector", "wqs", g_masks)
                lg.wait("vector", "pe", tok["qt_pe"])
                kw_toks[c] = lg.op("vector", lambda e, c=c: e.tensor_tensor(
                    kw_f32(c), k_fm[:, c * 128:(c + 1) * 128], mask_kw[:],
                    ALU.mult), inc="dve")
                if full:
                    lg.wait("vector", "act", ev["q"][c // 4])
                    qt_toks[c] = lg.op("vector", lambda e, c=c: e.tensor_tensor(
                        qt_all(c), q_fm[:, c * 128:(c + 1) * 128], mask_dk[:],
                        ALU.mult), inc="dve")
                    bi_a = 6 + par
                    pe_a = mm_group(
                        bi_a, ps[bi_a][:, 0:128],
                        [(lambda c=c: k_fm[:, c * 128:(c + 1) * 128],
                          lambda c=c: q_fm[:, c * 128:(c + 1) * 128])],
                        waits=[("act", max(ev["k"][c // 4], ev["q"][c // 4]))])
                    lg.wait("vector", "pe", pe_a)
                    lg.wait("vector", "lds", g_consts)
                    lg.wait("vector", "pe", tok["slab_attn_pe"])
                    abf_toks[c] = lg.op("vector", lambda e, c=c, bi=bi_a:
                                        e.tensor_tensor(abf_all(c),
                                                        ps[bi][:, 0:128],
                                                        mask_a[:], ALU.mult),
                                        inc="dve")
                    bank_guard[bi_a] = ("dve", abf_toks[c])
            tp_toks = [None] * 8
            for c in range(8):
                _, ev_k = pe_tr_block(
                    ktm_all(c), kw_f32(c), c % 4,
                    [("dve", kw_toks[c]), ("pe", tok["qt_pe"])], 0)
                _, ev_v = pe_tr_block(
                    vtm_all(c), v_f32[:, c * 128:(c + 1) * 128], (c + 1) % 4,
                    [("act", ev["v"][c // 4]), ("pe", tok["slab_attn_pe"])], 0)
                tp_toks[c] = [("dve", ev_v)]
            tr_pair = None
            for c in range(8):
                par = c % 2
                bi = 4 + par
                pe_t = mm_group(
                    bi, ps[bi][:, 0:128],
                    [(lambda c=c: ktm_all(c), lambda c=c: vtm_all(c))],
                    waits=tp_toks[c])
                lg.wait("vector", "dve", tok["st_tok"])
                if full:
                    lg.wait("vector", "pe", tok["slab_attn_pe"])
                    _scur = st_cur()
                    snap_toks[c] = lg.op("vector", lambda e, c=c, s_=_scur:
                                         e.tensor_copy(snap_all(c), s_),
                                         inc="dve")
                lg.wait("vector", "pe", pe_t)
                cur, nxt = st_cur(), st_nxt()
                tu = lg.op("vector", lambda e, bi=bi, cur=cur, nxt=nxt:
                           e.scalar_tensor_tensor(nxt, cur, DECAY_C,
                                                  ps[bi][:, 0:128],
                                                  ALU.mult, ALU.add), inc="dve")
                st_slot[0] = 1 - st_slot[0]
                tok["st_tok"] = tu
                bank_guard[bi] = ("dve", tu)
            return abf_toks, snap_toks, qt_toks, tp_toks

        hb_act = 0
        hb_dve = 0
        for ph in range(2):
            xp_toks = []
            lg.wait("gpsimd", "act", hb_act)
            lg.wait("gpsimd", "dve", hb_dve)
            for i in range(8):
                SY.dma(lambda e, i=i, ph=ph: e.dma_start(
                    xp_buf(i), xp_t[ph * 8 + i]))
                if i % 4 == 3:
                    g = SY.close()
                    for j in range(i - 3, i + 1):
                        xp_toks.append(g)
            SY.emit_all()
            srcs = [xp_buf(i) for i in range(8)]
            if ph == 0:
                dbg_tap("pre_xp", lambda: xp_buf(0), 1024,
                        wait_sems=("lds",))
            tr_toks = ln_to_xfm(srcs, xp_toks, "lds")
            if ph == 0:
                dbg_tap("pre_stats", lambda: stats[:], 64,
                        wait_sems=("dve", "act"))
            hb_act = lg.count["act"]
            hb_dve = lg.count["dve"]
            ev = proj_pass(tr_toks, do_q=False)
            tok["xfm_pe"] = lg.count["pe"]
            attn_chunks(ev, full=False)
            tok["slab_attn_pe"] = lg.count["pe"]
            tok["qt_pe"] = lg.count["pe"]
        lg.wait("vector", "lds", g_consts)
        lg.wait("vector", "dve", tok["st_tok"])
        _cur, _nxt = st_cur(), st_nxt()
        tok["st_tok"] = lg.op("vector", lambda e, c=_cur, x=_nxt:
                              e.tensor_scalar(x, c, smask[:], None, ALU.mult),
                              inc="dve")
        st_slot[0] = 1 - st_slot[0]
        dbg_tap("prelude_st", lambda: st_cur(), 128, wait_sems=("dve",))

        g1_guard_dve = 0
        for Hh in range(2):
            WQ.emit_all()
            TR.emit_all()
            SY.emit_all()
            lg.wait("gpsimd", "pe", tok["qt_pe"])
            SY.dma(lambda e: e.dma_start(bo_bc, bo_in))
            if Hh == 1:
                load_cw(max(tok["w1_slab"] + tok["w2_slab"]))
            x_toks = []
            for i in range(8):
                if Hh == 1:
                    SY.ensure(tok["store"][i])
                    lg.wait("gpsimd", "lds", tok["store"][i])
                SY.dma(lambda e, i=i, Hh=Hh: e.dma_start(
                    x_tile(i), xo_t[Hh * 8 + i]))
                if i % 4 == 3:
                    g = SY.close()
                    for j in range(i - 3, i + 1):
                        x_toks.append(g)
            SY.emit_all()
            g_bo = x_toks[0]

            tr_toks = (ln_to_xfm([x_tile(i) for i in range(4)],
                                 x_toks[0:4], "lds", 0) +
                       ln_to_xfm([x_tile(i) for i in range(4, 8)],
                                 x_toks[4:8], "lds", 4))
            ev = proj_pass(tr_toks, do_q=True)
            tok["xfm_pe"] = lg.count["pe"]

            abf_toks, snap_toks, qt_toks, tp_toks = attn_chunks(ev, full=True)

            outs_toks = [0] * 8
            for c in range(8):
                par = c % 2
                bi_o = 6 + par
                pe_o = mm_group(
                    bi_o, ps[bi_o][:, 128:256],
                    [(lambda c=c: vtm_all(c), lambda c=c: abf_all(c)),
                     (lambda c=c: snap_all(c), lambda c=c: qt_all(c))],
                    waits=[("dve", max(abf_toks[c], snap_toks[c], qt_toks[c]))] +
                    tp_toks[c])
                tok["qt_pe"] = pe_o
                tok["slab_attn_pe"] = pe_o
                lg.wait("scalar", "pe", pe_o)
                outs_toks[c] = lg.op("scalar", lambda e, c=c, bi=bi_o:
                                     e.activation(outs_all(c), ps[bi][:, 128:256],
                                                  AF.Copy), inc="act")
                bank_guard[bi_o] = ("act", outs_toks[c])

            y_toks = [0] * 8
            for c in range(8):
                par = c % 2
                bY = (0, 1) if par == 0 else (2, 3)
                pe_y = 0
                for hd in range(2):
                    pe_y = mm_group(
                        bY[hd], ps[bY[hd]][:],
                        [(lambda c=c: outs_all(c),
                          lambda hd=hd: cw_o[:, hd * 512:(hd + 1) * 512])],
                        waits=[("act", outs_toks[c]), ("wqs", cwtok["g"])])
                tok["slab_attn_pe"] = max(tok["slab_attn_pe"], pe_y)
                lg.wait("vector", "pe", pe_y)
                lg.wait("vector", "lds", g_bo)
                ty = 0
                for hd in range(2):
                    ty = lg.op("vector", lambda e, c=c, hd=hd, bY=bY:
                               e.tensor_tensor(
                                   y3_t(c)[:, hd * 512:(hd + 1) * 512],
                                   ps[bY[hd]][:],
                                   bo_bc[:, hd * 512:(hd + 1) * 512], ALU.add),
                               inc="dve")
                    bank_guard[bY[hd]] = ("dve", ty)
                y_toks[c] = ty
            stok2 = ln_stats_batch([y3_t(c) for c in range(8)], y_toks, "dve")
            xup = [0] * 8
            for c in range(8):
                lg.wait("vector", "dve", stok2)
                lg.op("vector", lambda e, c=c: e.scalar_tensor_tensor(
                    x_tile(c), y3_t(c), stats[:, C_R + c:C_R + c + 1],
                    x_tile(c), ALU.mult, ALU.add))
                xup[c] = lg.op("vector", lambda e, c=c: e.tensor_scalar(
                    x_tile(c), x_tile(c), stats[:, C_NMR + c:C_NMR + c + 1],
                    None, ALU.add), inc="dve")
            tr_toks3 = (ln_to_xfm([x_tile(i) for i in range(4)],
                                  xup[0:4], "dve", 0) +
                        ln_to_xfm([x_tile(i) for i in range(4, 8)],
                                  xup[4:8], "dve", 4))
            attn_x_dve = lg.count["dve"]
            if debug_tag == "attn" and Hh == 0:
                dbg_tap("attn", lambda: x_tile(0), 1024, wait_sems=("dve",))


            w12_toks = {}

            def load_w12(oc):
                par = oc % 2
                WQ.emit_all()
                lg.wait("gpsimd", "pe", max(tok["w1_slab"][par],
                                            tok["w2_slab"][par],
                                            tok["xfm_pe"],
                                            tok["slab_attn_pe"]))
                WQ.dma(lambda e, oc=oc, par=par: e.dma_start(
                    sl_w1(par), w1t[oc]))
                WQ.dma(lambda e, oc=oc, par=par: e.dma_start(
                    sl_w2(par), w2t[oc]))
                w12_toks[oc] = WQ.close()

            load_w12(0)
            load_w12(1)
            h_done_dve = 0
            for oc in range(32):
                par = oc % 2
                p1b = (0, 1) if par == 0 else (4, 5)
                p2b = (2, 3) if par == 0 else (6, 7)
                for tg in range(2):
                    pe1 = mm_group(
                        p1b[tg], ps[p1b[tg]][:],
                        [(lambda ic=ic, par=par: sl_w1(par)[:, ic * 128:(ic + 1) * 128],
                          lambda ic=ic, tg=tg: xfm_c(ic)[:, tg * 512:(tg + 1) * 512])
                         for ic in range(8)],
                        waits=tok_waits(tr_toks3[tg * 4:(tg + 1) * 4]) + [
                               ("wqs", w12_toks[oc]), ("lds", g_consts)])
                    pe2 = mm_group(
                        p2b[tg], ps[p2b[tg]][:],
                        [(lambda ic=ic, par=par: sl_w2(par)[:, ic * 128:(ic + 1) * 128],
                          lambda ic=ic, tg=tg: xfm_c(ic)[:, tg * 512:(tg + 1) * 512])
                         for ic in range(8)],
                        waits=[("wqs", w12_toks[oc])])
                    tok["w1_slab"][par] = pe1
                    tok["w2_slab"][par] = pe2
                    tok["xfm_pe"] = pe2
                    lg.wait("scalar", "pe", pe1)
                    lg.wait("scalar", "dve", tok["t1_free"].get(par, 0))
                    ta = lg.op("scalar",
                               lambda e, oc=oc, par=par, b=ps[p1b[tg]]:
                               e.activation(t1_buf(par), b[:], AF.Silu,
                                            scale=s1_s[:, oc:oc + 1]), inc="act")
                    bank_guard[p1b[tg]] = ("act", ta)
                    lg.wait("vector", "pe", pe2)
                    lg.wait("vector", "act", ta)
                    lg.wait("vector", "pe", tok["h_read_pe"])
                    th = lg.op("vector",
                               lambda e, oc=oc, par=par, tg=tg, b=ps[p2b[tg]]:
                               e.scalar_tensor_tensor(
                                   h_c(oc)[:, tg * 512:(tg + 1) * 512], b[:],
                                   s2_s[:, oc:oc + 1], t1_buf(par),
                                   ALU.mult, ALU.mult), inc="dve")
                    bank_guard[p2b[tg]] = ("dve", th)
                    tok["t1_free"][par] = th
                    h_done_dve = th
                if oc + 2 < 32:
                    load_w12(oc + 2)

            def hidden_layer(wsrc, in_c, out_c, scale_s, in_dve, in_act,
                             evac_dve_guard, first_slab_extra_pe):
                slab_toks = {}

                def load_h(oc):
                    par = oc % 2
                    WQ.emit_all()
                    lg.wait("gpsimd", "pe", tok["h_slab_pe"][par])
                    WQ.dma(lambda e, oc=oc, par=par, wsrc=wsrc:
                           e.dma_start(sl_h(par), wsrc[oc]))
                    slab_toks[oc] = WQ.close()

                WQ.emit_all()
                lg.wait("gpsimd", "pe", first_slab_extra_pe)
                load_h(0)
                load_h(1)
                last_evac = 0
                for oc in range(32):
                    par = oc % 2
                    banks = (0, 1) if par == 0 else (2, 3)
                    for tg in range(2):
                        waits = [("wqs", slab_toks[oc]),
                                 ("lds", g_consts)]
                        if in_dve:
                            waits.append(("dve", in_dve))
                        if in_act:
                            waits.append(("act", in_act))
                        pe_t = mm_group(
                            banks[tg], ps[banks[tg]][:],
                            [(lambda ic=ic, par=par:
                              sl_h(par)[:, ic * 128:(ic + 1) * 128],
                              lambda ic=ic, tg=tg:
                              in_c(ic)[:, tg * 512:(tg + 1) * 512])
                             for ic in range(32)],
                            waits=waits)
                        tok["h_slab_pe"][par] = pe_t
                        lg.wait("scalar", "pe", pe_t)
                        lg.wait("scalar", "dve", evac_dve_guard)
                        ta = lg.op(
                            "scalar",
                            lambda e, oc=oc, tg=tg, b=ps[banks[tg]], ss=scale_s:
                            e.activation(out_c(oc)[:, tg * 512:(tg + 1) * 512],
                                         b[:], AF.Gelu,
                                         scale=ss[:, oc:oc + 1]), inc="act")
                        bank_guard[banks[tg]] = ("act", ta)
                        last_evac = ta
                    if oc + 2 < 32:
                        load_h(oc + 2)
                return last_evac

            first_extra = max(tok["w1_slab"] + tok["w2_slab"] +
                              [tok["slab_attn_pe"]])
            g1_done_act = hidden_layer(h1t, h_c, g1_c, sh1_s,
                                       h_done_dve, 0, g1_guard_dve, first_extra)
            tok["h_read_pe"] = max(tok["h_slab_pe"])
            g2_done_act = hidden_layer(h2t, g1_c, g2_c, sh2_s,
                                       0, g1_done_act, h_done_dve, 0)
            gbuf_pe = max(tok["h_slab_pe"])

            def ln4_half(base, srcs4, toks4, Hh):
                stok4 = ln_stats_batch(srcs4, toks4, "dve", cb=base)
                for j in range(4):
                    i = base + j
                    lg.wait("vector", "dve", stok4)
                    lg.op("vector", lambda e, i=i: e.scalar_tensor_tensor(
                        x_tile(i), y3_t(i), stats[:, C_R + i:C_R + i + 1],
                        x_tile(i), ALU.mult, ALU.add))
                    tx = lg.op("vector", lambda e, i=i: e.tensor_scalar(
                        x_tile(i), x_tile(i),
                        stats[:, C_NMR + i:C_NMR + i + 1],
                        None, ALU.add), inc="dve")
                    lg.wait("gpsimd", "dve", tx)
                    SY.dma(lambda e, i=i, Hh=Hh: e.dma_start(
                        out_t[Hh * 8 + i], x_tile(i)))
                    tok["store"][i] = SY.close()

            w3_toks = {}
            w3_pe = {}

            def load_w3(oq):
                par = oq % 2
                WQ.emit_all()
                if par == 0:
                    lg.wait("gpsimd", "pe", tok["xfm_pe"])
                else:
                    lg.wait("gpsimd", "pe", max(tok["h_slab_pe"]))
                    lg.wait("gpsimd", "pe", tok["slab_attn_pe"])
                if oq >= 2:
                    lg.wait("gpsimd", "pe", w3_pe[oq - 2])
                WQ.dma(lambda e, oq=oq, par=par: e.dma_start(
                    sl_w3(par), w3t[oq % 4]))
                w3_toks[oq] = WQ.close()
                WQ.emit_all()

            load_w3(0)
            load_w3(1)
            y3_dve_toks = [0] * 8
            for L in range(4):
                oq, par = L, L % 2
                last_pe = 0
                for tt in range(8):
                    bi = 4 + (tt % 4)
                    pe_t = mm_group(
                        bi, ps[bi][:, 0:256],
                        [(lambda ic=ic, tt=tt:
                          g2_c(ic)[:, tt * 128:(tt + 1) * 128],
                          lambda ic=ic, par=par:
                          sl_w3(par)[:, ic * 256:(ic + 1) * 256])
                         for ic in range(32)],
                        waits=[("wqs", w3_toks[L]), ("act", g2_done_act),
                               ("lds", g_consts)])
                    last_pe = pe_t
                    lg.wait("vector", "pe", pe_t)
                    tv = lg.op("vector", lambda e, oq=oq, tt=tt, bi=bi:
                               e.tensor_tensor(
                                   y3_t(tt)[:, oq * 256:(oq + 1) * 256],
                                   ps[bi][:, 0:256],
                                   s3_bc[:, oq * 256:(oq + 1) * 256], ALU.mult),
                               inc="dve")
                    bank_guard[bi] = ("dve", tv)
                    y3_dve_toks[tt] = tv
                    if L == 3 and tt == 3:
                        ln4_half(0, [y3_t(i) for i in range(4)],
                                 y3_dve_toks[0:4], Hh)
                w3_pe[L] = last_pe
                if L + 2 < 4:
                    load_w3(L + 2)
            ln4_half(4, [y3_t(i) for i in range(4, 8)],
                     y3_dve_toks[4:8], Hh)
            w3_pe[6] = w3_pe[2]
            w3_pe[7] = w3_pe[3]
            WQ.emit_all()
            tok["xfm_pe"] = max(tok["xfm_pe"], w3_pe[6])
            tok["slab_attn_pe"] = max(tok["slab_attn_pe"], w3_pe[7])
            tok["w1_slab"] = [w3_pe[7]] * 2
            tok["w2_slab"] = [w3_pe[7]] * 2
            tok["h_read_pe"] = max(tok["h_read_pe"], w3_pe[7])
            tok["qt_pe"] = max(tok["qt_pe"], gbuf_pe, w3_pe[7])
            if debug_tag == "y3" and Hh == 0:
                dbg_tap("y3", lambda: y3_t(0), 1024, wait_sems=("dve",))

            g1_guard_dve = lg.count["dve"]

        g_final = SY.flush()
        TR.flush()
        WQ.flush()
        lg.wait("gpsimd", "lds", g_final)

        cleanup = True
        with nc.Block() as block:
            @block.sync
            def _(e):
                for fn in lg.prog["sync"]:
                    fn(e)

            @block.gpsimd
            def _(e):
                for fn in lg.prog["gpsimd"]:
                    fn(e)

            @block.tensor
            def _(e):
                for fn in lg.prog["tensor"]:
                    fn(e)

            @block.vector
            def _(e):
                for fn in lg.prog["vector"]:
                    fn(e)

            @block.scalar
            def _(e):
                for fn in lg.prog["scalar"]:
                    fn(e)

        nc.all_engine_barrier()
        for h in lg.sems.values():
            nc.gpsimd.sem_clear(h)
        nc.all_engine_barrier()

    nc._dev_ledger = lg.events if hasattr(lg, 'events') else None
    return nc



def _prep_consts(inp):
    f = np.float32
    Wk = np.ascontiguousarray(np.asarray(inp["L0_attn_Wk"], f).T)
    Wv = np.ascontiguousarray(np.asarray(inp["L0_attn_Wv"], f).T)
    Wq = np.ascontiguousarray(np.asarray(inp["L0_attn_Wq"], f).T)
    Wo = np.ascontiguousarray(np.asarray(inp["L0_attn_Wo"], f).T)
    bk = np.ascontiguousarray(np.asarray(inp["L0_attn_bk"], f).reshape(M, 1))
    bv = np.ascontiguousarray(np.asarray(inp["L0_attn_bv"], f).reshape(M, 1))
    bq = np.ascontiguousarray(np.asarray(inp["L0_attn_bq"], f).reshape(M, 1))
    bo = np.ascontiguousarray(
        np.broadcast_to(np.asarray(inp["L0_attn_bo"], f), (128, D)))
    s3 = np.ascontiguousarray(
        np.broadcast_to(np.asarray(inp["L0_ffn_w3_s"], f), (128, D)))

    def tile_wT(w, n_in_chunks, n_out_chunks):
        wt = np.asarray(w, f).T
        t = wt.reshape(n_in_chunks, 128, n_out_chunks, 128)
        t = np.ascontiguousarray(t.transpose(2, 1, 0, 3))
        return t.reshape(n_out_chunks, 128, n_in_chunks * 128)

    w1t = tile_wT(inp["L0_ffn_w1_w"], 8, 32)
    w2t = tile_wT(inp["L0_ffn_w2_w"], 8, 32)
    h1t = tile_wT(inp["L0_ffn_h1_w"], 32, 32)
    h2t = tile_wT(inp["L0_ffn_h2_w"], 32, 32)
    w3T = np.asarray(inp["L0_ffn_w3_w"], f).T
    t = w3T.reshape(32, 128, 4, 256)
    w3t = np.ascontiguousarray(t.transpose(2, 1, 0, 3)).reshape(4, 128, 32 * 256)

    sc = lambda s_: np.ascontiguousarray(np.asarray(s_, f).reshape(32, 128).T)
    s1, s2 = sc(inp["L0_ffn_w1_s"]), sc(inp["L0_ffn_w2_s"])
    sh1, sh2 = sc(inp["L0_ffn_h1_s"]), sc(inp["L0_ffn_h2_s"])

    jj, ii = np.meshgrid(np.arange(128), np.arange(128), indexing="ij")
    mka = np.where(ii > jj, LR * DECAY ** (ii - 1 - jj), 0.0).astype(f)
    mkd = np.ascontiguousarray(
        np.broadcast_to((DECAY ** np.arange(128)).astype(f), (128, 128)))
    mkw = np.ascontiguousarray(
        np.broadcast_to((LR * DECAY ** (127 - np.arange(128))).astype(f),
                        (128, 128)))
    return dict(wk=Wk, wv=Wv, wq=Wq, wo=Wo, bk=bk, bv=bv, bq=bq, bo=bo,
                s3=s3, w1t=w1t, w2t=w2t, h1t=h1t, h2t=h2t, w3t=w3t,
                s1=s1, s2=s2, sh1=sh1, sh2=sh2, mka=mka, mkd=mkd, mkw=mkw,
                identin=np.eye(128, dtype=f))


def make_in_maps(inputs):
    x = np.asarray(inputs["L0_attn_x"], np.float32)
    consts = _prep_consts(inputs)
    zeros = np.zeros((TOK, D), np.float32)
    ones_m = np.ones((128, 1), np.float32)
    zeros_m = np.zeros((128, 1), np.float32)
    in_maps = []
    for c in range(8):
        b, h = c // 2, c % 2
        m = dict(consts)
        m["xo"] = np.ascontiguousarray(x[b, h * TOK:(h + 1) * TOK])
        m["xp"] = np.ascontiguousarray(x[b, 0:TOK]) if h == 1 else zeros
        m["smask"] = ones_m if h == 1 else zeros_m
        in_maps.append(m)
    return in_maps


def kernel(**inputs):
    nc = build_program()
    in_maps = make_in_maps(inputs)
    res = run_bass_kernel_spmd(nc, in_maps, core_ids=list(range(8)))
    out = np.empty((B, S, D), np.float32)
    for c in range(8):
        b, h = c // 2, c % 2
        out[b, h * TOK:(h + 1) * TOK] = res.results[c]["out"]
    return out


# version 18
# speedup vs baseline: 1.2084x; 1.0032x over previous
from collections import deque
from contextlib import ExitStack

import numpy as np

import concourse.bass as bass
import concourse.mybir as mybir
from concourse.bass_utils import run_bass_kernel_spmd

F32 = mybir.dt.float32
BF16 = mybir.dt.bfloat16
AF = mybir.ActivationFunctionType
ALU = mybir.AluOpType
AX = mybir.AxisListType

B, S, D, M, HID = 4, 4096, 1024, 128, 4096
TOK = 2048
DECAY, LR = 0.99, 0.01
EPS = 1e-5
DECAY_C = float(DECAY ** 128)
R1024 = 1.0 / 1024.0
USE_TS = False

ENGS = ("sync", "gpsimd", "tensor", "vector", "scalar")
C_SUM, C_SQ, C_M, C_VAR, C_SD, C_R, C_NMR, C_TMP = (i * 8 for i in range(8))


class Ledger:
    def __init__(self, nc, ctx, sems):
        self.nc = nc
        self.prog = {e: [] for e in ENGS}
        self.sems = {}
        self.count = {}
        self.waited = {e: {} for e in ENGS}
        self.events = {e: [] for e in ENGS}
        for s in sems:
            self.sems[s] = ctx.enter_context(nc.semaphore(s))
            self.count[s] = 0

    def wait(self, eng, name, val=None):
        if val is None:
            val = self.count[name]
        if val <= 0 or val <= self.waited[eng].get(name, 0):
            return
        self.waited[eng][name] = val
        self.events[eng].append(("w", name, val))
        h = self.sems[name]
        self.prog[eng].append(lambda e, h=h, v=val: e.wait_ge(h, v))

    def op(self, eng, fn, inc=None, inck=1):
        if inc is None:
            self.prog[eng].append(fn)
            return None
        self.count[inc] += inck
        self.events[eng].append(("i", inc, inck))
        h = self.sems[inc]
        self.prog[eng].append(lambda e, fn=fn, h=h, k=inck: fn(e).then_inc(h, k))
        return self.count[inc]


class DmaStream:

    def __init__(self, lg, eng, qnames, sig):
        self.lg = lg
        self.eng = eng
        self.q = qnames
        self.sig = sig
        self.qi = 0
        self.issued = {q: 0 for q in qnames}
        self.open = False
        self.pending = deque()
        self.n_closed = 0

    def begin(self):
        if not self.open:
            q = self.q[self.qi]
            while any(p[0] == q for p in self.pending):
                self._emit_one()
            self.open = True

    def dma(self, fn):
        self.begin()
        q = self.q[self.qi]
        self.issued[q] += 1
        self.lg.op(self.eng, fn, inc=q, inck=16)

    def close(self):
        if not self.open:
            return self.n_closed
        q = self.q[self.qi]
        self.n_closed += 1
        self.pending.append((q, self.issued[q] * 16, self.n_closed))
        self.open = False
        self.qi = (self.qi + 1) % len(self.q)
        return self.n_closed

    def _emit_one(self):
        q, v, _ = self.pending.popleft()
        self.lg.wait(self.eng, q, v)
        h = self.lg.sems[self.sig]
        self.lg.count[self.sig] += 1
        self.lg.events[self.eng].append(("i", self.sig, 1))
        self.lg.prog[self.eng].append(lambda e, h=h: e.sem_inc(h, 1))

    def ensure(self, gtok):
        while self.pending and self.lg.count[self.sig] < gtok:
            self._emit_one()

    def emit_all(self):
        while self.pending:
            self._emit_one()

    def flush(self):
        self.close()
        while self.pending:
            self._emit_one()
        return self.lg.count[self.sig]


def build_program(debug_tag=None):
    nc = bass.Bass("TRN2", target_bir_lowering=False)

    di = lambda n, shp: nc.dram_tensor(n, shp, F32, kind="ExternalInput").ap()
    xo = di("xo", [TOK, D])
    xp = di("xp", [TOK, D])
    wk, wq, wv = di("wk", [D, M]), di("wq", [D, M]), di("wv", [D, M])
    wo = di("wo", [M, D])
    bk, bq, bv = di("bk", [M, 1]), di("bq", [M, 1]), di("bv", [M, 1])
    bo_in = di("bo", [128, D])
    s3_in = di("s3", [128, D])
    s1_in, s2_in = di("s1", [128, 32]), di("s2", [128, 32])
    sh1_in, sh2_in = di("sh1", [128, 32]), di("sh2", [128, 32])
    mka_in = di("mka", [128, 128])
    mkd_in = di("mkd", [128, 128])
    mkw_in = di("mkw", [128, 128])
    smask_in = di("smask", [128, 1])
    ident_in = di("identin", [128, 128])
    w1t = di("w1t", [32, 128, 1024])
    w2t = di("w2t", [32, 128, 1024])
    h1t = di("h1t", [32, 128, 4096])
    h2t = di("h2t", [32, 128, 4096])
    w3t = di("w3t", [4, 128, 8192])
    out = nc.dram_tensor("out", [TOK, D], F32, kind="ExternalOutput").ap()
    dbg = None
    if debug_tag is not None:
        dbg = nc.dram_tensor("dbg", [128, 4096], F32, kind="ExternalOutput").ap()

    xo_t = xo.rearrange("(t p) d -> t p d", p=128)
    xp_t = xp.rearrange("(t p) d -> t p d", p=128)
    out_t = out.rearrange("(t p) d -> t p d", p=128)

    ctx = ExitStack()
    with ctx:
        sb = lambda n, shp, dt: ctx.enter_context(nc.sbuf_tensor(n, shp, dt))
        xres = sb("xres", [128, 8 * 1024], F32)
        bk_s, bq_s, bv_s = (sb(n, [128, 1], F32) for n in ("bk_s", "bq_s", "bv_s"))
        s3_bc = sb("s3_bc", [128, 1024], F32)
        s1_s, s2_s = sb("s1_s", [128, 32], F32), sb("s2_s", [128, 32], F32)
        sh1_s, sh2_s = sb("sh1_s", [128, 32], F32), sb("sh2_s", [128, 32], F32)
        mask_a = sb("mask_a", [128, 128], F32)
        mask_dk = sb("mask_dk", [128, 128], BF16)
        mask_kw = sb("mask_kw", [128, 128], BF16)
        smask = sb("smask_b", [128, 1], F32)
        st = sb("st", [128, 256], F32)
        stats = sb("stats", [128, 64], F32)
        ident = sb("ident", [128, 128], F32)
        xfm = sb("xfm", [128, 8192], BF16)
        kqv = sb("kqv", [128, 3840], BF16)
        hbuf = sb("hbuf", [128, 32768], BF16)
        gbuf = sb("gbuf", [128, 32768], BF16)
        slab = sb("slab", [128, 8192], BF16)

        x_tile = lambda i: xres[:, i * 1024:(i + 1) * 1024]
        xfm_c = lambda c: xfm[:, c * 1024:(c + 1) * 1024]
        cw_k = slab[:, 0:1024]
        cw_q = slab[:, 1024:2048]
        cw_v = slab[:, 2048:3072]
        cw_o = slab[:, 3072:4096]
        bo_bc = gbuf[:, 24576:26624].bitcast(F32)
        k_fm = kqv[:, 0:1024]
        q_fm = kqv[:, 1024:2048]
        v_fm = kqv[:, 2048:3072]
        kw_scr = lambda p: kqv[:, 3072 + p * 128: 3072 + (p + 1) * 128]
        ktm_scr = lambda p: kqv[:, 3328 + p * 128: 3328 + (p + 1) * 128]
        vtm_scr = lambda p: kqv[:, 3584 + p * 128: 3584 + (p + 1) * 128]
        t1_buf = lambda par: kqv[:, par * 1024:(par + 1) * 1024].bitcast(F32)
        h_c = lambda ic: hbuf[:, ic * 1024:(ic + 1) * 1024]
        g2_c = h_c
        xp_buf = lambda i: hbuf[:, i * 2048:(i + 1) * 2048].bitcast(F32)
        g1_c = lambda ic: gbuf[:, ic * 1024:(ic + 1) * 1024]
        y_buf = gbuf[:, 0:2048].bitcast(F32)
        xln_buf = lambda p: gbuf[:, 2048 + p * 1024: 2048 + (p + 1) * 1024]
        sq_buf = gbuf[:, 4096:6144].bitcast(F32)
        qt_all = lambda c: gbuf[:, 6144 + c * 128: 6144 + (c + 1) * 128]
        kw_f32 = lambda c: gbuf[:, 26624 + c * 256: 26624 + (c + 1) * 256].bitcast(F32)
        ktm_all = lambda c: gbuf[:, 12288 + c * 128: 12288 + (c + 1) * 128]
        v_f32 = gbuf[:, 14336:16384].bitcast(F32)
        y3_t = lambda i: gbuf[:, 8192 + i * 2048: 8192 + (i + 1) * 2048].bitcast(F32)
        sl_w1 = lambda p: slab[:, p * 1024:(p + 1) * 1024]
        sl_w2 = lambda p: slab[:, 2048 + p * 1024: 2048 + (p + 1) * 1024]
        sl_h = lambda p: slab[:, p * 4096:(p + 1) * 4096]
        vtm_all = lambda c: slab[:, 4096 + c * 128: 4096 + (c + 1) * 128]
        abf_all = lambda c: slab[:, 5120 + c * 128: 5120 + (c + 1) * 128]
        snap_all = lambda c: slab[:, 6144 + c * 128: 6144 + (c + 1) * 128]
        outs_all = lambda c: slab[:, 7168 + c * 128: 7168 + (c + 1) * 128]
        sl_w3 = lambda p: (xfm if p == 0 else slab)[:, 0:8192]

        st_slot = [0]
        st_cur = lambda: st[:, st_slot[0] * 128:(st_slot[0] + 1) * 128]
        st_nxt = lambda: st[:, (1 - st_slot[0]) * 128:(2 - st_slot[0]) * 128]

        ps = [ctx.enter_context(nc.psum_tensor(f"ps{i}", [128, 512], F32))
              for i in range(8)]
        bank_guard = [("dve", 0)] * 8

        lg = Ledger(nc, ctx, (
            "q_sy0", "q_sy1", "q_tr0", "q_tr1", "q_tr2", "q_tr3",
            "q_ts0", "q_ts1", "q_ts2", "q_ts3",
            "q_wq0", "q_wq1", "lds", "trs", "tss", "wqs", "pe", "act", "dve", "gp"))
        SY = DmaStream(lg, "gpsimd", ["q_sy0", "q_sy1"], "lds")
        TR = DmaStream(lg, "scalar", ["q_tr0", "q_tr1", "q_tr2", "q_tr3"], "trs")
        TS = DmaStream(lg, "sync", ["q_ts0", "q_ts1", "q_ts2", "q_ts3"], "tss")
        WQ = DmaStream(lg, "gpsimd", ["q_wq0", "q_wq1"], "wqs")

        for h in lg.sems.values():
            nc.gpsimd.sem_clear(h)
        nc.all_engine_barrier()

        dbg_state = {"done": False}

        def dbg_tap(tag, ap_fn, ncols, wait_sems=()):
            if debug_tag != tag or dbg_state["done"]:
                return
            dbg_state["done"] = True
            if "lds" in wait_sems:
                gg = SY.close()
                SY.ensure(gg)
            for sname in wait_sems:
                lg.wait("gpsimd", sname)
            SY.close()
            SY.dma(lambda e, a=ap_fn(), n=ncols: e.dma_start(dbg[:, 0:n], a))
            g = SY.close()
            SY.ensure(g)
            for e_ in ENGS:
                lg.wait(e_, "lds", g)

        cwtok = {"g": 0}

        def load_cw(guard_pe):
            WQ.emit_all()
            lg.wait("gpsimd", "pe", guard_pe)
            WQ.dma(lambda e: e.dma_start(
                cw_k.rearrange("p (c m) -> p c m", m=128),
                wk.rearrange("(c p) m -> p c m", p=128)))
            WQ.dma(lambda e: e.dma_start(
                cw_q.rearrange("p (c m) -> p c m", m=128),
                wq.rearrange("(c p) m -> p c m", p=128)))
            WQ.dma(lambda e: e.dma_start(
                cw_v.rearrange("p (c m) -> p c m", m=128),
                wv.rearrange("(c p) m -> p c m", p=128)))
            WQ.dma(lambda e: e.dma_start(cw_o, wo))
            cwtok["g"] = WQ.close()
            WQ.emit_all()

        load_cw(0)
        WQ.dma(lambda e: e.dma_start(mask_dk[:], mkd_in))
        WQ.dma(lambda e: e.dma_start(mask_kw[:], mkw_in))
        g_masks = WQ.close()
        WQ.emit_all()

        for dst, src in ((bk_s[:], bk), (bq_s[:], bq), (bv_s[:], bv),
                         (s3_bc[:], s3_in), (s1_s[:], s1_in), (s2_s[:], s2_in),
                         (sh1_s[:], sh1_in), (sh2_s[:], sh2_in),
                         (mask_a[:], mka_in), (smask[:], smask_in),
                         (ident[:], ident_in)):
            SY.dma(lambda e, d=dst, s=src: e.dma_start(d, s))
        g_consts = SY.close()

        lg.op("vector", lambda e: e.memset(st[:, 0:128], 0.0))


        tok = {
            "xfm_pe": 0,
            "xln_tr": [None, None],
            "xsc_pe": [0, 0],
            "kw_tr": [0, 0],
            "vtm_pe": [0, 0],
            "w1_slab": [0, 0], "w2_slab": [0, 0],
            "h_slab_pe": [0, 0],
            "slab_attn_pe": 0,
            "qt_pe": 0,
            "t1_free": {},
            "store": [0] * 8,
            "y_sq_act": 0,
            "st_tok": 0,
            "h_read_pe": 0,
        }

        def mm_group(bank_idx, out_ap, pairs, waits=()):
            sname, v = bank_guard[bank_idx]
            lg.wait("tensor", sname, v)
            for sn, vv in waits:
                lg.wait("tensor", sn, vv)
            n = len(pairs)
            t = None
            for j, (lf, rf) in enumerate(pairs):
                t = lg.op("tensor",
                          lambda e, lf=lf, rf=rf, st_=(j == 0), sp=(j == n - 1),
                          o=out_ap: e.matmul(o, lf(), rf(), start=st_, stop=sp),
                          inc="pe" if j == n - 1 else None)
            return t

        def ln_stats_batch(srcs, src_toks, src_sem, cb=0):
            n = len(srcs)
            a_last = None
            for i, (sap, stok) in enumerate(zip(srcs, src_toks)):
                if stok:
                    lg.wait("scalar", src_sem, stok)
                a_last = lg.op(
                    "scalar",
                    lambda e, sap=sap, i=i, cb=cb: e.activation(
                        sq_buf, sap, AF.Square,
                        accum_out=stats[:, C_SQ + cb + i:C_SQ + cb + i + 1]),
                    inc="act")
            t_red = None
            for i, (sap, stok) in enumerate(zip(srcs, src_toks)):
                if stok:
                    lg.wait("vector", src_sem, stok)
                t_red = lg.op("vector",
                              lambda e, sap=sap, i=i, cb=cb: e.reduce_sum(
                                  stats[:, C_SUM + cb + i:C_SUM + cb + i + 1], sap,
                                  axis=AX.X), inc="dve")
            lg.wait("vector", "act", a_last)
            lg.wait("vector", "dve", t_red)
            tM = lg.op("vector", lambda e, n=n, cb=cb: e.tensor_scalar(
                stats[:, C_M + cb:C_M + cb + n], stats[:, C_SUM + cb:C_SUM + cb + n],
                R1024, None, ALU.mult), inc="dve")
            lg.op("vector", lambda e, n=n, cb=cb: e.tensor_scalar(
                stats[:, C_SD + cb:C_SD + cb + n], stats[:, C_SQ + cb:C_SQ + cb + n],
                R1024, None, ALU.mult))
            lg.wait("vector", "dve", tM)
            tT = lg.op("vector", lambda e, n=n, cb=cb: e.tensor_tensor(
                stats[:, C_TMP + cb:C_TMP + cb + n], stats[:, C_M + cb:C_M + cb + n],
                stats[:, C_M + cb:C_M + cb + n], ALU.mult), inc="dve")
            lg.wait("vector", "dve", tT)
            tv = lg.op("vector", lambda e, n=n, cb=cb: e.scalar_tensor_tensor(
                stats[:, C_VAR + cb:C_VAR + cb + n], stats[:, C_SD + cb:C_SD + cb + n], EPS,
                stats[:, C_TMP + cb:C_TMP + cb + n], ALU.add, ALU.subtract), inc="dve")
            lg.wait("scalar", "dve", tv)
            ta = lg.op("scalar", lambda e, n=n, cb=cb: e.activation(
                stats[:, C_SD + cb:C_SD + cb + n], stats[:, C_VAR + cb:C_VAR + cb + n],
                AF.Sqrt), inc="act")
            lg.wait("vector", "act", ta)
            tR = lg.op("vector", lambda e, n=n, cb=cb: e.reciprocal(
                stats[:, C_R + cb:C_R + cb + n], stats[:, C_SD + cb:C_SD + cb + n]), inc="dve")
            lg.wait("vector", "dve", tR)
            tn = lg.op("vector", lambda e, n=n, cb=cb: e.scalar_tensor_tensor(
                stats[:, C_NMR + cb:C_NMR + cb + n], stats[:, C_M + cb:C_M + cb + n], -1.0,
                stats[:, C_R + cb:C_R + cb + n], ALU.mult, ALU.mult), inc="dve")
            return tn

        def trx_wait(eng, toks):
            if toks is None:
                return
            for s_, v in toks:
                lg.wait(eng, s_, v)

        def tok_waits(toklists):
            m = {}
            for tl in toklists:
                if tl is None:
                    continue
                for s_, v in tl:
                    m[s_] = max(m.get(s_, 0), v)
            return list(m.items())

        def pe_tr_block(dst_ap, src_ap, bank_idx, waits, xfm_guard_pe):
            sname, v = bank_guard[bank_idx]
            lg.wait("tensor", sname, v)
            for sn, vv in waits:
                lg.wait("tensor", sn, vv)
            lg.wait("tensor", "lds", g_consts)
            t = lg.op("tensor", lambda e, d=src_ap, bi=bank_idx: e.transpose(
                ps[bi][:, 0:128], d, ident[:]), inc="pe")
            lg.wait("vector", "pe", t)
            lg.wait("vector", "pe", xfm_guard_pe)
            ev = lg.op("vector", lambda e, o=dst_ap, bi=bank_idx: e.tensor_copy(
                o, ps[bi][:, 0:128]), inc="dve")
            bank_guard[bank_idx] = ("dve", ev)
            return t, ev

        def ln_to_xfm(srcs, src_toks, src_sem, base=0, dma_path=False):
            stok = ln_stats_batch(srcs, src_toks, src_sem, cb=base)
            tr_toks = [None] * len(srcs)
            for j, sap in enumerate(srcs):
                i = base + j
                p = i % 2
                if dma_path:
                    lg.wait("scalar", "dve", stok)
                    trx_wait("scalar", tok["xln_tr"][p])
                    t_ap = lg.op("scalar", lambda e, s=sap, i=i, p=p:
                                 e.activation(
                                     xln_buf(p), s, AF.Identity,
                                     scale=stats[:, C_R + i:C_R + i + 1],
                                     bias=stats[:, C_NMR + i:C_NMR + i + 1]),
                                 inc="act")
                    lg.wait("scalar", "pe", tok["xfm_pe"])
                    lg.wait("scalar", "act", t_ap)
                    for c in range(8):
                        TR.dma(lambda e, c=c, i=i, p=p: e.dma_start(
                            xfm_c(c)[:, i * 128:(i + 1) * 128],
                            xln_buf(p)[:, c * 128:(c + 1) * 128],
                            transpose=True))
                    g = TR.close()
                    TR.emit_all()
                    tok["xln_tr"][p] = [("trs", g)]
                    tr_toks[j] = [("trs", g)]
                    continue
                xsc = y3_t(p)
                lg.wait("scalar", "dve", stok)
                lg.wait("scalar", "pe", tok["xsc_pe"][p])
                t_ap = lg.op("scalar", lambda e, s=sap, i=i, x=xsc: e.activation(
                    x, s, AF.Identity,
                    scale=stats[:, C_R + i:C_R + i + 1],
                    bias=stats[:, C_NMR + i:C_NMR + i + 1]), inc="act")
                last_ev = 0
                last_t = 0
                for c in range(8):
                    last_t, last_ev = pe_tr_block(
                        xfm_c(c)[:, i * 128:(i + 1) * 128],
                        xsc[:, c * 128:(c + 1) * 128],
                        6 + c % 2, [("act", t_ap)], tok["xfm_pe"])
                tok["xsc_pe"][p] = last_t
                tr_toks[j] = [("dve", last_ev)]
            return tr_toks

        def proj_pass(tr_toks, do_q):
            plist = [("k", cw_k, bk_s, k_fm, (0, 1)),
                     ("v", cw_v, bv_s, v_f32, (2, 3))]
            if do_q:
                plist.append(("q", cw_q, bq_s, q_fm, (4, 5)))
            ev = {}
            for name, cw, bias, dst, banks in plist:
                ev[name] = [0, 0]
                for tg in range(2):
                    bi = banks[tg]
                    tws = tok_waits(tr_toks[tg * 4:(tg + 1) * 4])
                    pe_t = mm_group(
                        bi, ps[bi][:],
                        [(lambda ic=ic, cw=cw: cw[:, ic * 128:(ic + 1) * 128],
                          lambda ic=ic, tg=tg: xfm_c(ic)[:, tg * 512:(tg + 1) * 512])
                         for ic in range(8)],
                        waits=tws + [("wqs", cwtok["g"]), ("lds", g_consts)])
                    lg.wait("scalar", "pe", pe_t)
                    lg.wait("scalar", "pe", tok["qt_pe"])
                    a_t = lg.op("scalar",
                                lambda e, b=ps[bi], d=dst, tg=tg, bias=bias:
                                e.activation(d[:, tg * 512:(tg + 1) * 512], b[:],
                                             AF.Identity, bias=bias[:]),
                                inc="act")
                    ev[name][tg] = a_t
                    bank_guard[bi] = ("act", a_t)
            return ev

        def attn_chunks(ev, full):
            kw_toks = [0] * 8
            abf_toks = [0] * 8
            qt_toks = [0] * 8
            snap_toks = [0] * 8
            outs_toks = [0] * 8
            for c in range(8):
                par = c % 2
                lg.wait("vector", "act", ev["k"][c // 4])
                lg.wait("vector", "wqs", g_masks)
                lg.wait("vector", "pe", tok["qt_pe"])
                kw_toks[c] = lg.op("vector", lambda e, c=c: e.tensor_tensor(
                    kw_f32(c), k_fm[:, c * 128:(c + 1) * 128], mask_kw[:],
                    ALU.mult), inc="dve")
                if full:
                    lg.wait("vector", "act", ev["q"][c // 4])
                    qt_toks[c] = lg.op("vector", lambda e, c=c: e.tensor_tensor(
                        qt_all(c), q_fm[:, c * 128:(c + 1) * 128], mask_dk[:],
                        ALU.mult), inc="dve")
                    bi_a = 6 + par
                    pe_a = mm_group(
                        bi_a, ps[bi_a][:, 0:128],
                        [(lambda c=c: k_fm[:, c * 128:(c + 1) * 128],
                          lambda c=c: q_fm[:, c * 128:(c + 1) * 128])],
                        waits=[("act", max(ev["k"][c // 4], ev["q"][c // 4]))])
                    lg.wait("vector", "pe", pe_a)
                    lg.wait("vector", "lds", g_consts)
                    lg.wait("vector", "pe", tok["slab_attn_pe"])
                    abf_toks[c] = lg.op("vector", lambda e, c=c, bi=bi_a:
                                        e.tensor_tensor(abf_all(c),
                                                        ps[bi][:, 0:128],
                                                        mask_a[:], ALU.mult),
                                        inc="dve")
                    bank_guard[bi_a] = ("dve", abf_toks[c])
            tp_toks = [None] * 8
            for c in range(8):
                _, ev_k = pe_tr_block(
                    ktm_all(c), kw_f32(c), c % 4,
                    [("dve", kw_toks[c]), ("pe", tok["qt_pe"])], 0)
                _, ev_v = pe_tr_block(
                    vtm_all(c), v_f32[:, c * 128:(c + 1) * 128], (c + 1) % 4,
                    [("act", ev["v"][c // 4]), ("pe", tok["slab_attn_pe"])], 0)
                tp_toks[c] = [("dve", ev_v)]
            tr_pair = None
            for c in range(8):
                par = c % 2
                bi = 4 + par
                pe_t = mm_group(
                    bi, ps[bi][:, 0:128],
                    [(lambda c=c: ktm_all(c), lambda c=c: vtm_all(c))],
                    waits=tp_toks[c])
                lg.wait("vector", "dve", tok["st_tok"])
                if full:
                    lg.wait("vector", "pe", tok["slab_attn_pe"])
                    _scur = st_cur()
                    snap_toks[c] = lg.op("vector", lambda e, c=c, s_=_scur:
                                         e.tensor_copy(snap_all(c), s_),
                                         inc="dve")
                lg.wait("vector", "pe", pe_t)
                cur, nxt = st_cur(), st_nxt()
                tu = lg.op("vector", lambda e, bi=bi, cur=cur, nxt=nxt:
                           e.scalar_tensor_tensor(nxt, cur, DECAY_C,
                                                  ps[bi][:, 0:128],
                                                  ALU.mult, ALU.add), inc="dve")
                st_slot[0] = 1 - st_slot[0]
                tok["st_tok"] = tu
                bank_guard[bi] = ("dve", tu)
                if full:
                    bi_o = 6 + par
                    pe_o = mm_group(
                        bi_o, ps[bi_o][:, 128:256],
                        [(lambda c=c: vtm_all(c), lambda c=c: abf_all(c)),
                         (lambda c=c: snap_all(c), lambda c=c: qt_all(c))],
                        waits=[("dve", max(abf_toks[c], snap_toks[c],
                                           qt_toks[c]))] + tp_toks[c])
                    tok["qt_pe"] = pe_o
                    tok["slab_attn_pe"] = pe_o
                    lg.wait("scalar", "pe", pe_o)
                    outs_toks[c] = lg.op(
                        "scalar", lambda e, c=c, bi=bi_o: e.activation(
                            outs_all(c), ps[bi][:, 128:256], AF.Copy),
                        inc="act")
                    bank_guard[bi_o] = ("act", outs_toks[c])
            return abf_toks, snap_toks, qt_toks, outs_toks

        hb_act = 0
        hb_dve = 0
        for ph in range(2):
            xp_toks = []
            lg.wait("gpsimd", "act", hb_act)
            lg.wait("gpsimd", "dve", hb_dve)
            for i in range(8):
                SY.dma(lambda e, i=i, ph=ph: e.dma_start(
                    xp_buf(i), xp_t[ph * 8 + i]))
                if i % 4 == 3:
                    g = SY.close()
                    for j in range(i - 3, i + 1):
                        xp_toks.append(g)
            SY.emit_all()
            srcs = [xp_buf(i) for i in range(8)]
            if ph == 0:
                dbg_tap("pre_xp", lambda: xp_buf(0), 1024,
                        wait_sems=("lds",))
            tr_toks = ln_to_xfm(srcs, xp_toks, "lds")
            if ph == 0:
                dbg_tap("pre_stats", lambda: stats[:], 64,
                        wait_sems=("dve", "act"))
            hb_act = lg.count["act"]
            hb_dve = lg.count["dve"]
            ev = proj_pass(tr_toks, do_q=False)
            tok["xfm_pe"] = lg.count["pe"]
            attn_chunks(ev, full=False)
            tok["slab_attn_pe"] = lg.count["pe"]
            tok["qt_pe"] = lg.count["pe"]
        lg.wait("vector", "lds", g_consts)
        lg.wait("vector", "dve", tok["st_tok"])
        _cur, _nxt = st_cur(), st_nxt()
        tok["st_tok"] = lg.op("vector", lambda e, c=_cur, x=_nxt:
                              e.tensor_scalar(x, c, smask[:], None, ALU.mult),
                              inc="dve")
        st_slot[0] = 1 - st_slot[0]
        dbg_tap("prelude_st", lambda: st_cur(), 128, wait_sems=("dve",))

        g1_guard_dve = 0
        for Hh in range(2):
            WQ.emit_all()
            TR.emit_all()
            SY.emit_all()
            lg.wait("gpsimd", "pe", tok["qt_pe"])
            SY.dma(lambda e: e.dma_start(bo_bc, bo_in))
            if Hh == 1:
                load_cw(max(tok["w1_slab"] + tok["w2_slab"]))
            x_toks = []
            for i in range(8):
                if Hh == 1:
                    SY.ensure(tok["store"][i])
                    lg.wait("gpsimd", "lds", tok["store"][i])
                SY.dma(lambda e, i=i, Hh=Hh: e.dma_start(
                    x_tile(i), xo_t[Hh * 8 + i]))
                if i % 4 == 3:
                    g = SY.close()
                    for j in range(i - 3, i + 1):
                        x_toks.append(g)
            SY.emit_all()
            g_bo = x_toks[0]

            tr_toks = (ln_to_xfm([x_tile(i) for i in range(4)],
                                 x_toks[0:4], "lds", 0) +
                       ln_to_xfm([x_tile(i) for i in range(4, 8)],
                                 x_toks[4:8], "lds", 4))
            ev = proj_pass(tr_toks, do_q=True)
            tok["xfm_pe"] = lg.count["pe"]

            abf_toks, snap_toks, qt_toks, outs_toks = attn_chunks(
                ev, full=True)

            y_toks = [0] * 8
            for c in range(8):
                par = c % 2
                bY = (0, 1) if par == 0 else (2, 3)
                pe_y = 0
                for hd in range(2):
                    pe_y = mm_group(
                        bY[hd], ps[bY[hd]][:],
                        [(lambda c=c: outs_all(c),
                          lambda hd=hd: cw_o[:, hd * 512:(hd + 1) * 512])],
                        waits=[("act", outs_toks[c]), ("wqs", cwtok["g"])])
                tok["slab_attn_pe"] = max(tok["slab_attn_pe"], pe_y)
                lg.wait("vector", "pe", pe_y)
                lg.wait("vector", "lds", g_bo)
                ty = 0
                for hd in range(2):
                    ty = lg.op("vector", lambda e, c=c, hd=hd, bY=bY:
                               e.tensor_tensor(
                                   y3_t(c)[:, hd * 512:(hd + 1) * 512],
                                   ps[bY[hd]][:],
                                   bo_bc[:, hd * 512:(hd + 1) * 512], ALU.add),
                               inc="dve")
                    bank_guard[bY[hd]] = ("dve", ty)
                y_toks[c] = ty
            stok2 = ln_stats_batch([y3_t(c) for c in range(8)], y_toks, "dve")
            xup = [0] * 8
            for c in range(8):
                lg.wait("vector", "dve", stok2)
                lg.op("vector", lambda e, c=c: e.scalar_tensor_tensor(
                    x_tile(c), y3_t(c), stats[:, C_R + c:C_R + c + 1],
                    x_tile(c), ALU.mult, ALU.add))
                xup[c] = lg.op("vector", lambda e, c=c: e.tensor_scalar(
                    x_tile(c), x_tile(c), stats[:, C_NMR + c:C_NMR + c + 1],
                    None, ALU.add), inc="dve")
            tr_toks3 = (ln_to_xfm([x_tile(i) for i in range(4)],
                                  xup[0:4], "dve", 0) +
                        ln_to_xfm([x_tile(i) for i in range(4, 8)],
                                  xup[4:8], "dve", 4))
            attn_x_dve = lg.count["dve"]
            if debug_tag == "attn" and Hh == 0:
                dbg_tap("attn", lambda: x_tile(0), 1024, wait_sems=("dve",))


            w12_toks = {}

            def load_w12(oc):
                par = oc % 2
                WQ.emit_all()
                lg.wait("gpsimd", "pe", max(tok["w1_slab"][par],
                                            tok["w2_slab"][par],
                                            tok["xfm_pe"],
                                            tok["slab_attn_pe"]))
                WQ.dma(lambda e, oc=oc, par=par: e.dma_start(
                    sl_w1(par), w1t[oc]))
                WQ.dma(lambda e, oc=oc, par=par: e.dma_start(
                    sl_w2(par), w2t[oc]))
                w12_toks[oc] = WQ.close()

            load_w12(0)
            load_w12(1)
            h_done_dve = 0
            for oc in range(32):
                par = oc % 2
                p1b = (0, 1) if par == 0 else (4, 5)
                p2b = (2, 3) if par == 0 else (6, 7)
                for tg in range(2):
                    pe1 = mm_group(
                        p1b[tg], ps[p1b[tg]][:],
                        [(lambda ic=ic, par=par: sl_w1(par)[:, ic * 128:(ic + 1) * 128],
                          lambda ic=ic, tg=tg: xfm_c(ic)[:, tg * 512:(tg + 1) * 512])
                         for ic in range(8)],
                        waits=tok_waits(tr_toks3[tg * 4:(tg + 1) * 4]) + [
                               ("wqs", w12_toks[oc]), ("lds", g_consts)])
                    pe2 = mm_group(
                        p2b[tg], ps[p2b[tg]][:],
                        [(lambda ic=ic, par=par: sl_w2(par)[:, ic * 128:(ic + 1) * 128],
                          lambda ic=ic, tg=tg: xfm_c(ic)[:, tg * 512:(tg + 1) * 512])
                         for ic in range(8)],
                        waits=[("wqs", w12_toks[oc])])
                    tok["w1_slab"][par] = pe1
                    tok["w2_slab"][par] = pe2
                    tok["xfm_pe"] = pe2
                    lg.wait("scalar", "pe", pe1)
                    lg.wait("scalar", "dve", tok["t1_free"].get(par, 0))
                    ta = lg.op("scalar",
                               lambda e, oc=oc, par=par, b=ps[p1b[tg]]:
                               e.activation(t1_buf(par), b[:], AF.Silu,
                                            scale=s1_s[:, oc:oc + 1]), inc="act")
                    bank_guard[p1b[tg]] = ("act", ta)
                    lg.wait("vector", "pe", pe2)
                    lg.wait("vector", "act", ta)
                    lg.wait("vector", "pe", tok["h_read_pe"])
                    th = lg.op("vector",
                               lambda e, oc=oc, par=par, tg=tg, b=ps[p2b[tg]]:
                               e.scalar_tensor_tensor(
                                   h_c(oc)[:, tg * 512:(tg + 1) * 512], b[:],
                                   s2_s[:, oc:oc + 1], t1_buf(par),
                                   ALU.mult, ALU.mult), inc="dve")
                    bank_guard[p2b[tg]] = ("dve", th)
                    tok["t1_free"][par] = th
                    h_done_dve = th
                if oc + 2 < 32:
                    load_w12(oc + 2)

            def hidden_layer(wsrc, in_c, out_c, scale_s, in_dve, in_act,
                             evac_dve_guard, first_slab_extra_pe):
                slab_toks = {}

                def load_h(oc):
                    par = oc % 2
                    WQ.emit_all()
                    lg.wait("gpsimd", "pe", tok["h_slab_pe"][par])
                    WQ.dma(lambda e, oc=oc, par=par, wsrc=wsrc:
                           e.dma_start(sl_h(par), wsrc[oc]))
                    slab_toks[oc] = WQ.close()

                WQ.emit_all()
                lg.wait("gpsimd", "pe", first_slab_extra_pe)
                load_h(0)
                load_h(1)
                last_evac = 0
                for oc in range(32):
                    par = oc % 2
                    banks = (0, 1) if par == 0 else (2, 3)
                    for tg in range(2):
                        waits = [("wqs", slab_toks[oc]),
                                 ("lds", g_consts)]
                        if in_dve:
                            waits.append(("dve", in_dve))
                        if in_act:
                            waits.append(("act", in_act))
                        pe_t = mm_group(
                            banks[tg], ps[banks[tg]][:],
                            [(lambda ic=ic, par=par:
                              sl_h(par)[:, ic * 128:(ic + 1) * 128],
                              lambda ic=ic, tg=tg:
                              in_c(ic)[:, tg * 512:(tg + 1) * 512])
                             for ic in range(32)],
                            waits=waits)
                        tok["h_slab_pe"][par] = pe_t
                        lg.wait("scalar", "pe", pe_t)
                        lg.wait("scalar", "dve", evac_dve_guard)
                        ta = lg.op(
                            "scalar",
                            lambda e, oc=oc, tg=tg, b=ps[banks[tg]], ss=scale_s:
                            e.activation(out_c(oc)[:, tg * 512:(tg + 1) * 512],
                                         b[:], AF.Gelu,
                                         scale=ss[:, oc:oc + 1]), inc="act")
                        bank_guard[banks[tg]] = ("act", ta)
                        last_evac = ta
                    if oc + 2 < 32:
                        load_h(oc + 2)
                return last_evac

            first_extra = max(tok["w1_slab"] + tok["w2_slab"] +
                              [tok["slab_attn_pe"]])
            g1_done_act = hidden_layer(h1t, h_c, g1_c, sh1_s,
                                       h_done_dve, 0, g1_guard_dve, first_extra)
            tok["h_read_pe"] = max(tok["h_slab_pe"])
            g2_done_act = hidden_layer(h2t, g1_c, g2_c, sh2_s,
                                       0, g1_done_act, h_done_dve, 0)
            gbuf_pe = max(tok["h_slab_pe"])

            def ln4_half(base, srcs4, toks4, Hh):
                stok4 = ln_stats_batch(srcs4, toks4, "dve", cb=base)
                for j in range(4):
                    i = base + j
                    lg.wait("vector", "dve", stok4)
                    lg.op("vector", lambda e, i=i: e.scalar_tensor_tensor(
                        x_tile(i), y3_t(i), stats[:, C_R + i:C_R + i + 1],
                        x_tile(i), ALU.mult, ALU.add))
                    tx = lg.op("vector", lambda e, i=i: e.tensor_scalar(
                        x_tile(i), x_tile(i),
                        stats[:, C_NMR + i:C_NMR + i + 1],
                        None, ALU.add), inc="dve")
                    lg.wait("gpsimd", "dve", tx)
                    SY.dma(lambda e, i=i, Hh=Hh: e.dma_start(
                        out_t[Hh * 8 + i], x_tile(i)))
                    tok["store"][i] = SY.close()

            w3_toks = {}
            w3_pe = {}

            def load_w3(oq):
                par = oq % 2
                WQ.emit_all()
                if par == 0:
                    lg.wait("gpsimd", "pe", tok["xfm_pe"])
                else:
                    lg.wait("gpsimd", "pe", max(tok["h_slab_pe"]))
                    lg.wait("gpsimd", "pe", tok["slab_attn_pe"])
                if oq >= 2:
                    lg.wait("gpsimd", "pe", w3_pe[oq - 2])
                WQ.dma(lambda e, oq=oq, par=par: e.dma_start(
                    sl_w3(par), w3t[oq % 4]))
                w3_toks[oq] = WQ.close()
                WQ.emit_all()

            load_w3(0)
            load_w3(1)
            y3_dve_toks = [0] * 8
            for L in range(4):
                oq, par = L, L % 2
                last_pe = 0
                for tt in range(8):
                    bi = 4 + (tt % 4)
                    pe_t = mm_group(
                        bi, ps[bi][:, 0:256],
                        [(lambda ic=ic, tt=tt:
                          g2_c(ic)[:, tt * 128:(tt + 1) * 128],
                          lambda ic=ic, par=par:
                          sl_w3(par)[:, ic * 256:(ic + 1) * 256])
                         for ic in range(32)],
                        waits=[("wqs", w3_toks[L]), ("act", g2_done_act),
                               ("lds", g_consts)])
                    last_pe = pe_t
                    lg.wait("vector", "pe", pe_t)
                    tv = lg.op("vector", lambda e, oq=oq, tt=tt, bi=bi:
                               e.tensor_tensor(
                                   y3_t(tt)[:, oq * 256:(oq + 1) * 256],
                                   ps[bi][:, 0:256],
                                   s3_bc[:, oq * 256:(oq + 1) * 256], ALU.mult),
                               inc="dve")
                    bank_guard[bi] = ("dve", tv)
                    y3_dve_toks[tt] = tv
                    if L == 3 and tt == 3:
                        ln4_half(0, [y3_t(i) for i in range(4)],
                                 y3_dve_toks[0:4], Hh)
                w3_pe[L] = last_pe
                if L + 2 < 4:
                    load_w3(L + 2)
            ln4_half(4, [y3_t(i) for i in range(4, 8)],
                     y3_dve_toks[4:8], Hh)
            w3_pe[6] = w3_pe[2]
            w3_pe[7] = w3_pe[3]
            WQ.emit_all()
            tok["xfm_pe"] = max(tok["xfm_pe"], w3_pe[6])
            tok["slab_attn_pe"] = max(tok["slab_attn_pe"], w3_pe[7])
            tok["w1_slab"] = [w3_pe[7]] * 2
            tok["w2_slab"] = [w3_pe[7]] * 2
            tok["h_read_pe"] = max(tok["h_read_pe"], w3_pe[7])
            tok["qt_pe"] = max(tok["qt_pe"], gbuf_pe, w3_pe[7])
            if debug_tag == "y3" and Hh == 0:
                dbg_tap("y3", lambda: y3_t(0), 1024, wait_sems=("dve",))

            g1_guard_dve = lg.count["dve"]

        g_final = SY.flush()
        TR.flush()
        WQ.flush()
        lg.wait("gpsimd", "lds", g_final)

        cleanup = True
        with nc.Block() as block:
            @block.sync
            def _(e):
                for fn in lg.prog["sync"]:
                    fn(e)

            @block.gpsimd
            def _(e):
                for fn in lg.prog["gpsimd"]:
                    fn(e)

            @block.tensor
            def _(e):
                for fn in lg.prog["tensor"]:
                    fn(e)

            @block.vector
            def _(e):
                for fn in lg.prog["vector"]:
                    fn(e)

            @block.scalar
            def _(e):
                for fn in lg.prog["scalar"]:
                    fn(e)

        nc.all_engine_barrier()
        for h in lg.sems.values():
            nc.gpsimd.sem_clear(h)
        nc.all_engine_barrier()

    nc._dev_ledger = lg.events if hasattr(lg, 'events') else None
    return nc



def _prep_consts(inp):
    f = np.float32
    Wk = np.ascontiguousarray(np.asarray(inp["L0_attn_Wk"], f).T)
    Wv = np.ascontiguousarray(np.asarray(inp["L0_attn_Wv"], f).T)
    Wq = np.ascontiguousarray(np.asarray(inp["L0_attn_Wq"], f).T)
    Wo = np.ascontiguousarray(np.asarray(inp["L0_attn_Wo"], f).T)
    bk = np.ascontiguousarray(np.asarray(inp["L0_attn_bk"], f).reshape(M, 1))
    bv = np.ascontiguousarray(np.asarray(inp["L0_attn_bv"], f).reshape(M, 1))
    bq = np.ascontiguousarray(np.asarray(inp["L0_attn_bq"], f).reshape(M, 1))
    bo = np.ascontiguousarray(
        np.broadcast_to(np.asarray(inp["L0_attn_bo"], f), (128, D)))
    s3 = np.ascontiguousarray(
        np.broadcast_to(np.asarray(inp["L0_ffn_w3_s"], f), (128, D)))

    def tile_wT(w, n_in_chunks, n_out_chunks):
        wt = np.asarray(w, f).T
        t = wt.reshape(n_in_chunks, 128, n_out_chunks, 128)
        t = np.ascontiguousarray(t.transpose(2, 1, 0, 3))
        return t.reshape(n_out_chunks, 128, n_in_chunks * 128)

    w1t = tile_wT(inp["L0_ffn_w1_w"], 8, 32)
    w2t = tile_wT(inp["L0_ffn_w2_w"], 8, 32)
    h1t = tile_wT(inp["L0_ffn_h1_w"], 32, 32)
    h2t = tile_wT(inp["L0_ffn_h2_w"], 32, 32)
    w3T = np.asarray(inp["L0_ffn_w3_w"], f).T
    t = w3T.reshape(32, 128, 4, 256)
    w3t = np.ascontiguousarray(t.transpose(2, 1, 0, 3)).reshape(4, 128, 32 * 256)

    sc = lambda s_: np.ascontiguousarray(np.asarray(s_, f).reshape(32, 128).T)
    s1, s2 = sc(inp["L0_ffn_w1_s"]), sc(inp["L0_ffn_w2_s"])
    sh1, sh2 = sc(inp["L0_ffn_h1_s"]), sc(inp["L0_ffn_h2_s"])

    jj, ii = np.meshgrid(np.arange(128), np.arange(128), indexing="ij")
    mka = np.where(ii > jj, LR * DECAY ** (ii - 1 - jj), 0.0).astype(f)
    mkd = np.ascontiguousarray(
        np.broadcast_to((DECAY ** np.arange(128)).astype(f), (128, 128)))
    mkw = np.ascontiguousarray(
        np.broadcast_to((LR * DECAY ** (127 - np.arange(128))).astype(f),
                        (128, 128)))
    return dict(wk=Wk, wv=Wv, wq=Wq, wo=Wo, bk=bk, bv=bv, bq=bq, bo=bo,
                s3=s3, w1t=w1t, w2t=w2t, h1t=h1t, h2t=h2t, w3t=w3t,
                s1=s1, s2=s2, sh1=sh1, sh2=sh2, mka=mka, mkd=mkd, mkw=mkw,
                identin=np.eye(128, dtype=f))


def make_in_maps(inputs):
    x = np.asarray(inputs["L0_attn_x"], np.float32)
    consts = _prep_consts(inputs)
    zeros = np.zeros((TOK, D), np.float32)
    ones_m = np.ones((128, 1), np.float32)
    zeros_m = np.zeros((128, 1), np.float32)
    in_maps = []
    for c in range(8):
        b, h = c // 2, c % 2
        m = dict(consts)
        m["xo"] = np.ascontiguousarray(x[b, h * TOK:(h + 1) * TOK])
        m["xp"] = np.ascontiguousarray(x[b, 0:TOK]) if h == 1 else zeros
        m["smask"] = ones_m if h == 1 else zeros_m
        in_maps.append(m)
    return in_maps


def kernel(**inputs):
    nc = build_program()
    in_maps = make_in_maps(inputs)
    res = run_bass_kernel_spmd(nc, in_maps, core_ids=list(range(8)))
    out = np.empty((B, S, D), np.float32)
    for c in range(8):
        b, h = c // 2, c % 2
        out[b, h * TOK:(h + 1) * TOK] = res.results[c]["out"]
    return out


# version 19 (final)
# speedup vs baseline: 1.2147x; 1.0052x over previous
from collections import deque
from contextlib import ExitStack

import numpy as np

import concourse.bass as bass
import concourse.mybir as mybir
from concourse.bass_utils import run_bass_kernel_spmd

F32 = mybir.dt.float32
BF16 = mybir.dt.bfloat16
AF = mybir.ActivationFunctionType
ALU = mybir.AluOpType
AX = mybir.AxisListType

B, S, D, M, HID = 4, 4096, 1024, 128, 4096
TOK = 2048
DECAY, LR = 0.99, 0.01
EPS = 1e-5
DECAY_C = float(DECAY ** 128)
R1024 = 1.0 / 1024.0
USE_TS = False

ENGS = ("sync", "gpsimd", "tensor", "vector", "scalar")
C_SUM, C_SQ, C_M, C_VAR, C_SD, C_R, C_NMR, C_TMP = (i * 8 for i in range(8))


class Ledger:
    def __init__(self, nc, ctx, sems):
        self.nc = nc
        self.prog = {e: [] for e in ENGS}
        self.sems = {}
        self.count = {}
        self.waited = {e: {} for e in ENGS}
        self.events = {e: [] for e in ENGS}
        for s in sems:
            self.sems[s] = ctx.enter_context(nc.semaphore(s))
            self.count[s] = 0

    def wait(self, eng, name, val=None):
        if val is None:
            val = self.count[name]
        if val <= 0 or val <= self.waited[eng].get(name, 0):
            return
        self.waited[eng][name] = val
        self.events[eng].append(("w", name, val))
        h = self.sems[name]
        self.prog[eng].append(lambda e, h=h, v=val: e.wait_ge(h, v))

    def op(self, eng, fn, inc=None, inck=1):
        if inc is None:
            self.prog[eng].append(fn)
            return None
        self.count[inc] += inck
        self.events[eng].append(("i", inc, inck))
        h = self.sems[inc]
        self.prog[eng].append(lambda e, fn=fn, h=h, k=inck: fn(e).then_inc(h, k))
        return self.count[inc]


class DmaStream:

    def __init__(self, lg, eng, qnames, sig):
        self.lg = lg
        self.eng = eng
        self.q = qnames
        self.sig = sig
        self.qi = 0
        self.issued = {q: 0 for q in qnames}
        self.open = False
        self.pending = deque()
        self.n_closed = 0

    def begin(self):
        if not self.open:
            q = self.q[self.qi]
            while any(p[0] == q for p in self.pending):
                self._emit_one()
            self.open = True

    def dma(self, fn):
        self.begin()
        q = self.q[self.qi]
        self.issued[q] += 1
        self.lg.op(self.eng, fn, inc=q, inck=16)

    def close(self):
        if not self.open:
            return self.n_closed
        q = self.q[self.qi]
        self.n_closed += 1
        self.pending.append((q, self.issued[q] * 16, self.n_closed))
        self.open = False
        self.qi = (self.qi + 1) % len(self.q)
        return self.n_closed

    def _emit_one(self):
        q, v, _ = self.pending.popleft()
        self.lg.wait(self.eng, q, v)
        h = self.lg.sems[self.sig]
        self.lg.count[self.sig] += 1
        self.lg.events[self.eng].append(("i", self.sig, 1))
        self.lg.prog[self.eng].append(lambda e, h=h: e.sem_inc(h, 1))

    def ensure(self, gtok):
        while self.pending and self.lg.count[self.sig] < gtok:
            self._emit_one()

    def emit_all(self):
        while self.pending:
            self._emit_one()

    def flush(self):
        self.close()
        while self.pending:
            self._emit_one()
        return self.lg.count[self.sig]


def build_program(debug_tag=None):
    nc = bass.Bass("TRN2", target_bir_lowering=False)

    di = lambda n, shp: nc.dram_tensor(n, shp, F32, kind="ExternalInput").ap()
    xo = di("xo", [TOK, D])
    xp = di("xp", [TOK, D])
    wk, wq, wv = di("wk", [D, M]), di("wq", [D, M]), di("wv", [D, M])
    wo = di("wo", [M, D])
    bk, bq, bv = di("bk", [M, 1]), di("bq", [M, 1]), di("bv", [M, 1])
    bo_in = di("bo", [128, D])
    s3_in = di("s3", [128, D])
    s1_in, s2_in = di("s1", [128, 32]), di("s2", [128, 32])
    sh1_in, sh2_in = di("sh1", [128, 32]), di("sh2", [128, 32])
    mka_in = di("mka", [128, 128])
    mkd_in = di("mkd", [128, 128])
    mkw_in = di("mkw", [128, 128])
    smask_in = di("smask", [128, 1])
    ident_in = di("identin", [128, 128])
    w1t = di("w1t", [32, 128, 1024])
    w2t = di("w2t", [32, 128, 1024])
    h1t = di("h1t", [32, 128, 4096])
    h2t = di("h2t", [32, 128, 4096])
    w3t = di("w3t", [4, 128, 8192])
    out = nc.dram_tensor("out", [TOK, D], F32, kind="ExternalOutput").ap()
    dbg = None
    if debug_tag is not None:
        dbg = nc.dram_tensor("dbg", [128, 4096], F32, kind="ExternalOutput").ap()

    xo_t = xo.rearrange("(t p) d -> t p d", p=128)
    xp_t = xp.rearrange("(t p) d -> t p d", p=128)
    out_t = out.rearrange("(t p) d -> t p d", p=128)

    ctx = ExitStack()
    with ctx:
        sb = lambda n, shp, dt: ctx.enter_context(nc.sbuf_tensor(n, shp, dt))
        xres = sb("xres", [128, 8 * 1024], F32)
        bk_s, bq_s, bv_s = (sb(n, [128, 1], F32) for n in ("bk_s", "bq_s", "bv_s"))
        s3_bc = sb("s3_bc", [128, 1024], F32)
        s1_s, s2_s = sb("s1_s", [128, 32], F32), sb("s2_s", [128, 32], F32)
        sh1_s, sh2_s = sb("sh1_s", [128, 32], F32), sb("sh2_s", [128, 32], F32)
        mask_a = sb("mask_a", [128, 128], F32)
        mask_dk = sb("mask_dk", [128, 128], BF16)
        mask_kw = sb("mask_kw", [128, 128], BF16)
        smask = sb("smask_b", [128, 1], F32)
        st = sb("st", [128, 256], F32)
        stats = sb("stats", [128, 64], F32)
        ident = sb("ident", [128, 128], F32)
        xfm = sb("xfm", [128, 8192], BF16)
        kqv = sb("kqv", [128, 3840], BF16)
        hbuf = sb("hbuf", [128, 32768], BF16)
        gbuf = sb("gbuf", [128, 32768], BF16)
        slab = sb("slab", [128, 8192], BF16)

        x_tile = lambda i: xres[:, i * 1024:(i + 1) * 1024]
        xfm_c = lambda c: xfm[:, c * 1024:(c + 1) * 1024]
        cw_k = slab[:, 0:1024]
        cw_q = slab[:, 1024:2048]
        cw_v = slab[:, 2048:3072]
        cw_o = slab[:, 3072:4096]
        bo_bc = gbuf[:, 24576:26624].bitcast(F32)
        k_fm = kqv[:, 0:1024]
        q_fm = kqv[:, 1024:2048]
        v_fm = kqv[:, 2048:3072]
        kw_scr = lambda p: kqv[:, 3072 + p * 128: 3072 + (p + 1) * 128]
        ktm_scr = lambda p: kqv[:, 3328 + p * 128: 3328 + (p + 1) * 128]
        vtm_scr = lambda p: kqv[:, 3584 + p * 128: 3584 + (p + 1) * 128]
        t1_buf = lambda par: kqv[:, par * 1024:(par + 1) * 1024].bitcast(F32)
        h_c = lambda ic: hbuf[:, ic * 1024:(ic + 1) * 1024]
        g2_c = h_c
        xp_buf = lambda i: hbuf[:, i * 2048:(i + 1) * 2048].bitcast(F32)
        g1_c = lambda ic: gbuf[:, ic * 1024:(ic + 1) * 1024]
        y_buf = gbuf[:, 0:2048].bitcast(F32)
        xln_buf = lambda p: gbuf[:, 2048 + p * 1024: 2048 + (p + 1) * 1024]
        sq_buf = gbuf[:, 4096:6144].bitcast(F32)
        qt_all = lambda c: gbuf[:, 6144 + c * 128: 6144 + (c + 1) * 128]
        kw_f32 = lambda c: gbuf[:, 26624 + c * 256: 26624 + (c + 1) * 256].bitcast(F32)
        ktm_all = lambda c: gbuf[:, 12288 + c * 128: 12288 + (c + 1) * 128]
        v_f32 = gbuf[:, 14336:16384].bitcast(F32)
        y3_t = lambda i: gbuf[:, 8192 + i * 2048: 8192 + (i + 1) * 2048].bitcast(F32)
        sl_w1 = lambda p: slab[:, p * 1024:(p + 1) * 1024]
        sl_w2 = lambda p: slab[:, 2048 + p * 1024: 2048 + (p + 1) * 1024]
        sl_h = lambda p: slab[:, p * 4096:(p + 1) * 4096]
        vtm_all = lambda c: slab[:, 4096 + c * 128: 4096 + (c + 1) * 128]
        abf_all = lambda c: slab[:, 5120 + c * 128: 5120 + (c + 1) * 128]
        snap_all = lambda c: slab[:, 6144 + c * 128: 6144 + (c + 1) * 128]
        outs_all = lambda c: slab[:, 7168 + c * 128: 7168 + (c + 1) * 128]
        sl_w3 = lambda p: (xfm if p == 0 else slab)[:, 0:8192]

        st_slot = [0]
        st_cur = lambda: st[:, st_slot[0] * 128:(st_slot[0] + 1) * 128]
        st_nxt = lambda: st[:, (1 - st_slot[0]) * 128:(2 - st_slot[0]) * 128]

        ps = [ctx.enter_context(nc.psum_tensor(f"ps{i}", [128, 512], F32))
              for i in range(8)]
        bank_guard = [("dve", 0)] * 8

        lg = Ledger(nc, ctx, (
            "q_sy0", "q_sy1", "q_tr0", "q_tr1", "q_tr2", "q_tr3",
            "q_ts0", "q_ts1", "q_ts2", "q_ts3",
            "q_wq0", "q_wq1", "lds", "trs", "tss", "wqs", "pe", "act", "dve", "gp"))
        SY = DmaStream(lg, "sync", ["q_sy0", "q_sy1"], "lds")
        TR = DmaStream(lg, "scalar", ["q_tr0", "q_tr1", "q_tr2", "q_tr3"], "trs")
        TS = DmaStream(lg, "sync", ["q_ts0", "q_ts1", "q_ts2", "q_ts3"], "tss")
        WQ = DmaStream(lg, "gpsimd", ["q_wq0", "q_wq1"], "wqs")

        for h in lg.sems.values():
            nc.gpsimd.sem_clear(h)
        nc.all_engine_barrier()

        dbg_state = {"done": False}

        def dbg_tap(tag, ap_fn, ncols, wait_sems=()):
            if debug_tag != tag or dbg_state["done"]:
                return
            dbg_state["done"] = True
            if "lds" in wait_sems:
                gg = SY.close()
                SY.ensure(gg)
            for sname in wait_sems:
                lg.wait("gpsimd", sname)
            SY.close()
            SY.dma(lambda e, a=ap_fn(), n=ncols: e.dma_start(dbg[:, 0:n], a))
            g = SY.close()
            SY.ensure(g)
            for e_ in ENGS:
                lg.wait(e_, "lds", g)

        cwtok = {"g": 0}

        def load_cw(guard_pe):
            WQ.emit_all()
            lg.wait("gpsimd", "pe", guard_pe)
            WQ.dma(lambda e: e.dma_start(
                cw_k.rearrange("p (c m) -> p c m", m=128),
                wk.rearrange("(c p) m -> p c m", p=128)))
            WQ.dma(lambda e: e.dma_start(
                cw_q.rearrange("p (c m) -> p c m", m=128),
                wq.rearrange("(c p) m -> p c m", p=128)))
            WQ.dma(lambda e: e.dma_start(
                cw_v.rearrange("p (c m) -> p c m", m=128),
                wv.rearrange("(c p) m -> p c m", p=128)))
            WQ.dma(lambda e: e.dma_start(cw_o, wo))
            cwtok["g"] = WQ.close()
            WQ.emit_all()

        load_cw(0)
        WQ.dma(lambda e: e.dma_start(mask_dk[:], mkd_in))
        WQ.dma(lambda e: e.dma_start(mask_kw[:], mkw_in))
        g_masks = WQ.close()
        WQ.emit_all()

        for dst, src in ((bk_s[:], bk), (bq_s[:], bq), (bv_s[:], bv),
                         (s3_bc[:], s3_in), (s1_s[:], s1_in), (s2_s[:], s2_in),
                         (sh1_s[:], sh1_in), (sh2_s[:], sh2_in),
                         (mask_a[:], mka_in), (smask[:], smask_in),
                         (ident[:], ident_in)):
            SY.dma(lambda e, d=dst, s=src: e.dma_start(d, s))
        g_consts = SY.close()

        lg.op("vector", lambda e: e.memset(st[:, 0:128], 0.0))


        tok = {
            "xfm_pe": 0,
            "xln_tr": [None, None],
            "xsc_pe": [0, 0],
            "kw_tr": [0, 0],
            "vtm_pe": [0, 0],
            "w1_slab": [0, 0], "w2_slab": [0, 0],
            "h_slab_pe": [0, 0],
            "slab_attn_pe": 0,
            "qt_pe": 0,
            "t1_free": {},
            "store": [0] * 8,
            "y_sq_act": 0,
            "st_tok": 0,
            "h_read_pe": 0,
        }

        def mm_group(bank_idx, out_ap, pairs, waits=()):
            sname, v = bank_guard[bank_idx]
            lg.wait("tensor", sname, v)
            for sn, vv in waits:
                lg.wait("tensor", sn, vv)
            n = len(pairs)
            t = None
            for j, (lf, rf) in enumerate(pairs):
                t = lg.op("tensor",
                          lambda e, lf=lf, rf=rf, st_=(j == 0), sp=(j == n - 1),
                          o=out_ap: e.matmul(o, lf(), rf(), start=st_, stop=sp),
                          inc="pe" if j == n - 1 else None)
            return t

        def ln_stats_batch(srcs, src_toks, src_sem, cb=0):
            n = len(srcs)
            a_last = None
            for i, (sap, stok) in enumerate(zip(srcs, src_toks)):
                if stok:
                    lg.wait("scalar", src_sem, stok)
                a_last = lg.op(
                    "scalar",
                    lambda e, sap=sap, i=i, cb=cb: e.activation(
                        sq_buf, sap, AF.Square,
                        accum_out=stats[:, C_SQ + cb + i:C_SQ + cb + i + 1]),
                    inc="act")
            t_red = None
            for i, (sap, stok) in enumerate(zip(srcs, src_toks)):
                if stok:
                    lg.wait("vector", src_sem, stok)
                t_red = lg.op("vector",
                              lambda e, sap=sap, i=i, cb=cb: e.reduce_sum(
                                  stats[:, C_SUM + cb + i:C_SUM + cb + i + 1], sap,
                                  axis=AX.X), inc="dve")
            lg.wait("vector", "act", a_last)
            lg.wait("vector", "dve", t_red)
            tM = lg.op("vector", lambda e, n=n, cb=cb: e.tensor_scalar(
                stats[:, C_M + cb:C_M + cb + n], stats[:, C_SUM + cb:C_SUM + cb + n],
                R1024, None, ALU.mult), inc="dve")
            lg.op("vector", lambda e, n=n, cb=cb: e.tensor_scalar(
                stats[:, C_SD + cb:C_SD + cb + n], stats[:, C_SQ + cb:C_SQ + cb + n],
                R1024, None, ALU.mult))
            lg.wait("vector", "dve", tM)
            tT = lg.op("vector", lambda e, n=n, cb=cb: e.tensor_tensor(
                stats[:, C_TMP + cb:C_TMP + cb + n], stats[:, C_M + cb:C_M + cb + n],
                stats[:, C_M + cb:C_M + cb + n], ALU.mult), inc="dve")
            lg.wait("vector", "dve", tT)
            tv = lg.op("vector", lambda e, n=n, cb=cb: e.scalar_tensor_tensor(
                stats[:, C_VAR + cb:C_VAR + cb + n], stats[:, C_SD + cb:C_SD + cb + n], EPS,
                stats[:, C_TMP + cb:C_TMP + cb + n], ALU.add, ALU.subtract), inc="dve")
            lg.wait("scalar", "dve", tv)
            ta = lg.op("scalar", lambda e, n=n, cb=cb: e.activation(
                stats[:, C_SD + cb:C_SD + cb + n], stats[:, C_VAR + cb:C_VAR + cb + n],
                AF.Sqrt), inc="act")
            lg.wait("vector", "act", ta)
            tR = lg.op("vector", lambda e, n=n, cb=cb: e.reciprocal(
                stats[:, C_R + cb:C_R + cb + n], stats[:, C_SD + cb:C_SD + cb + n]), inc="dve")
            lg.wait("vector", "dve", tR)
            tn = lg.op("vector", lambda e, n=n, cb=cb: e.scalar_tensor_tensor(
                stats[:, C_NMR + cb:C_NMR + cb + n], stats[:, C_M + cb:C_M + cb + n], -1.0,
                stats[:, C_R + cb:C_R + cb + n], ALU.mult, ALU.mult), inc="dve")
            return tn

        def trx_wait(eng, toks):
            if toks is None:
                return
            for s_, v in toks:
                lg.wait(eng, s_, v)

        def tok_waits(toklists):
            m = {}
            for tl in toklists:
                if tl is None:
                    continue
                for s_, v in tl:
                    m[s_] = max(m.get(s_, 0), v)
            return list(m.items())

        def pe_tr_block(dst_ap, src_ap, bank_idx, waits, xfm_guard_pe):
            sname, v = bank_guard[bank_idx]
            lg.wait("tensor", sname, v)
            for sn, vv in waits:
                lg.wait("tensor", sn, vv)
            lg.wait("tensor", "lds", g_consts)
            t = lg.op("tensor", lambda e, d=src_ap, bi=bank_idx: e.transpose(
                ps[bi][:, 0:128], d, ident[:]), inc="pe")
            lg.wait("vector", "pe", t)
            lg.wait("vector", "pe", xfm_guard_pe)
            ev = lg.op("vector", lambda e, o=dst_ap, bi=bank_idx: e.tensor_copy(
                o, ps[bi][:, 0:128]), inc="dve")
            bank_guard[bank_idx] = ("dve", ev)
            return t, ev

        def ln_to_xfm(srcs, src_toks, src_sem, base=0, dma_path=False):
            stok = ln_stats_batch(srcs, src_toks, src_sem, cb=base)
            tr_toks = [None] * len(srcs)
            for j, sap in enumerate(srcs):
                i = base + j
                p = i % 2
                if dma_path:
                    lg.wait("scalar", "dve", stok)
                    trx_wait("scalar", tok["xln_tr"][p])
                    t_ap = lg.op("scalar", lambda e, s=sap, i=i, p=p:
                                 e.activation(
                                     xln_buf(p), s, AF.Identity,
                                     scale=stats[:, C_R + i:C_R + i + 1],
                                     bias=stats[:, C_NMR + i:C_NMR + i + 1]),
                                 inc="act")
                    lg.wait("scalar", "pe", tok["xfm_pe"])
                    lg.wait("scalar", "act", t_ap)
                    for c in range(8):
                        TR.dma(lambda e, c=c, i=i, p=p: e.dma_start(
                            xfm_c(c)[:, i * 128:(i + 1) * 128],
                            xln_buf(p)[:, c * 128:(c + 1) * 128],
                            transpose=True))
                    g = TR.close()
                    TR.emit_all()
                    tok["xln_tr"][p] = [("trs", g)]
                    tr_toks[j] = [("trs", g)]
                    continue
                xsc = y3_t(p)
                lg.wait("scalar", "dve", stok)
                lg.wait("scalar", "pe", tok["xsc_pe"][p])
                t_ap = lg.op("scalar", lambda e, s=sap, i=i, x=xsc: e.activation(
                    x, s, AF.Identity,
                    scale=stats[:, C_R + i:C_R + i + 1],
                    bias=stats[:, C_NMR + i:C_NMR + i + 1]), inc="act")
                last_ev = 0
                last_t = 0
                for c in range(8):
                    last_t, last_ev = pe_tr_block(
                        xfm_c(c)[:, i * 128:(i + 1) * 128],
                        xsc[:, c * 128:(c + 1) * 128],
                        6 + c % 2, [("act", t_ap)], tok["xfm_pe"])
                tok["xsc_pe"][p] = last_t
                tr_toks[j] = [("dve", last_ev)]
            return tr_toks

        def proj_pass(tr_toks, do_q):
            plist = [("k", cw_k, bk_s, k_fm, (0, 1)),
                     ("v", cw_v, bv_s, v_f32, (2, 3))]
            if do_q:
                plist.append(("q", cw_q, bq_s, q_fm, (4, 5)))
            ev = {}
            for name, cw, bias, dst, banks in plist:
                ev[name] = [0, 0]
                for tg in range(2):
                    bi = banks[tg]
                    tws = tok_waits(tr_toks[tg * 4:(tg + 1) * 4])
                    pe_t = mm_group(
                        bi, ps[bi][:],
                        [(lambda ic=ic, cw=cw: cw[:, ic * 128:(ic + 1) * 128],
                          lambda ic=ic, tg=tg: xfm_c(ic)[:, tg * 512:(tg + 1) * 512])
                         for ic in range(8)],
                        waits=tws + [("wqs", cwtok["g"]), ("lds", g_consts)])
                    lg.wait("scalar", "pe", pe_t)
                    lg.wait("scalar", "pe", tok["qt_pe"])
                    a_t = lg.op("scalar",
                                lambda e, b=ps[bi], d=dst, tg=tg, bias=bias:
                                e.activation(d[:, tg * 512:(tg + 1) * 512], b[:],
                                             AF.Identity, bias=bias[:]),
                                inc="act")
                    ev[name][tg] = a_t
                    bank_guard[bi] = ("act", a_t)
            return ev

        def attn_chunks(ev, full):
            kw_toks = [0] * 8
            abf_toks = [0] * 8
            qt_toks = [0] * 8
            snap_toks = [0] * 8
            outs_toks = [0] * 8
            for c in range(8):
                par = c % 2
                lg.wait("vector", "act", ev["k"][c // 4])
                lg.wait("vector", "wqs", g_masks)
                lg.wait("vector", "pe", tok["qt_pe"])
                kw_toks[c] = lg.op("vector", lambda e, c=c: e.tensor_tensor(
                    kw_f32(c), k_fm[:, c * 128:(c + 1) * 128], mask_kw[:],
                    ALU.mult), inc="dve")
                if full:
                    lg.wait("vector", "act", ev["q"][c // 4])
                    qt_toks[c] = lg.op("vector", lambda e, c=c: e.tensor_tensor(
                        qt_all(c), q_fm[:, c * 128:(c + 1) * 128], mask_dk[:],
                        ALU.mult), inc="dve")
                    bi_a = 6 + par
                    pe_a = mm_group(
                        bi_a, ps[bi_a][:, 0:128],
                        [(lambda c=c: k_fm[:, c * 128:(c + 1) * 128],
                          lambda c=c: q_fm[:, c * 128:(c + 1) * 128])],
                        waits=[("act", max(ev["k"][c // 4], ev["q"][c // 4]))])
                    lg.wait("vector", "pe", pe_a)
                    lg.wait("vector", "lds", g_consts)
                    lg.wait("vector", "pe", tok["slab_attn_pe"])
                    abf_toks[c] = lg.op("vector", lambda e, c=c, bi=bi_a:
                                        e.tensor_tensor(abf_all(c),
                                                        ps[bi][:, 0:128],
                                                        mask_a[:], ALU.mult),
                                        inc="dve")
                    bank_guard[bi_a] = ("dve", abf_toks[c])
            tp_toks = [None] * 8
            for c in range(8):
                _, ev_k = pe_tr_block(
                    ktm_all(c), kw_f32(c), c % 4,
                    [("dve", kw_toks[c]), ("pe", tok["qt_pe"])], 0)
                _, ev_v = pe_tr_block(
                    vtm_all(c), v_f32[:, c * 128:(c + 1) * 128], (c + 1) % 4,
                    [("act", ev["v"][c // 4]), ("pe", tok["slab_attn_pe"])], 0)
                tp_toks[c] = [("dve", ev_v)]
            tr_pair = None
            for c in range(8):
                par = c % 2
                bi = 4 + par
                pe_t = mm_group(
                    bi, ps[bi][:, 0:128],
                    [(lambda c=c: ktm_all(c), lambda c=c: vtm_all(c))],
                    waits=tp_toks[c])
                lg.wait("vector", "dve", tok["st_tok"])
                if full:
                    lg.wait("vector", "pe", tok["slab_attn_pe"])
                    _scur = st_cur()
                    snap_toks[c] = lg.op("vector", lambda e, c=c, s_=_scur:
                                         e.tensor_copy(snap_all(c), s_),
                                         inc="dve")
                lg.wait("vector", "pe", pe_t)
                cur, nxt = st_cur(), st_nxt()
                tu = lg.op("vector", lambda e, bi=bi, cur=cur, nxt=nxt:
                           e.scalar_tensor_tensor(nxt, cur, DECAY_C,
                                                  ps[bi][:, 0:128],
                                                  ALU.mult, ALU.add), inc="dve")
                st_slot[0] = 1 - st_slot[0]
                tok["st_tok"] = tu
                bank_guard[bi] = ("dve", tu)
                if full:
                    bi_o = 6 + par
                    pe_o = mm_group(
                        bi_o, ps[bi_o][:, 128:256],
                        [(lambda c=c: vtm_all(c), lambda c=c: abf_all(c)),
                         (lambda c=c: snap_all(c), lambda c=c: qt_all(c))],
                        waits=[("dve", max(abf_toks[c], snap_toks[c],
                                           qt_toks[c]))] + tp_toks[c])
                    tok["qt_pe"] = pe_o
                    tok["slab_attn_pe"] = pe_o
                    lg.wait("scalar", "pe", pe_o)
                    outs_toks[c] = lg.op(
                        "scalar", lambda e, c=c, bi=bi_o: e.activation(
                            outs_all(c), ps[bi][:, 128:256], AF.Copy),
                        inc="act")
                    bank_guard[bi_o] = ("act", outs_toks[c])
            return abf_toks, snap_toks, qt_toks, outs_toks

        hb_act = 0
        hb_dve = 0
        for ph in range(2):
            xp_toks = []
            lg.wait("sync", "act", hb_act)
            lg.wait("sync", "dve", hb_dve)
            for i in range(8):
                SY.dma(lambda e, i=i, ph=ph: e.dma_start(
                    xp_buf(i), xp_t[ph * 8 + i]))
                if i % 4 == 3:
                    g = SY.close()
                    for j in range(i - 3, i + 1):
                        xp_toks.append(g)
            SY.emit_all()
            srcs = [xp_buf(i) for i in range(8)]
            if ph == 0:
                dbg_tap("pre_xp", lambda: xp_buf(0), 1024,
                        wait_sems=("lds",))
            tr_toks = ln_to_xfm(srcs, xp_toks, "lds")
            if ph == 0:
                dbg_tap("pre_stats", lambda: stats[:], 64,
                        wait_sems=("dve", "act"))
            hb_act = lg.count["act"]
            hb_dve = lg.count["dve"]
            ev = proj_pass(tr_toks, do_q=False)
            tok["xfm_pe"] = lg.count["pe"]
            attn_chunks(ev, full=False)
            tok["slab_attn_pe"] = lg.count["pe"]
            tok["qt_pe"] = lg.count["pe"]
        lg.wait("vector", "lds", g_consts)
        lg.wait("vector", "dve", tok["st_tok"])
        _cur, _nxt = st_cur(), st_nxt()
        tok["st_tok"] = lg.op("vector", lambda e, c=_cur, x=_nxt:
                              e.tensor_scalar(x, c, smask[:], None, ALU.mult),
                              inc="dve")
        st_slot[0] = 1 - st_slot[0]
        dbg_tap("prelude_st", lambda: st_cur(), 128, wait_sems=("dve",))

        g1_guard_dve = 0
        for Hh in range(2):
            WQ.emit_all()
            TR.emit_all()
            SY.emit_all()
            lg.wait("sync", "pe", tok["qt_pe"])
            SY.dma(lambda e: e.dma_start(bo_bc, bo_in))
            if Hh == 1:
                load_cw(max(tok["w1_slab"] + tok["w2_slab"]))
            x_toks = []
            for i in range(8):
                if Hh == 1:
                    SY.ensure(tok["store"][i])
                    lg.wait("sync", "lds", tok["store"][i])
                SY.dma(lambda e, i=i, Hh=Hh: e.dma_start(
                    x_tile(i), xo_t[Hh * 8 + i]))
                if i % 4 == 3:
                    g = SY.close()
                    for j in range(i - 3, i + 1):
                        x_toks.append(g)
            SY.emit_all()
            g_bo = x_toks[0]

            tr_toks = (ln_to_xfm([x_tile(i) for i in range(4)],
                                 x_toks[0:4], "lds", 0) +
                       ln_to_xfm([x_tile(i) for i in range(4, 8)],
                                 x_toks[4:8], "lds", 4))
            ev = proj_pass(tr_toks, do_q=True)
            tok["xfm_pe"] = lg.count["pe"]

            abf_toks, snap_toks, qt_toks, outs_toks = attn_chunks(
                ev, full=True)

            y_toks = [0] * 8
            for c in range(8):
                par = c % 2
                bY = (0, 1) if par == 0 else (2, 3)
                pe_y = 0
                for hd in range(2):
                    pe_y = mm_group(
                        bY[hd], ps[bY[hd]][:],
                        [(lambda c=c: outs_all(c),
                          lambda hd=hd: cw_o[:, hd * 512:(hd + 1) * 512])],
                        waits=[("act", outs_toks[c]), ("wqs", cwtok["g"])])
                tok["slab_attn_pe"] = max(tok["slab_attn_pe"], pe_y)
                lg.wait("vector", "pe", pe_y)
                lg.wait("vector", "lds", g_bo)
                ty = 0
                for hd in range(2):
                    ty = lg.op("vector", lambda e, c=c, hd=hd, bY=bY:
                               e.tensor_tensor(
                                   y3_t(c)[:, hd * 512:(hd + 1) * 512],
                                   ps[bY[hd]][:],
                                   bo_bc[:, hd * 512:(hd + 1) * 512], ALU.add),
                               inc="dve")
                    bank_guard[bY[hd]] = ("dve", ty)
                y_toks[c] = ty
            stok2 = ln_stats_batch([y3_t(c) for c in range(8)], y_toks, "dve")
            xup = [0] * 8
            for c in range(8):
                lg.wait("vector", "dve", stok2)
                lg.op("vector", lambda e, c=c: e.scalar_tensor_tensor(
                    x_tile(c), y3_t(c), stats[:, C_R + c:C_R + c + 1],
                    x_tile(c), ALU.mult, ALU.add))
                xup[c] = lg.op("vector", lambda e, c=c: e.tensor_scalar(
                    x_tile(c), x_tile(c), stats[:, C_NMR + c:C_NMR + c + 1],
                    None, ALU.add), inc="dve")
            tr_toks3 = (ln_to_xfm([x_tile(i) for i in range(4)],
                                  xup[0:4], "dve", 0) +
                        ln_to_xfm([x_tile(i) for i in range(4, 8)],
                                  xup[4:8], "dve", 4))
            attn_x_dve = lg.count["dve"]
            if debug_tag == "attn" and Hh == 0:
                dbg_tap("attn", lambda: x_tile(0), 1024, wait_sems=("dve",))


            w12_toks = {}

            def load_w12(oc):
                par = oc % 2
                WQ.emit_all()
                lg.wait("gpsimd", "pe", max(tok["w1_slab"][par],
                                            tok["w2_slab"][par],
                                            tok["xfm_pe"],
                                            tok["slab_attn_pe"]))
                WQ.dma(lambda e, oc=oc, par=par: e.dma_start(
                    sl_w1(par), w1t[oc]))
                WQ.dma(lambda e, oc=oc, par=par: e.dma_start(
                    sl_w2(par), w2t[oc]))
                w12_toks[oc] = WQ.close()

            load_w12(0)
            load_w12(1)
            h_done_dve = 0
            for oc in range(32):
                par = oc % 2
                p1b = (0, 1) if par == 0 else (4, 5)
                p2b = (2, 3) if par == 0 else (6, 7)
                for tg in range(2):
                    pe1 = mm_group(
                        p1b[tg], ps[p1b[tg]][:],
                        [(lambda ic=ic, par=par: sl_w1(par)[:, ic * 128:(ic + 1) * 128],
                          lambda ic=ic, tg=tg: xfm_c(ic)[:, tg * 512:(tg + 1) * 512])
                         for ic in range(8)],
                        waits=tok_waits(tr_toks3[tg * 4:(tg + 1) * 4]) + [
                               ("wqs", w12_toks[oc]), ("lds", g_consts)])
                    pe2 = mm_group(
                        p2b[tg], ps[p2b[tg]][:],
                        [(lambda ic=ic, par=par: sl_w2(par)[:, ic * 128:(ic + 1) * 128],
                          lambda ic=ic, tg=tg: xfm_c(ic)[:, tg * 512:(tg + 1) * 512])
                         for ic in range(8)],
                        waits=[("wqs", w12_toks[oc])])
                    tok["w1_slab"][par] = pe1
                    tok["w2_slab"][par] = pe2
                    tok["xfm_pe"] = pe2
                    lg.wait("scalar", "pe", pe1)
                    lg.wait("scalar", "dve", tok["t1_free"].get(par, 0))
                    ta = lg.op("scalar",
                               lambda e, oc=oc, par=par, b=ps[p1b[tg]]:
                               e.activation(t1_buf(par), b[:], AF.Silu,
                                            scale=s1_s[:, oc:oc + 1]), inc="act")
                    bank_guard[p1b[tg]] = ("act", ta)
                    lg.wait("vector", "pe", pe2)
                    lg.wait("vector", "act", ta)
                    lg.wait("vector", "pe", tok["h_read_pe"])
                    th = lg.op("vector",
                               lambda e, oc=oc, par=par, tg=tg, b=ps[p2b[tg]]:
                               e.scalar_tensor_tensor(
                                   h_c(oc)[:, tg * 512:(tg + 1) * 512], b[:],
                                   s2_s[:, oc:oc + 1], t1_buf(par),
                                   ALU.mult, ALU.mult), inc="dve")
                    bank_guard[p2b[tg]] = ("dve", th)
                    tok["t1_free"][par] = th
                    h_done_dve = th
                if oc + 2 < 32:
                    load_w12(oc + 2)

            def hidden_layer(wsrc, in_c, out_c, scale_s, in_dve, in_act,
                             evac_dve_guard, first_slab_extra_pe):
                slab_toks = {}

                def load_h(oc):
                    par = oc % 2
                    WQ.emit_all()
                    lg.wait("gpsimd", "pe", tok["h_slab_pe"][par])
                    WQ.dma(lambda e, oc=oc, par=par, wsrc=wsrc:
                           e.dma_start(sl_h(par), wsrc[oc]))
                    slab_toks[oc] = WQ.close()

                WQ.emit_all()
                lg.wait("gpsimd", "pe", first_slab_extra_pe)
                load_h(0)
                load_h(1)
                last_evac = 0
                for oc in range(32):
                    par = oc % 2
                    banks = (0, 1) if par == 0 else (2, 3)
                    for tg in range(2):
                        waits = [("wqs", slab_toks[oc]),
                                 ("lds", g_consts)]
                        if in_dve:
                            waits.append(("dve", in_dve))
                        if in_act:
                            waits.append(("act", in_act))
                        pe_t = mm_group(
                            banks[tg], ps[banks[tg]][:],
                            [(lambda ic=ic, par=par:
                              sl_h(par)[:, ic * 128:(ic + 1) * 128],
                              lambda ic=ic, tg=tg:
                              in_c(ic)[:, tg * 512:(tg + 1) * 512])
                             for ic in range(32)],
                            waits=waits)
                        tok["h_slab_pe"][par] = pe_t
                        lg.wait("scalar", "pe", pe_t)
                        lg.wait("scalar", "dve", evac_dve_guard)
                        ta = lg.op(
                            "scalar",
                            lambda e, oc=oc, tg=tg, b=ps[banks[tg]], ss=scale_s:
                            e.activation(out_c(oc)[:, tg * 512:(tg + 1) * 512],
                                         b[:], AF.Gelu,
                                         scale=ss[:, oc:oc + 1]), inc="act")
                        bank_guard[banks[tg]] = ("act", ta)
                        last_evac = ta
                    if oc + 2 < 32:
                        load_h(oc + 2)
                return last_evac

            first_extra = max(tok["w1_slab"] + tok["w2_slab"] +
                              [tok["slab_attn_pe"]])
            g1_done_act = hidden_layer(h1t, h_c, g1_c, sh1_s,
                                       h_done_dve, 0, g1_guard_dve, first_extra)
            tok["h_read_pe"] = max(tok["h_slab_pe"])
            g2_done_act = hidden_layer(h2t, g1_c, g2_c, sh2_s,
                                       0, g1_done_act, h_done_dve, 0)
            gbuf_pe = max(tok["h_slab_pe"])

            def ln4_half(base, srcs4, toks4, Hh):
                stok4 = ln_stats_batch(srcs4, toks4, "dve", cb=base)
                for j in range(4):
                    i = base + j
                    lg.wait("vector", "dve", stok4)
                    lg.op("vector", lambda e, i=i: e.scalar_tensor_tensor(
                        x_tile(i), y3_t(i), stats[:, C_R + i:C_R + i + 1],
                        x_tile(i), ALU.mult, ALU.add))
                    tx = lg.op("vector", lambda e, i=i: e.tensor_scalar(
                        x_tile(i), x_tile(i),
                        stats[:, C_NMR + i:C_NMR + i + 1],
                        None, ALU.add), inc="dve")
                    lg.wait("sync", "dve", tx)
                    SY.dma(lambda e, i=i, Hh=Hh: e.dma_start(
                        out_t[Hh * 8 + i], x_tile(i)))
                    tok["store"][i] = SY.close()

            w3_toks = {}
            w3_pe = {}

            def load_w3(oq):
                par = oq % 2
                WQ.emit_all()
                if par == 0:
                    lg.wait("gpsimd", "pe", tok["xfm_pe"])
                else:
                    lg.wait("gpsimd", "pe", max(tok["h_slab_pe"]))
                    lg.wait("gpsimd", "pe", tok["slab_attn_pe"])
                if oq >= 2:
                    lg.wait("gpsimd", "pe", w3_pe[oq - 2])
                WQ.dma(lambda e, oq=oq, par=par: e.dma_start(
                    sl_w3(par), w3t[oq % 4]))
                w3_toks[oq] = WQ.close()
                WQ.emit_all()

            load_w3(0)
            load_w3(1)
            y3_dve_toks = [0] * 8
            for L in range(4):
                oq, par = L, L % 2
                last_pe = 0
                for tt in range(8):
                    bi = 4 + (tt % 4)
                    pe_t = mm_group(
                        bi, ps[bi][:, 0:256],
                        [(lambda ic=ic, tt=tt:
                          g2_c(ic)[:, tt * 128:(tt + 1) * 128],
                          lambda ic=ic, par=par:
                          sl_w3(par)[:, ic * 256:(ic + 1) * 256])
                         for ic in range(32)],
                        waits=[("wqs", w3_toks[L]), ("act", g2_done_act),
                               ("lds", g_consts)])
                    last_pe = pe_t
                    lg.wait("vector", "pe", pe_t)
                    tv = lg.op("vector", lambda e, oq=oq, tt=tt, bi=bi:
                               e.tensor_tensor(
                                   y3_t(tt)[:, oq * 256:(oq + 1) * 256],
                                   ps[bi][:, 0:256],
                                   s3_bc[:, oq * 256:(oq + 1) * 256], ALU.mult),
                               inc="dve")
                    bank_guard[bi] = ("dve", tv)
                    y3_dve_toks[tt] = tv
                    if L == 3 and tt == 3:
                        ln4_half(0, [y3_t(i) for i in range(4)],
                                 y3_dve_toks[0:4], Hh)
                w3_pe[L] = last_pe
                if L + 2 < 4:
                    load_w3(L + 2)
            ln4_half(4, [y3_t(i) for i in range(4, 8)],
                     y3_dve_toks[4:8], Hh)
            w3_pe[6] = w3_pe[2]
            w3_pe[7] = w3_pe[3]
            WQ.emit_all()
            tok["xfm_pe"] = max(tok["xfm_pe"], w3_pe[6])
            tok["slab_attn_pe"] = max(tok["slab_attn_pe"], w3_pe[7])
            tok["w1_slab"] = [w3_pe[7]] * 2
            tok["w2_slab"] = [w3_pe[7]] * 2
            tok["h_read_pe"] = max(tok["h_read_pe"], w3_pe[7])
            tok["qt_pe"] = max(tok["qt_pe"], gbuf_pe, w3_pe[7])
            if debug_tag == "y3" and Hh == 0:
                dbg_tap("y3", lambda: y3_t(0), 1024, wait_sems=("dve",))

            g1_guard_dve = lg.count["dve"]

        g_final = SY.flush()
        TR.flush()
        WQ.flush()
        lg.wait("sync", "lds", g_final)

        cleanup = True
        with nc.Block() as block:
            @block.sync
            def _(e):
                for fn in lg.prog["sync"]:
                    fn(e)

            @block.gpsimd
            def _(e):
                for fn in lg.prog["gpsimd"]:
                    fn(e)

            @block.tensor
            def _(e):
                for fn in lg.prog["tensor"]:
                    fn(e)

            @block.vector
            def _(e):
                for fn in lg.prog["vector"]:
                    fn(e)

            @block.scalar
            def _(e):
                for fn in lg.prog["scalar"]:
                    fn(e)

        nc.all_engine_barrier()
        for h in lg.sems.values():
            nc.gpsimd.sem_clear(h)
        nc.all_engine_barrier()

    nc._dev_ledger = lg.events if hasattr(lg, 'events') else None
    return nc



def _prep_consts(inp):
    f = np.float32
    Wk = np.ascontiguousarray(np.asarray(inp["L0_attn_Wk"], f).T)
    Wv = np.ascontiguousarray(np.asarray(inp["L0_attn_Wv"], f).T)
    Wq = np.ascontiguousarray(np.asarray(inp["L0_attn_Wq"], f).T)
    Wo = np.ascontiguousarray(np.asarray(inp["L0_attn_Wo"], f).T)
    bk = np.ascontiguousarray(np.asarray(inp["L0_attn_bk"], f).reshape(M, 1))
    bv = np.ascontiguousarray(np.asarray(inp["L0_attn_bv"], f).reshape(M, 1))
    bq = np.ascontiguousarray(np.asarray(inp["L0_attn_bq"], f).reshape(M, 1))
    bo = np.ascontiguousarray(
        np.broadcast_to(np.asarray(inp["L0_attn_bo"], f), (128, D)))
    s3 = np.ascontiguousarray(
        np.broadcast_to(np.asarray(inp["L0_ffn_w3_s"], f), (128, D)))

    def tile_wT(w, n_in_chunks, n_out_chunks):
        wt = np.asarray(w, f).T
        t = wt.reshape(n_in_chunks, 128, n_out_chunks, 128)
        t = np.ascontiguousarray(t.transpose(2, 1, 0, 3))
        return t.reshape(n_out_chunks, 128, n_in_chunks * 128)

    w1t = tile_wT(inp["L0_ffn_w1_w"], 8, 32)
    w2t = tile_wT(inp["L0_ffn_w2_w"], 8, 32)
    h1t = tile_wT(inp["L0_ffn_h1_w"], 32, 32)
    h2t = tile_wT(inp["L0_ffn_h2_w"], 32, 32)
    w3T = np.asarray(inp["L0_ffn_w3_w"], f).T
    t = w3T.reshape(32, 128, 4, 256)
    w3t = np.ascontiguousarray(t.transpose(2, 1, 0, 3)).reshape(4, 128, 32 * 256)

    sc = lambda s_: np.ascontiguousarray(np.asarray(s_, f).reshape(32, 128).T)
    s1, s2 = sc(inp["L0_ffn_w1_s"]), sc(inp["L0_ffn_w2_s"])
    sh1, sh2 = sc(inp["L0_ffn_h1_s"]), sc(inp["L0_ffn_h2_s"])

    jj, ii = np.meshgrid(np.arange(128), np.arange(128), indexing="ij")
    mka = np.where(ii > jj, LR * DECAY ** (ii - 1 - jj), 0.0).astype(f)
    mkd = np.ascontiguousarray(
        np.broadcast_to((DECAY ** np.arange(128)).astype(f), (128, 128)))
    mkw = np.ascontiguousarray(
        np.broadcast_to((LR * DECAY ** (127 - np.arange(128))).astype(f),
                        (128, 128)))
    return dict(wk=Wk, wv=Wv, wq=Wq, wo=Wo, bk=bk, bv=bv, bq=bq, bo=bo,
                s3=s3, w1t=w1t, w2t=w2t, h1t=h1t, h2t=h2t, w3t=w3t,
                s1=s1, s2=s2, sh1=sh1, sh2=sh2, mka=mka, mkd=mkd, mkw=mkw,
                identin=np.eye(128, dtype=f))


def make_in_maps(inputs):
    x = np.asarray(inputs["L0_attn_x"], np.float32)
    consts = _prep_consts(inputs)
    zeros = np.zeros((TOK, D), np.float32)
    ones_m = np.ones((128, 1), np.float32)
    zeros_m = np.zeros((128, 1), np.float32)
    in_maps = []
    for c in range(8):
        b, h = c // 2, c % 2
        m = dict(consts)
        m["xo"] = np.ascontiguousarray(x[b, h * TOK:(h + 1) * TOK])
        m["xp"] = np.ascontiguousarray(x[b, 0:TOK]) if h == 1 else zeros
        m["smask"] = ones_m if h == 1 else zeros_m
        in_maps.append(m)
    return in_maps


def kernel(**inputs):
    nc = build_program()
    in_maps = make_in_maps(inputs)
    res = run_bass_kernel_spmd(nc, in_maps, core_ids=list(range(8)))
    out = np.empty((B, S, D), np.float32)
    for c in range(8):
        b, h = c // 2, c % 2
        out[b, h * TOK:(h + 1) * TOK] = res.results[c]["out"]
    return out


# version 20
# speedup vs baseline: 1.2228x; 1.0067x over previous
from collections import deque
from contextlib import ExitStack

import numpy as np

import concourse.bass as bass
import concourse.mybir as mybir
from concourse.bass_utils import run_bass_kernel_spmd

F32 = mybir.dt.float32
BF16 = mybir.dt.bfloat16
AF = mybir.ActivationFunctionType
ALU = mybir.AluOpType
AX = mybir.AxisListType

B, S, D, M, HID = 4, 4096, 1024, 128, 4096
TOK = 2048
DECAY, LR = 0.99, 0.01
EPS = 1e-5
DECAY_C = float(DECAY ** 128)
R1024 = 1.0 / 1024.0
USE_TS = False

ENGS = ("sync", "gpsimd", "tensor", "vector", "scalar")
C_SUM, C_SQ, C_M, C_VAR, C_SD, C_R, C_NMR, C_TMP = (i * 8 for i in range(8))


class Ledger:
    def __init__(self, nc, ctx, sems):
        self.nc = nc
        self.prog = {e: [] for e in ENGS}
        self.sems = {}
        self.count = {}
        self.waited = {e: {} for e in ENGS}
        self.events = {e: [] for e in ENGS}
        for s in sems:
            self.sems[s] = ctx.enter_context(nc.semaphore(s))
            self.count[s] = 0

    def wait(self, eng, name, val=None):
        if val is None:
            val = self.count[name]
        if val <= 0 or val <= self.waited[eng].get(name, 0):
            return
        self.waited[eng][name] = val
        self.events[eng].append(("w", name, val))
        h = self.sems[name]
        self.prog[eng].append(lambda e, h=h, v=val: e.wait_ge(h, v))

    def op(self, eng, fn, inc=None, inck=1):
        if inc is None:
            self.prog[eng].append(fn)
            return None
        self.count[inc] += inck
        self.events[eng].append(("i", inc, inck))
        h = self.sems[inc]
        self.prog[eng].append(lambda e, fn=fn, h=h, k=inck: fn(e).then_inc(h, k))
        return self.count[inc]


class DmaStream:

    def __init__(self, lg, eng, qnames, sig):
        self.lg = lg
        self.eng = eng
        self.q = qnames
        self.sig = sig
        self.qi = 0
        self.issued = {q: 0 for q in qnames}
        self.open = False
        self.pending = deque()
        self.n_closed = 0

    def begin(self):
        if not self.open:
            q = self.q[self.qi]
            while any(p[0] == q for p in self.pending):
                self._emit_one()
            self.open = True

    def dma(self, fn):
        self.begin()
        q = self.q[self.qi]
        self.issued[q] += 1
        self.lg.op(self.eng, fn, inc=q, inck=16)

    def close(self):
        if not self.open:
            return self.n_closed
        q = self.q[self.qi]
        self.n_closed += 1
        self.pending.append((q, self.issued[q] * 16, self.n_closed))
        self.open = False
        self.qi = (self.qi + 1) % len(self.q)
        return self.n_closed

    def _emit_one(self):
        q, v, _ = self.pending.popleft()
        self.lg.wait(self.eng, q, v)
        h = self.lg.sems[self.sig]
        self.lg.count[self.sig] += 1
        self.lg.events[self.eng].append(("i", self.sig, 1))
        self.lg.prog[self.eng].append(lambda e, h=h: e.sem_inc(h, 1))

    def ensure(self, gtok):
        while self.pending and self.lg.count[self.sig] < gtok:
            self._emit_one()

    def emit_all(self):
        while self.pending:
            self._emit_one()

    def flush(self):
        self.close()
        while self.pending:
            self._emit_one()
        return self.lg.count[self.sig]


def build_program(debug_tag=None):
    nc = bass.Bass("TRN2", target_bir_lowering=False)

    di = lambda n, shp: nc.dram_tensor(n, shp, F32, kind="ExternalInput").ap()
    xo = di("xo", [TOK, D])
    xp = di("xp", [TOK, D])
    wk, wq, wv = di("wk", [D, M]), di("wq", [D, M]), di("wv", [D, M])
    wo = di("wo", [M, D])
    bk, bq, bv = di("bk", [M, 1]), di("bq", [M, 1]), di("bv", [M, 1])
    bo_in = di("bo", [128, D])
    s3_in = di("s3", [128, D])
    s1_in, s2_in = di("s1", [128, 32]), di("s2", [128, 32])
    sh1_in, sh2_in = di("sh1", [128, 32]), di("sh2", [128, 32])
    mka_in = di("mka", [128, 128])
    mkd_in = di("mkd", [128, 128])
    mkw_in = di("mkw", [128, 128])
    smask_in = di("smask", [128, 1])
    ident_in = di("identin", [128, 128])
    w1t = di("w1t", [32, 128, 1024])
    w2t = di("w2t", [32, 128, 1024])
    h1t = di("h1t", [32, 128, 4096])
    h2t = di("h2t", [32, 128, 4096])
    w3t = di("w3t", [4, 128, 8192])
    out = nc.dram_tensor("out", [TOK, D], F32, kind="ExternalOutput").ap()
    dbg = None
    if debug_tag is not None:
        dbg = nc.dram_tensor("dbg", [128, 4096], F32, kind="ExternalOutput").ap()

    xo_t = xo.rearrange("(t p) d -> t p d", p=128)
    xp_t = xp.rearrange("(t p) d -> t p d", p=128)
    out_t = out.rearrange("(t p) d -> t p d", p=128)

    ctx = ExitStack()
    with ctx:
        sb = lambda n, shp, dt: ctx.enter_context(nc.sbuf_tensor(n, shp, dt))
        xres = sb("xres", [128, 8 * 1024], F32)
        bk_s, bq_s, bv_s = (sb(n, [128, 1], F32) for n in ("bk_s", "bq_s", "bv_s"))
        s3_bc = sb("s3_bc", [128, 1024], F32)
        s1_s, s2_s = sb("s1_s", [128, 32], F32), sb("s2_s", [128, 32], F32)
        sh1_s, sh2_s = sb("sh1_s", [128, 32], F32), sb("sh2_s", [128, 32], F32)
        mask_a = sb("mask_a", [128, 128], F32)
        mask_dk = sb("mask_dk", [128, 128], BF16)
        mask_kw = sb("mask_kw", [128, 128], BF16)
        smask = sb("smask_b", [128, 1], F32)
        st = sb("st", [128, 256], F32)
        stats = sb("stats", [128, 64], F32)
        ident = sb("ident", [128, 128], F32)
        xfm = sb("xfm", [128, 8192], BF16)
        kqv = sb("kqv", [128, 3840], BF16)
        hbuf = sb("hbuf", [128, 32768], BF16)
        gbuf = sb("gbuf", [128, 32768], BF16)
        slab = sb("slab", [128, 8192], BF16)

        x_tile = lambda i: xres[:, i * 1024:(i + 1) * 1024]
        xfm_c = lambda c: xfm[:, c * 1024:(c + 1) * 1024]
        cw_k = slab[:, 0:1024]
        cw_q = slab[:, 1024:2048]
        cw_v = slab[:, 2048:3072]
        cw_o = slab[:, 3072:4096]
        bo_bc = gbuf[:, 24576:26624].bitcast(F32)
        k_fm = kqv[:, 0:1024]
        q_fm = kqv[:, 1024:2048]
        v_fm = kqv[:, 2048:3072]
        kw_scr = lambda p: kqv[:, 3072 + p * 128: 3072 + (p + 1) * 128]
        ktm_scr = lambda p: kqv[:, 3328 + p * 128: 3328 + (p + 1) * 128]
        vtm_scr = lambda p: kqv[:, 3584 + p * 128: 3584 + (p + 1) * 128]
        t1_buf = lambda par: kqv[:, par * 1024:(par + 1) * 1024].bitcast(F32)
        h_c = lambda ic: hbuf[:, ic * 1024:(ic + 1) * 1024]
        g2_c = h_c
        xp_buf = lambda i: hbuf[:, i * 2048:(i + 1) * 2048].bitcast(F32)
        g1_c = lambda ic: gbuf[:, ic * 1024:(ic + 1) * 1024]
        y_buf = gbuf[:, 0:2048].bitcast(F32)
        xln_buf = lambda p: gbuf[:, 2048 + p * 1024: 2048 + (p + 1) * 1024]
        sq_buf = gbuf[:, 4096:6144].bitcast(F32)
        qt_all = lambda c: gbuf[:, 6144 + c * 128: 6144 + (c + 1) * 128]
        kw_f32 = lambda c: gbuf[:, 26624 + c * 256: 26624 + (c + 1) * 256].bitcast(F32)
        ktm_all = lambda c: gbuf[:, 12288 + c * 128: 12288 + (c + 1) * 128]
        v_f32 = gbuf[:, 14336:16384].bitcast(F32)
        y3_t = lambda i: gbuf[:, 8192 + i * 2048: 8192 + (i + 1) * 2048].bitcast(F32)
        sl_w1 = lambda p: slab[:, p * 1024:(p + 1) * 1024]
        sl_w2 = lambda p: slab[:, 2048 + p * 1024: 2048 + (p + 1) * 1024]
        sl_h = lambda p: slab[:, p * 4096:(p + 1) * 4096]
        vtm_all = lambda c: slab[:, 4096 + c * 128: 4096 + (c + 1) * 128]
        abf_all = lambda c: slab[:, 5120 + c * 128: 5120 + (c + 1) * 128]
        snap_all = lambda c: slab[:, 6144 + c * 128: 6144 + (c + 1) * 128]
        outs_all = lambda c: slab[:, 7168 + c * 128: 7168 + (c + 1) * 128]
        sl_w3 = lambda p: (xfm if p == 0 else slab)[:, 0:8192]

        st_slot = [0]
        st_cur = lambda: st[:, st_slot[0] * 128:(st_slot[0] + 1) * 128]
        st_nxt = lambda: st[:, (1 - st_slot[0]) * 128:(2 - st_slot[0]) * 128]

        ps = [ctx.enter_context(nc.psum_tensor(f"ps{i}", [128, 512], F32))
              for i in range(8)]
        bank_guard = [("dve", 0)] * 8

        lg = Ledger(nc, ctx, (
            "q_sy0", "q_sy1", "q_tr0", "q_tr1", "q_tr2", "q_tr3",
            "q_ts0", "q_ts1", "q_ts2", "q_ts3",
            "q_wq0", "q_wq1", "lds", "trs", "tss", "wqs", "pe", "act", "dve", "gp"))
        SY = DmaStream(lg, "sync", ["q_sy0", "q_sy1"], "lds")
        TR = DmaStream(lg, "scalar", ["q_tr0", "q_tr1", "q_tr2", "q_tr3"], "trs")
        TS = DmaStream(lg, "sync", ["q_ts0", "q_ts1", "q_ts2", "q_ts3"], "tss")
        WQ = DmaStream(lg, "gpsimd", ["q_wq0", "q_wq1"], "wqs")

        for h in lg.sems.values():
            nc.gpsimd.sem_clear(h)
        nc.all_engine_barrier()

        dbg_state = {"done": False}

        def dbg_tap(tag, ap_fn, ncols, wait_sems=()):
            if debug_tag != tag or dbg_state["done"]:
                return
            dbg_state["done"] = True
            if "lds" in wait_sems:
                gg = SY.close()
                SY.ensure(gg)
            for sname in wait_sems:
                lg.wait("gpsimd", sname)
            SY.close()
            SY.dma(lambda e, a=ap_fn(), n=ncols: e.dma_start(dbg[:, 0:n], a))
            g = SY.close()
            SY.ensure(g)
            for e_ in ENGS:
                lg.wait(e_, "lds", g)

        cwtok = {"g": 0}

        def load_cw(guard_pe):
            WQ.emit_all()
            lg.wait("gpsimd", "pe", guard_pe)
            WQ.dma(lambda e: e.dma_start(
                cw_k.rearrange("p (c m) -> p c m", m=128),
                wk.rearrange("(c p) m -> p c m", p=128)))
            WQ.dma(lambda e: e.dma_start(
                cw_q.rearrange("p (c m) -> p c m", m=128),
                wq.rearrange("(c p) m -> p c m", p=128)))
            WQ.dma(lambda e: e.dma_start(
                cw_v.rearrange("p (c m) -> p c m", m=128),
                wv.rearrange("(c p) m -> p c m", p=128)))
            WQ.dma(lambda e: e.dma_start(cw_o, wo))
            cwtok["g"] = WQ.close()
            WQ.emit_all()

        load_cw(0)
        WQ.dma(lambda e: e.dma_start(mask_dk[:], mkd_in))
        WQ.dma(lambda e: e.dma_start(mask_kw[:], mkw_in))
        g_masks = WQ.close()
        WQ.emit_all()

        for dst, src in ((bk_s[:], bk), (bq_s[:], bq), (bv_s[:], bv),
                         (s3_bc[:], s3_in), (s1_s[:], s1_in), (s2_s[:], s2_in),
                         (sh1_s[:], sh1_in), (sh2_s[:], sh2_in),
                         (mask_a[:], mka_in), (smask[:], smask_in),
                         (ident[:], ident_in)):
            SY.dma(lambda e, d=dst, s=src: e.dma_start(d, s))
        g_consts = SY.close()

        lg.op("vector", lambda e: e.memset(st[:, 0:128], 0.0))


        tok = {
            "xfm_pe": 0,
            "xln_tr": [None, None],
            "xsc_pe": [0, 0],
            "kw_tr": [0, 0],
            "vtm_pe": [0, 0],
            "w1_slab": [0, 0], "w2_slab": [0, 0],
            "h_slab_pe": [0, 0],
            "slab_attn_pe": 0,
            "qt_pe": 0,
            "t1_free": {},
            "store": [0] * 8,
            "y_sq_act": 0,
            "st_tok": 0,
            "h_read_pe": 0,
        }

        def mm_group(bank_idx, out_ap, pairs, waits=()):
            sname, v = bank_guard[bank_idx]
            lg.wait("tensor", sname, v)
            for sn, vv in waits:
                lg.wait("tensor", sn, vv)
            n = len(pairs)
            t = None
            for j, (lf, rf) in enumerate(pairs):
                t = lg.op("tensor",
                          lambda e, lf=lf, rf=rf, st_=(j == 0), sp=(j == n - 1),
                          o=out_ap: e.matmul(o, lf(), rf(), start=st_, stop=sp),
                          inc="pe" if j == n - 1 else None)
            return t

        def ln_stats_batch(srcs, src_toks, src_sem, cb=0):
            n = len(srcs)
            a_last = None
            for i, (sap, stok) in enumerate(zip(srcs, src_toks)):
                if stok:
                    lg.wait("scalar", src_sem, stok)
                a_last = lg.op(
                    "scalar",
                    lambda e, sap=sap, i=i, cb=cb: e.activation(
                        sq_buf, sap, AF.Square,
                        accum_out=stats[:, C_SQ + cb + i:C_SQ + cb + i + 1]),
                    inc="act")
            t_red = None
            for i, (sap, stok) in enumerate(zip(srcs, src_toks)):
                if stok:
                    lg.wait("vector", src_sem, stok)
                t_red = lg.op("vector",
                              lambda e, sap=sap, i=i, cb=cb: e.reduce_sum(
                                  stats[:, C_SUM + cb + i:C_SUM + cb + i + 1], sap,
                                  axis=AX.X), inc="dve")
            lg.wait("vector", "act", a_last)
            lg.wait("vector", "dve", t_red)
            tM = lg.op("vector", lambda e, n=n, cb=cb: e.tensor_scalar(
                stats[:, C_M + cb:C_M + cb + n], stats[:, C_SUM + cb:C_SUM + cb + n],
                R1024, None, ALU.mult), inc="dve")
            lg.op("vector", lambda e, n=n, cb=cb: e.tensor_scalar(
                stats[:, C_SD + cb:C_SD + cb + n], stats[:, C_SQ + cb:C_SQ + cb + n],
                R1024, None, ALU.mult))
            lg.wait("vector", "dve", tM)
            tT = lg.op("vector", lambda e, n=n, cb=cb: e.tensor_tensor(
                stats[:, C_TMP + cb:C_TMP + cb + n], stats[:, C_M + cb:C_M + cb + n],
                stats[:, C_M + cb:C_M + cb + n], ALU.mult), inc="dve")
            lg.wait("vector", "dve", tT)
            tv = lg.op("vector", lambda e, n=n, cb=cb: e.scalar_tensor_tensor(
                stats[:, C_VAR + cb:C_VAR + cb + n], stats[:, C_SD + cb:C_SD + cb + n], EPS,
                stats[:, C_TMP + cb:C_TMP + cb + n], ALU.add, ALU.subtract), inc="dve")
            lg.wait("scalar", "dve", tv)
            ta = lg.op("scalar", lambda e, n=n, cb=cb: e.activation(
                stats[:, C_SD + cb:C_SD + cb + n], stats[:, C_VAR + cb:C_VAR + cb + n],
                AF.Sqrt), inc="act")
            lg.wait("vector", "act", ta)
            tR = lg.op("vector", lambda e, n=n, cb=cb: e.reciprocal(
                stats[:, C_R + cb:C_R + cb + n], stats[:, C_SD + cb:C_SD + cb + n]), inc="dve")
            lg.wait("vector", "dve", tR)
            tn = lg.op("vector", lambda e, n=n, cb=cb: e.scalar_tensor_tensor(
                stats[:, C_NMR + cb:C_NMR + cb + n], stats[:, C_M + cb:C_M + cb + n], -1.0,
                stats[:, C_R + cb:C_R + cb + n], ALU.mult, ALU.mult), inc="dve")
            return tn

        def trx_wait(eng, toks):
            if toks is None:
                return
            for s_, v in toks:
                lg.wait(eng, s_, v)

        def tok_waits(toklists):
            m = {}
            for tl in toklists:
                if tl is None:
                    continue
                for s_, v in tl:
                    m[s_] = max(m.get(s_, 0), v)
            return list(m.items())

        def pe_tr_block(dst_ap, src_ap, bank_idx, waits, xfm_guard_pe):
            sname, v = bank_guard[bank_idx]
            lg.wait("tensor", sname, v)
            for sn, vv in waits:
                lg.wait("tensor", sn, vv)
            lg.wait("tensor", "lds", g_consts)
            t = lg.op("tensor", lambda e, d=src_ap, bi=bank_idx: e.transpose(
                ps[bi][:, 0:128], d, ident[:]), inc="pe")
            lg.wait("vector", "pe", t)
            lg.wait("vector", "pe", xfm_guard_pe)
            ev = lg.op("vector", lambda e, o=dst_ap, bi=bank_idx: e.tensor_copy(
                o, ps[bi][:, 0:128]), inc="dve")
            bank_guard[bank_idx] = ("dve", ev)
            return t, ev

        def ln_to_xfm(srcs, src_toks, src_sem, base=0, dma_path=False):
            stok = ln_stats_batch(srcs, src_toks, src_sem, cb=base)
            tr_toks = [None] * len(srcs)
            for j, sap in enumerate(srcs):
                i = base + j
                p = i % 2
                if dma_path:
                    lg.wait("scalar", "dve", stok)
                    trx_wait("scalar", tok["xln_tr"][p])
                    t_ap = lg.op("scalar", lambda e, s=sap, i=i, p=p:
                                 e.activation(
                                     xln_buf(p), s, AF.Identity,
                                     scale=stats[:, C_R + i:C_R + i + 1],
                                     bias=stats[:, C_NMR + i:C_NMR + i + 1]),
                                 inc="act")
                    lg.wait("scalar", "pe", tok["xfm_pe"])
                    lg.wait("scalar", "act", t_ap)
                    for c in range(8):
                        TR.dma(lambda e, c=c, i=i, p=p: e.dma_start(
                            xfm_c(c)[:, i * 128:(i + 1) * 128],
                            xln_buf(p)[:, c * 128:(c + 1) * 128],
                            transpose=True))
                    g = TR.close()
                    TR.emit_all()
                    tok["xln_tr"][p] = [("trs", g)]
                    tr_toks[j] = [("trs", g)]
                    continue
                xsc = y3_t(p)
                lg.wait("scalar", "dve", stok)
                lg.wait("scalar", "pe", tok["xsc_pe"][p])
                t_ap = lg.op("scalar", lambda e, s=sap, i=i, x=xsc: e.activation(
                    x, s, AF.Identity,
                    scale=stats[:, C_R + i:C_R + i + 1],
                    bias=stats[:, C_NMR + i:C_NMR + i + 1]), inc="act")
                last_ev = 0
                last_t = 0
                for c in range(8):
                    last_t, last_ev = pe_tr_block(
                        xfm_c(c)[:, i * 128:(i + 1) * 128],
                        xsc[:, c * 128:(c + 1) * 128],
                        6 + c % 2, [("act", t_ap)], tok["xfm_pe"])
                tok["xsc_pe"][p] = last_t
                tr_toks[j] = [("dve", last_ev)]
            return tr_toks

        def proj_pass(tr_toks, do_q):
            plist = [("k", cw_k, bk_s, k_fm, (0, 1)),
                     ("v", cw_v, bv_s, v_f32, (2, 3))]
            if do_q:
                plist.append(("q", cw_q, bq_s, q_fm, (4, 5)))
            ev = {}
            for name, cw, bias, dst, banks in plist:
                ev[name] = [0, 0]
                for tg in range(2):
                    bi = banks[tg]
                    tws = tok_waits(tr_toks[tg * 4:(tg + 1) * 4])
                    pe_t = mm_group(
                        bi, ps[bi][:],
                        [(lambda ic=ic, cw=cw: cw[:, ic * 128:(ic + 1) * 128],
                          lambda ic=ic, tg=tg: xfm_c(ic)[:, tg * 512:(tg + 1) * 512])
                         for ic in range(8)],
                        waits=tws + [("wqs", cwtok["g"]), ("lds", g_consts)])
                    lg.wait("scalar", "pe", pe_t)
                    lg.wait("scalar", "pe", tok["qt_pe"])
                    a_t = lg.op("scalar",
                                lambda e, b=ps[bi], d=dst, tg=tg, bias=bias:
                                e.activation(d[:, tg * 512:(tg + 1) * 512], b[:],
                                             AF.Identity, bias=bias[:]),
                                inc="act")
                    ev[name][tg] = a_t
                    bank_guard[bi] = ("act", a_t)
            return ev

        def attn_chunks(ev, full):
            kw_toks = [0] * 8
            abf_toks = [0] * 8
            qt_toks = [0] * 8
            snap_toks = [0] * 8
            outs_toks = [0] * 8
            for c in range(8):
                par = c % 2
                lg.wait("vector", "act", ev["k"][c // 4])
                lg.wait("vector", "wqs", g_masks)
                lg.wait("vector", "pe", tok["qt_pe"])
                kw_toks[c] = lg.op("vector", lambda e, c=c: e.tensor_tensor(
                    kw_f32(c), k_fm[:, c * 128:(c + 1) * 128], mask_kw[:],
                    ALU.mult), inc="dve")
                if full:
                    lg.wait("vector", "act", ev["q"][c // 4])
                    qt_toks[c] = lg.op("vector", lambda e, c=c: e.tensor_tensor(
                        qt_all(c), q_fm[:, c * 128:(c + 1) * 128], mask_dk[:],
                        ALU.mult), inc="dve")
                    bi_a = 6 + par
                    pe_a = mm_group(
                        bi_a, ps[bi_a][:, 0:128],
                        [(lambda c=c: k_fm[:, c * 128:(c + 1) * 128],
                          lambda c=c: q_fm[:, c * 128:(c + 1) * 128])],
                        waits=[("act", max(ev["k"][c // 4], ev["q"][c // 4]))])
                    lg.wait("vector", "pe", pe_a)
                    lg.wait("vector", "lds", g_consts)
                    lg.wait("vector", "pe", tok["slab_attn_pe"])
                    abf_toks[c] = lg.op("vector", lambda e, c=c, bi=bi_a:
                                        e.tensor_tensor(abf_all(c),
                                                        ps[bi][:, 0:128],
                                                        mask_a[:], ALU.mult),
                                        inc="dve")
                    bank_guard[bi_a] = ("dve", abf_toks[c])
            tp_toks = [None] * 8
            for c in range(8):
                _, ev_k = pe_tr_block(
                    ktm_all(c), kw_f32(c), c % 4,
                    [("dve", kw_toks[c]), ("pe", tok["qt_pe"])], 0)
                _, ev_v = pe_tr_block(
                    vtm_all(c), v_f32[:, c * 128:(c + 1) * 128], (c + 1) % 4,
                    [("act", ev["v"][c // 4]), ("pe", tok["slab_attn_pe"])], 0)
                tp_toks[c] = [("dve", ev_v)]
            tr_pair = None
            for c in range(8):
                par = c % 2
                bi = 4 + par
                pe_t = mm_group(
                    bi, ps[bi][:, 0:128],
                    [(lambda c=c: ktm_all(c), lambda c=c: vtm_all(c))],
                    waits=tp_toks[c])
                lg.wait("vector", "dve", tok["st_tok"])
                if full:
                    lg.wait("vector", "pe", tok["slab_attn_pe"])
                    _scur = st_cur()
                    snap_toks[c] = lg.op("vector", lambda e, c=c, s_=_scur:
                                         e.tensor_copy(snap_all(c), s_),
                                         inc="dve")
                lg.wait("vector", "pe", pe_t)
                cur, nxt = st_cur(), st_nxt()
                tu = lg.op("vector", lambda e, bi=bi, cur=cur, nxt=nxt:
                           e.scalar_tensor_tensor(nxt, cur, DECAY_C,
                                                  ps[bi][:, 0:128],
                                                  ALU.mult, ALU.add), inc="dve")
                st_slot[0] = 1 - st_slot[0]
                tok["st_tok"] = tu
                bank_guard[bi] = ("dve", tu)
                if full:
                    bi_o = 6 + par
                    pe_o = mm_group(
                        bi_o, ps[bi_o][:, 128:256],
                        [(lambda c=c: vtm_all(c), lambda c=c: abf_all(c)),
                         (lambda c=c: snap_all(c), lambda c=c: qt_all(c))],
                        waits=[("dve", max(abf_toks[c], snap_toks[c],
                                           qt_toks[c]))] + tp_toks[c])
                    tok["qt_pe"] = pe_o
                    tok["slab_attn_pe"] = pe_o
                    lg.wait("scalar", "pe", pe_o)
                    outs_toks[c] = lg.op(
                        "scalar", lambda e, c=c, bi=bi_o: e.activation(
                            outs_all(c), ps[bi][:, 128:256], AF.Copy),
                        inc="act")
                    bank_guard[bi_o] = ("act", outs_toks[c])
            return abf_toks, snap_toks, qt_toks, outs_toks

        hb_act = 0
        hb_dve = 0
        for ph in range(2):
            xp_toks = []
            lg.wait("sync", "act", hb_act)
            lg.wait("sync", "dve", hb_dve)
            for i in range(8):
                SY.dma(lambda e, i=i, ph=ph: e.dma_start(
                    xp_buf(i), xp_t[ph * 8 + i]))
                if i % 2 == 1:
                    g = SY.close()
                    for j in range(i - 1, i + 1):
                        xp_toks.append(g)
            SY.emit_all()
            srcs = [xp_buf(i) for i in range(8)]
            if ph == 0:
                dbg_tap("pre_xp", lambda: xp_buf(0), 1024,
                        wait_sems=("lds",))
            tr_toks = ln_to_xfm(srcs, xp_toks, "lds")
            if ph == 0:
                dbg_tap("pre_stats", lambda: stats[:], 64,
                        wait_sems=("dve", "act"))
            hb_act = lg.count["act"]
            hb_dve = lg.count["dve"]
            ev = proj_pass(tr_toks, do_q=False)
            tok["xfm_pe"] = lg.count["pe"]
            attn_chunks(ev, full=False)
            tok["slab_attn_pe"] = lg.count["pe"]
            tok["qt_pe"] = lg.count["pe"]
        lg.wait("vector", "lds", g_consts)
        lg.wait("vector", "dve", tok["st_tok"])
        _cur, _nxt = st_cur(), st_nxt()
        tok["st_tok"] = lg.op("vector", lambda e, c=_cur, x=_nxt:
                              e.tensor_scalar(x, c, smask[:], None, ALU.mult),
                              inc="dve")
        st_slot[0] = 1 - st_slot[0]
        dbg_tap("prelude_st", lambda: st_cur(), 128, wait_sems=("dve",))

        g1_guard_dve = 0
        for Hh in range(2):
            WQ.emit_all()
            TR.emit_all()
            SY.emit_all()
            lg.wait("sync", "pe", tok["qt_pe"])
            SY.dma(lambda e: e.dma_start(bo_bc, bo_in))
            if Hh == 1:
                load_cw(max(tok["w1_slab"] + tok["w2_slab"]))
            x_toks = []
            for i in range(8):
                if Hh == 1:
                    SY.ensure(tok["store"][i])
                    lg.wait("sync", "lds", tok["store"][i])
                SY.dma(lambda e, i=i, Hh=Hh: e.dma_start(
                    x_tile(i), xo_t[Hh * 8 + i]))
                if i % 2 == 1:
                    g = SY.close()
                    for j in range(i - 1, i + 1):
                        x_toks.append(g)
            SY.emit_all()
            g_bo = x_toks[0]

            tr_toks = (ln_to_xfm([x_tile(i) for i in range(4)],
                                 x_toks[0:4], "lds", 0) +
                       ln_to_xfm([x_tile(i) for i in range(4, 8)],
                                 x_toks[4:8], "lds", 4))
            ev = proj_pass(tr_toks, do_q=True)
            tok["xfm_pe"] = lg.count["pe"]

            abf_toks, snap_toks, qt_toks, outs_toks = attn_chunks(
                ev, full=True)

            y_toks = [0] * 8
            for c in range(8):
                par = c % 2
                bY = (0, 1) if par == 0 else (2, 3)
                pe_y = 0
                for hd in range(2):
                    pe_y = mm_group(
                        bY[hd], ps[bY[hd]][:],
                        [(lambda c=c: outs_all(c),
                          lambda hd=hd: cw_o[:, hd * 512:(hd + 1) * 512])],
                        waits=[("act", outs_toks[c]), ("wqs", cwtok["g"])])
                tok["slab_attn_pe"] = max(tok["slab_attn_pe"], pe_y)
                lg.wait("vector", "pe", pe_y)
                lg.wait("vector", "lds", g_bo)
                ty = 0
                for hd in range(2):
                    ty = lg.op("vector", lambda e, c=c, hd=hd, bY=bY:
                               e.tensor_tensor(
                                   y3_t(c)[:, hd * 512:(hd + 1) * 512],
                                   ps[bY[hd]][:],
                                   bo_bc[:, hd * 512:(hd + 1) * 512], ALU.add),
                               inc="dve")
                    bank_guard[bY[hd]] = ("dve", ty)
                y_toks[c] = ty
            stok2 = ln_stats_batch([y3_t(c) for c in range(8)], y_toks, "dve")
            xup = [0] * 8
            for c in range(8):
                lg.wait("vector", "dve", stok2)
                lg.op("vector", lambda e, c=c: e.scalar_tensor_tensor(
                    x_tile(c), y3_t(c), stats[:, C_R + c:C_R + c + 1],
                    x_tile(c), ALU.mult, ALU.add))
                xup[c] = lg.op("vector", lambda e, c=c: e.tensor_scalar(
                    x_tile(c), x_tile(c), stats[:, C_NMR + c:C_NMR + c + 1],
                    None, ALU.add), inc="dve")
            tr_toks3 = (ln_to_xfm([x_tile(i) for i in range(4)],
                                  xup[0:4], "dve", 0) +
                        ln_to_xfm([x_tile(i) for i in range(4, 8)],
                                  xup[4:8], "dve", 4))
            attn_x_dve = lg.count["dve"]
            if debug_tag == "attn" and Hh == 0:
                dbg_tap("attn", lambda: x_tile(0), 1024, wait_sems=("dve",))


            w12_toks = {}

            def load_w12(oc):
                par = oc % 2
                WQ.emit_all()
                lg.wait("gpsimd", "pe", max(tok["w1_slab"][par],
                                            tok["w2_slab"][par],
                                            tok["xfm_pe"],
                                            tok["slab_attn_pe"]))
                WQ.dma(lambda e, oc=oc, par=par: e.dma_start(
                    sl_w1(par), w1t[oc]))
                WQ.dma(lambda e, oc=oc, par=par: e.dma_start(
                    sl_w2(par), w2t[oc]))
                w12_toks[oc] = WQ.close()

            load_w12(0)
            load_w12(1)
            h_done_dve = 0
            for oc in range(32):
                par = oc % 2
                p1b = (0, 1) if par == 0 else (4, 5)
                p2b = (2, 3) if par == 0 else (6, 7)
                for tg in range(2):
                    pe1 = mm_group(
                        p1b[tg], ps[p1b[tg]][:],
                        [(lambda ic=ic, par=par: sl_w1(par)[:, ic * 128:(ic + 1) * 128],
                          lambda ic=ic, tg=tg: xfm_c(ic)[:, tg * 512:(tg + 1) * 512])
                         for ic in range(8)],
                        waits=tok_waits(tr_toks3[tg * 4:(tg + 1) * 4]) + [
                               ("wqs", w12_toks[oc]), ("lds", g_consts)])
                    pe2 = mm_group(
                        p2b[tg], ps[p2b[tg]][:],
                        [(lambda ic=ic, par=par: sl_w2(par)[:, ic * 128:(ic + 1) * 128],
                          lambda ic=ic, tg=tg: xfm_c(ic)[:, tg * 512:(tg + 1) * 512])
                         for ic in range(8)],
                        waits=[("wqs", w12_toks[oc])])
                    tok["w1_slab"][par] = pe1
                    tok["w2_slab"][par] = pe2
                    tok["xfm_pe"] = pe2
                    lg.wait("scalar", "pe", pe1)
                    lg.wait("scalar", "dve", tok["t1_free"].get(par, 0))
                    ta = lg.op("scalar",
                               lambda e, oc=oc, par=par, b=ps[p1b[tg]]:
                               e.activation(t1_buf(par), b[:], AF.Silu,
                                            scale=s1_s[:, oc:oc + 1]), inc="act")
                    bank_guard[p1b[tg]] = ("act", ta)
                    lg.wait("vector", "pe", pe2)
                    lg.wait("vector", "act", ta)
                    lg.wait("vector", "pe", tok["h_read_pe"])
                    th = lg.op("vector",
                               lambda e, oc=oc, par=par, tg=tg, b=ps[p2b[tg]]:
                               e.scalar_tensor_tensor(
                                   h_c(oc)[:, tg * 512:(tg + 1) * 512], b[:],
                                   s2_s[:, oc:oc + 1], t1_buf(par),
                                   ALU.mult, ALU.mult), inc="dve")
                    bank_guard[p2b[tg]] = ("dve", th)
                    tok["t1_free"][par] = th
                    h_done_dve = th
                if oc + 2 < 32:
                    load_w12(oc + 2)

            def hidden_layer(wsrc, in_c, out_c, scale_s, in_dve, in_act,
                             evac_dve_guard, first_slab_extra_pe):
                slab_toks = {}

                def load_h(oc):
                    par = oc % 2
                    WQ.emit_all()
                    lg.wait("gpsimd", "pe", tok["h_slab_pe"][par])
                    WQ.dma(lambda e, oc=oc, par=par, wsrc=wsrc:
                           e.dma_start(sl_h(par), wsrc[oc]))
                    slab_toks[oc] = WQ.close()

                WQ.emit_all()
                lg.wait("gpsimd", "pe", first_slab_extra_pe)
                load_h(0)
                load_h(1)
                last_evac = 0
                for oc in range(32):
                    par = oc % 2
                    banks = (0, 1) if par == 0 else (2, 3)
                    for tg in range(2):
                        waits = [("wqs", slab_toks[oc]),
                                 ("lds", g_consts)]
                        if in_dve:
                            waits.append(("dve", in_dve))
                        if in_act:
                            waits.append(("act", in_act))
                        pe_t = mm_group(
                            banks[tg], ps[banks[tg]][:],
                            [(lambda ic=ic, par=par:
                              sl_h(par)[:, ic * 128:(ic + 1) * 128],
                              lambda ic=ic, tg=tg:
                              in_c(ic)[:, tg * 512:(tg + 1) * 512])
                             for ic in range(32)],
                            waits=waits)
                        tok["h_slab_pe"][par] = pe_t
                        lg.wait("scalar", "pe", pe_t)
                        lg.wait("scalar", "dve", evac_dve_guard)
                        ta = lg.op(
                            "scalar",
                            lambda e, oc=oc, tg=tg, b=ps[banks[tg]], ss=scale_s:
                            e.activation(out_c(oc)[:, tg * 512:(tg + 1) * 512],
                                         b[:], AF.Gelu,
                                         scale=ss[:, oc:oc + 1]), inc="act")
                        bank_guard[banks[tg]] = ("act", ta)
                        last_evac = ta
                    if oc + 2 < 32:
                        load_h(oc + 2)
                return last_evac

            first_extra = max(tok["w1_slab"] + tok["w2_slab"] +
                              [tok["slab_attn_pe"]])
            g1_done_act = hidden_layer(h1t, h_c, g1_c, sh1_s,
                                       h_done_dve, 0, g1_guard_dve, first_extra)
            tok["h_read_pe"] = max(tok["h_slab_pe"])
            g2_done_act = hidden_layer(h2t, g1_c, g2_c, sh2_s,
                                       0, g1_done_act, h_done_dve, 0)
            gbuf_pe = max(tok["h_slab_pe"])

            def ln4_half(base, srcs4, toks4, Hh):
                stok4 = ln_stats_batch(srcs4, toks4, "dve", cb=base)
                for j in range(4):
                    i = base + j
                    lg.wait("vector", "dve", stok4)
                    lg.op("vector", lambda e, i=i: e.scalar_tensor_tensor(
                        x_tile(i), y3_t(i), stats[:, C_R + i:C_R + i + 1],
                        x_tile(i), ALU.mult, ALU.add))
                    tx = lg.op("vector", lambda e, i=i: e.tensor_scalar(
                        x_tile(i), x_tile(i),
                        stats[:, C_NMR + i:C_NMR + i + 1],
                        None, ALU.add), inc="dve")
                    lg.wait("sync", "dve", tx)
                    SY.dma(lambda e, i=i, Hh=Hh: e.dma_start(
                        out_t[Hh * 8 + i], x_tile(i)))
                    tok["store"][i] = SY.close()

            w3_toks = {}
            w3_pe = {}

            def load_w3(oq):
                par = oq % 2
                WQ.emit_all()
                if par == 0:
                    lg.wait("gpsimd", "pe", tok["xfm_pe"])
                else:
                    lg.wait("gpsimd", "pe", max(tok["h_slab_pe"]))
                    lg.wait("gpsimd", "pe", tok["slab_attn_pe"])
                if oq >= 2:
                    lg.wait("gpsimd", "pe", w3_pe[oq - 2])
                WQ.dma(lambda e, oq=oq, par=par: e.dma_start(
                    sl_w3(par), w3t[oq % 4]))
                w3_toks[oq] = WQ.close()
                WQ.emit_all()

            load_w3(0)
            load_w3(1)
            y3_dve_toks = [0] * 8
            for L in range(4):
                oq, par = L, L % 2
                last_pe = 0
                for tt in range(8):
                    bi = 4 + (tt % 4)
                    pe_t = mm_group(
                        bi, ps[bi][:, 0:256],
                        [(lambda ic=ic, tt=tt:
                          g2_c(ic)[:, tt * 128:(tt + 1) * 128],
                          lambda ic=ic, par=par:
                          sl_w3(par)[:, ic * 256:(ic + 1) * 256])
                         for ic in range(32)],
                        waits=[("wqs", w3_toks[L]), ("act", g2_done_act),
                               ("lds", g_consts)])
                    last_pe = pe_t
                    lg.wait("vector", "pe", pe_t)
                    tv = lg.op("vector", lambda e, oq=oq, tt=tt, bi=bi:
                               e.tensor_tensor(
                                   y3_t(tt)[:, oq * 256:(oq + 1) * 256],
                                   ps[bi][:, 0:256],
                                   s3_bc[:, oq * 256:(oq + 1) * 256], ALU.mult),
                               inc="dve")
                    bank_guard[bi] = ("dve", tv)
                    y3_dve_toks[tt] = tv
                    if L == 3 and tt == 3:
                        ln4_half(0, [y3_t(i) for i in range(4)],
                                 y3_dve_toks[0:4], Hh)
                w3_pe[L] = last_pe
                if L + 2 < 4:
                    load_w3(L + 2)
            ln4_half(4, [y3_t(i) for i in range(4, 8)],
                     y3_dve_toks[4:8], Hh)
            w3_pe[6] = w3_pe[2]
            w3_pe[7] = w3_pe[3]
            WQ.emit_all()
            tok["xfm_pe"] = max(tok["xfm_pe"], w3_pe[6])
            tok["slab_attn_pe"] = max(tok["slab_attn_pe"], w3_pe[7])
            tok["w1_slab"] = [w3_pe[7]] * 2
            tok["w2_slab"] = [w3_pe[7]] * 2
            tok["h_read_pe"] = max(tok["h_read_pe"], w3_pe[7])
            tok["qt_pe"] = max(tok["qt_pe"], gbuf_pe, w3_pe[7])
            if debug_tag == "y3" and Hh == 0:
                dbg_tap("y3", lambda: y3_t(0), 1024, wait_sems=("dve",))

            g1_guard_dve = lg.count["dve"]

        g_final = SY.flush()
        TR.flush()
        WQ.flush()
        lg.wait("sync", "lds", g_final)

        cleanup = True
        with nc.Block() as block:
            @block.sync
            def _(e):
                for fn in lg.prog["sync"]:
                    fn(e)

            @block.gpsimd
            def _(e):
                for fn in lg.prog["gpsimd"]:
                    fn(e)

            @block.tensor
            def _(e):
                for fn in lg.prog["tensor"]:
                    fn(e)

            @block.vector
            def _(e):
                for fn in lg.prog["vector"]:
                    fn(e)

            @block.scalar
            def _(e):
                for fn in lg.prog["scalar"]:
                    fn(e)

        nc.all_engine_barrier()
        for h in lg.sems.values():
            nc.gpsimd.sem_clear(h)
        nc.all_engine_barrier()

    nc._dev_ledger = lg.events if hasattr(lg, 'events') else None
    return nc



def _prep_consts(inp):
    f = np.float32
    Wk = np.ascontiguousarray(np.asarray(inp["L0_attn_Wk"], f).T)
    Wv = np.ascontiguousarray(np.asarray(inp["L0_attn_Wv"], f).T)
    Wq = np.ascontiguousarray(np.asarray(inp["L0_attn_Wq"], f).T)
    Wo = np.ascontiguousarray(np.asarray(inp["L0_attn_Wo"], f).T)
    bk = np.ascontiguousarray(np.asarray(inp["L0_attn_bk"], f).reshape(M, 1))
    bv = np.ascontiguousarray(np.asarray(inp["L0_attn_bv"], f).reshape(M, 1))
    bq = np.ascontiguousarray(np.asarray(inp["L0_attn_bq"], f).reshape(M, 1))
    bo = np.ascontiguousarray(
        np.broadcast_to(np.asarray(inp["L0_attn_bo"], f), (128, D)))
    s3 = np.ascontiguousarray(
        np.broadcast_to(np.asarray(inp["L0_ffn_w3_s"], f), (128, D)))

    def tile_wT(w, n_in_chunks, n_out_chunks):
        wt = np.asarray(w, f).T
        t = wt.reshape(n_in_chunks, 128, n_out_chunks, 128)
        t = np.ascontiguousarray(t.transpose(2, 1, 0, 3))
        return t.reshape(n_out_chunks, 128, n_in_chunks * 128)

    w1t = tile_wT(inp["L0_ffn_w1_w"], 8, 32)
    w2t = tile_wT(inp["L0_ffn_w2_w"], 8, 32)
    h1t = tile_wT(inp["L0_ffn_h1_w"], 32, 32)
    h2t = tile_wT(inp["L0_ffn_h2_w"], 32, 32)
    w3T = np.asarray(inp["L0_ffn_w3_w"], f).T
    t = w3T.reshape(32, 128, 4, 256)
    w3t = np.ascontiguousarray(t.transpose(2, 1, 0, 3)).reshape(4, 128, 32 * 256)

    sc = lambda s_: np.ascontiguousarray(np.asarray(s_, f).reshape(32, 128).T)
    s1, s2 = sc(inp["L0_ffn_w1_s"]), sc(inp["L0_ffn_w2_s"])
    sh1, sh2 = sc(inp["L0_ffn_h1_s"]), sc(inp["L0_ffn_h2_s"])

    jj, ii = np.meshgrid(np.arange(128), np.arange(128), indexing="ij")
    mka = np.where(ii > jj, LR * DECAY ** (ii - 1 - jj), 0.0).astype(f)
    mkd = np.ascontiguousarray(
        np.broadcast_to((DECAY ** np.arange(128)).astype(f), (128, 128)))
    mkw = np.ascontiguousarray(
        np.broadcast_to((LR * DECAY ** (127 - np.arange(128))).astype(f),
                        (128, 128)))
    return dict(wk=Wk, wv=Wv, wq=Wq, wo=Wo, bk=bk, bv=bv, bq=bq, bo=bo,
                s3=s3, w1t=w1t, w2t=w2t, h1t=h1t, h2t=h2t, w3t=w3t,
                s1=s1, s2=s2, sh1=sh1, sh2=sh2, mka=mka, mkd=mkd, mkw=mkw,
                identin=np.eye(128, dtype=f))


def make_in_maps(inputs):
    x = np.asarray(inputs["L0_attn_x"], np.float32)
    consts = _prep_consts(inputs)
    zeros = np.zeros((TOK, D), np.float32)
    ones_m = np.ones((128, 1), np.float32)
    zeros_m = np.zeros((128, 1), np.float32)
    in_maps = []
    for c in range(8):
        b, h = c // 2, c % 2
        m = dict(consts)
        m["xo"] = np.ascontiguousarray(x[b, h * TOK:(h + 1) * TOK])
        m["xp"] = np.ascontiguousarray(x[b, 0:TOK]) if h == 1 else zeros
        m["smask"] = ones_m if h == 1 else zeros_m
        in_maps.append(m)
    return in_maps


def kernel(**inputs):
    nc = build_program()
    in_maps = make_in_maps(inputs)
    res = run_bass_kernel_spmd(nc, in_maps, core_ids=list(range(8)))
    out = np.empty((B, S, D), np.float32)
    for c in range(8):
        b, h = c // 2, c % 2
        out[b, h * TOK:(h + 1) * TOK] = res.results[c]["out"]
    return out
